# version 06
# baseline (speedup 1.0000x reference)
import numpy as np
import ml_dtypes
from contextlib import ExitStack

import concourse.bass as bass
import concourse.mybir as mybir
from concourse.bass_utils import run_bass_kernel_spmd

F32 = mybir.dt.float32
BF16 = mybir.dt.bfloat16
AF = mybir.ActivationFunctionType
ALU = mybir.AluOpType
AX = mybir.AxisListType

D = 1024
T = 4096
NDT = 8
EPS = 1e-5
NEG_BIG = -30000.0
RS2 = 2.0 ** (-0.5)
TWO_PI = 6.283185307179586
MAGIC = 12582912.0

ENGINES = ("tensor", "vector", "scalar", "gpsimd", "sync")


def _sl(i, w):
    return slice(i * w, (i + 1) * w)


class Buf:
    __slots__ = ("name", "wt", "readers")

    def __init__(self, name):
        self.name = name
        self.wt = None
        self.readers = []


class DSlot:

    def __init__(self, idx):
        self.idx = idx
        self.n = 0


class Op:
    __slots__ = ("engine", "fn", "deps", "eidx", "dslot", "marked")

    def __init__(self, engine, fn, deps, eidx, dslot=None):
        self.engine = engine
        self.fn = fn
        self.deps = deps
        self.eidx = eidx
        self.dslot = dslot
        self.marked = False


class Bld:
    def __init__(self, nc):
        self.nc = nc
        self.ops = []
        self.ecount = {e: 0 for e in ENGINES}
        self.dslots = []
        self.bufs = {}
        self.esem = {}
        self.dsem = []
        self.segs = []
        self._slot_touched = set()

    def buf(self, name):
        if name not in self.bufs:
            self.bufs[name] = Buf(name)
        return self.bufs[name]

    def dslot(self):
        s = DSlot(len(self.dslots))
        self.dslots.append(s)
        return s

    def _dep_tok(self, t):
        return t

    @staticmethod
    def _flat(bs):
        out = []
        for b in bs:
            if isinstance(b, (list, tuple)):
                out.extend(b)
            else:
                out.append(b)
        return out

    def _mk_op(self, engine, fn, r, w, extra, dslot=None):
        r = self._flat(r)
        w = self._flat(w)
        deps = []
        for b in r:
            if b.wt is not None:
                deps.append(b.wt)
        for b in w:
            if b.wt is not None:
                deps.append(b.wt)
            deps.extend(b.readers)
        deps.extend(extra)
        gi = len(self.ops)
        eidx = self.ecount[engine]
        self.ecount[engine] += 1
        if dslot is not None:
            dslot.n += 16
            tok = ("dma", dslot.idx, dslot.n, gi)
        else:
            tok = ("eng", engine, gi)
        kept = []
        for d in deps:
            if d is None:
                continue
            if d[0] == "eng" and d[1] == engine == "tensor":
                continue
            kept.append(d)
        op = Op(engine, fn, kept, eidx, dslot)
        self.ops.append(op)
        for b in r:
            b.readers.append(tok)
        for b in w:
            b.wt = tok
            b.readers = []
        return tok

    def op(self, engine, fn, r=(), w=(), extra=()):
        return self._mk_op(engine, fn, list(r), list(w), list(extra))

    def t(self, fn, r=(), w=(), extra=()):
        return self.op("tensor", fn, r, w, extra)

    def v(self, fn, r=(), w=(), extra=()):
        return self.op("vector", fn, r, w, extra)

    def s(self, fn, r=(), w=(), extra=()):
        return self.op("scalar", fn, r, w, extra)

    def g(self, fn, r=(), w=(), extra=()):
        return self.op("gpsimd", fn, r, w, extra)

    def dma(self, queue, slot, out_ap, in_ap, r=(), w=(), extra=(), **kw):
        extra = list(extra)
        if slot.n > 0:
            extra.append(("dma", slot.idx, slot.n, -1))

        def fn(eng):
            return eng.dma_start(out=out_ap, in_=in_ap, **kw)

        self._slot_touched.add(slot.idx)
        return self._mk_op(queue, fn, list(r), list(w), extra, dslot=slot)

    def fence_dma(self, slot):
        return self._mk_op("vector", lambda v: v.engine_nop(), [], [],
                           [("dma", slot.idx, slot.n, -1)])

    def seal_phase(self):
        for si in sorted(self._slot_touched):
            s = self.dslots[si]
            self._mk_op("vector", lambda v: v.engine_nop(), [], [],
                        [("dma", s.idx, s.n, -1)])
        self._slot_touched = set()
        self.segs.append(len(self.ops))
        for b in self.bufs.values():
            b.wt = None
            b.readers = []

    def build(self):
        nc = self.nc
        need = set()
        for op in self.ops:
            for d in op.deps:
                need.add((d[0], d[1], d[3] if d[0] == "dma" else d[2]))
        for op in self.ops:
            if op.dslot is not None:
                op.marked = True
            else:
                pass
        for kind, a, gi in list(need):
            if kind == "eng" and gi >= 0:
                self.ops[gi].marked = True
        self.cum = {}
        cnt = {e: 0 for e in ENGINES}
        for gi, op in enumerate(self.ops):
            if op.dslot is None and op.marked:
                cnt[op.engine] += 1
            self.cum[gi] = cnt[op.engine]
        for e in ENGINES:
            self.esem[e] = nc.alloc_semaphore("es_" + e)
        for s in self.dslots:
            self.dsem.append(nc.alloc_semaphore("ds_%d" % s.idx))

        cum = self.cum
        esem = self.esem
        dsem = self.dsem
        if not self.segs or self.segs[-1] != len(self.ops):
            self.segs.append(len(self.ops))
        wm = {e: {} for e in ENGINES}

        def run_engine(eng_handle, ename, seg_ops):
            w = wm[ename]
            for op in seg_ops:
                if op.engine != ename:
                    continue
                for d in op.deps:
                    if d[0] == "eng":
                        tgt = cum[d[2]]
                        key = ("e", d[1])
                        if tgt > 0 and w.get(key, 0) < tgt:
                            eng_handle.wait_ge(esem[d[1]], tgt)
                            w[key] = tgt
                    else:
                        tgt = d[2]
                        key = ("d", d[1])
                        if tgt > 0 and w.get(key, 0) < tgt:
                            eng_handle.wait_ge(dsem[d[1]], tgt)
                            w[key] = tgt
                inst = op.fn(eng_handle)
                if op.dslot is not None:
                    inst.then_inc(dsem[op.dslot.idx], 16)
                elif op.marked:
                    inst.then_inc(esem[ename], 1)

        prev = 0
        for seg_end in self.segs:
            seg = self.ops[prev:seg_end]
            prev = seg_end
            if not seg:
                continue
            with nc.Block() as blk:
                blk.tensor(lambda e, s=seg: run_engine(e, "tensor", s))
                blk.vector(lambda e, s=seg: run_engine(e, "vector", s))
                blk.scalar(lambda e, s=seg: run_engine(e, "scalar", s))
                blk.gpsimd(lambda e, s=seg: run_engine(e, "gpsimd", s))
                blk.sync(lambda e, s=seg: run_engine(e, "sync", s))


class MK:
    def __init__(self, nc, taps=()):
        self.nc = nc
        self.B = Bld(nc)
        self.es = ExitStack()
        self.pes = None
        self.taps = set(taps)
        self._psum_cur = 0
        self.psb = {}
        self._dram()
        self._global_sbuf()

    def sb(self, name, shape, dtype=F32):
        es = self.pes if self.pes is not None else self.es
        return es.enter_context(self.nc.sbuf_tensor(name, list(shape), dtype))

    def ps(self, name, shape, dtype=F32):
        es = self.pes if self.pes is not None else self.es
        h = es.enter_context(self.nc.psum_tensor(name, list(shape), dtype))
        nb = 2 if dtype == BF16 else 4
        per_part = int(np.prod(shape[1:])) * nb
        nbanks = -(-per_part // 2048)
        bufs = [self.B.buf("PSB%d" % (self._psum_cur + i)) for i in range(nbanks)]
        self._psum_cur += nbanks
        assert self._psum_cur <= 8, "PSUM overflow in phase"
        self.psb[name] = bufs
        return h

    def pb(self, name):
        return self.psb[name]

    def begin_phase(self):
        assert self.pes is None
        self.pes = ExitStack()
        self._psum_cur = 0
        self.psb = {}

    def end_phase(self):
        self.B.seal_phase()
        self.pes.close()
        self.pes = None

    def _dram(self):
        nc = self.nc
        d = {}

        def dd(name, shape, dtype=F32, kind="ExternalInput"):
            d[name] = nc.dram_tensor(name, list(shape), dtype, kind=kind)

        def scr(name, shape):
            dd(name, shape, F32, "ExternalOutput" if name in self.taps else "Internal")

        dd("xT", (D, T))
        dd("outT", (D, T), F32, "ExternalOutput")
        scr("xres", (D, T))
        scr("x2s", (D, T))
        scr("xins", (D, T))
        dd("xxxs", (D, T), BF16, "Internal")
        dd("cb", (128, 128 + 132 + 8 * 128), BF16)
        dd("cf", (128, 260), F32)
        dd("onesrow", (1, 128), BF16)
        dd("onesrow_f", (1, 128), F32)
        self.PV = ["ts", "tf", "ln1w", "ln1b", "ln2w", "ln2b",
                   "l1d", "l1g", "l1b2", "mur", "muk", "bout", "lng", "lnb"]
        dd("pv", (128, 8 * len(self.PV)))
        dd("pv64", (64, 2))
        dd("rbias", (1, 8))
        dd("wkT", (D, D)); dd("wvT", (D, D)); dd("wrT", (D, D))
        dd("wckT", (D, 4 * D)); dd("wcvT", (4 * D, D)); dd("wcrT", (D, D))
        dd("awq", (D, 128)); dd("awk", (D, 128)); dd("awv", (D, 128)); dd("awo", (128, D))
        dd("rw", (D, 8))
        dd("w1", (8, D, 8 * D))
        dd("w2", (8, 4 * D, D))
        dd("w1c", (8, D, 8 * D), BF16, "Internal")
        dd("w2c", (8, 4 * D, D), BF16, "Internal")
        dd("bmat", (64, D)); dd("cmatT", (64, D))
        dd("wqk", (D, D)); dd("bqk", (64, 16))
        dd("wv1", (D, 512)); dd("bv", (64, 8)); dd("bvr", (1, 512))
        dd("wout", (512, D))
        dd("fwrT", (D, D)); dd("fwkT", (D, D)); dd("fwvT", (D, D))
        self.d = d
        self.xres_b = [self.B.buf("xres%d" % i) for i in range(T // 256)]
        self.x2s_b = [self.B.buf("x2s%d" % i) for i in range(T // 256)]
        self.xins_b = [self.B.buf("xins%d" % i) for i in range(T // 256)]
        self.xxxs_b = [self.B.buf("xxxs%d" % i) for i in range(T // 256)]

    def dram_bufs(self, which, t0, t1):
        lst = {"xres": self.xres_b, "x2s": self.x2s_b, "xins": self.xins_b,
               "xxxs": self.xxxs_b}[which]
        return lst[t0 // 256:(t1 + 255) // 256]

    def _global_sbuf(self):
        sb = self.sb
        self.c_bf = sb("c_bf", (128, 128 + 132 + 8 * 128), BF16)
        self.c_f = sb("c_f", (128, 260), F32)
        self.c_or = sb("c_or", (1, 128), BF16)
        self.c_orf = sb("c_orf", (1, 128), F32)
        self.pvs = sb("pvs", (128, 8 * len(self.PV)))
        self.pv64s = sb("pv64s", (64, 2))
        self.rbs = sb("rbs", (1, 8))
        self.DV = ["c1ts", "cmts", "eU", "reluU", "omur", "omuk", "dp1", "omts", "nU"]
        self.dvs = sb("dvs", (128, 8 * len(self.DV)))
        self.rb_b = sb("rb_b", (128, 8))
        self.pw_re = sb("pw_re", (64, 12))
        self.pw_im = sb("pw_im", (64, 12))
        self.bbT = sb("bbT", (128, NDT * 128), BF16)
        self.x2last = sb("x2last", (128, NDT))

    def ident(self):
        return self.c_bf[:, 0:128]

    def tri(self):
        return self.c_bf[:, 128:257]

    def selc(self, e):
        return self.c_bf[0:8, 260 + e * 128: 260 + (e + 1) * 128]

    def causal(self):
        return self.c_f[:, 0:128]

    def ones_col_f(self):
        return self.c_f[:, 128:129]

    def ident_f(self):
        return self.c_f[:, 132:260]

    def pvc(self, name):
        i = self.PV.index(name)
        return self.pvs[:, 8 * i: 8 * (i + 1)]

    def dvc(self, name):
        i = self.DV.index(name)
        return self.dvs[:, 8 * i: 8 * (i + 1)]

    def ln_fm(self, xap, src_bufs, ncols, sq, rows, pG, pGb, pH, pHb, sqb, rowsb):
        B = self.B
        bf = B.buf
        CW = ncols
        for dd in range(NDT):
            B.s(lambda s, dd=dd: s.square(sq[:, 0:CW], xap[:, _sl(dd, CW)]),
                r=src_bufs, w=[sqb])
            B.t(lambda t, dd=dd: t.matmul(pG[0:1, 0:CW], self.ones_col_f(),
                                          xap[:, _sl(dd, CW)],
                                          start=(dd == 0), stop=(dd == NDT - 1)),
                r=src_bufs + [bf("c_f")], w=[pGb])
            B.t(lambda t, dd=dd: t.matmul(pH[0:1, 0:CW], self.ones_col_f(),
                                          sq[:, 0:CW],
                                          start=(dd == 0), stop=(dd == NDT - 1)),
                r=[sqb, bf("c_f")], w=[pHb])

        r0, r1 = bf("r_s0"), bf("r_s1")
        B.v(lambda v: v.tensor_scalar_mul(rows[0:1, 0:CW], pG[0:1, 0:CW], 1.0 / D),
            r=[pGb], w=[r0])
        B.v(lambda v: v.tensor_scalar_mul(rows[0:1, CW:2 * CW], pH[0:1, 0:CW], 1.0 / D),
            r=[pHb], w=[r1])
        B.v(lambda v: v.tensor_mul(rows[0:1, 0:CW], rows[0:1, 0:CW], rows[0:1, 0:CW]),
            r=[], w=[r0])
        B.v(lambda v: v.tensor_sub(rows[0:1, CW:2 * CW], rows[0:1, CW:2 * CW], rows[0:1, 0:CW]),
            r=[r0], w=[r1])
        B.v(lambda v: v.tensor_scalar_mul(rows[0:1, 0:CW], pG[0:1, 0:CW], -1.0 / D),
            r=[pGb], w=[r0])
        B.s(lambda s: s.activation(rows[0:1, CW:2 * CW], rows[0:1, CW:2 * CW], AF.Ln,
                                   bias=self.c_f[0:1, 129:130]), r=[bf("c_f")], w=[r1])
        B.s(lambda s: s.activation(rows[0:1, CW:2 * CW], rows[0:1, CW:2 * CW], AF.Exp,
                                   scale=-0.5), r=[], w=[r1])
        B.t(lambda t: t.matmul(pG[:, 0:CW], self.c_orf[:], rows[0:1, 0:CW], start=True, stop=True),
            r=[r0, bf("c_orf")], w=[pGb])
        B.t(lambda t: t.matmul(pH[:, 0:CW], self.c_orf[:], rows[0:1, CW:2 * CW], start=True, stop=True),
            r=[r1, bf("c_orf")], w=[pHb])

    def prologue(self):
        B = self.B
        d = self.d
        bf = B.buf
        self.begin_phase()
        sl_c = B.dslot()
        B.dma("sync", sl_c, self.c_bf[:], d["cb"][:], w=[bf("c_bf")])
        B.dma("sync", sl_c, self.c_f[:], d["cf"][:], w=[bf("c_f")])
        B.dma("sync", sl_c, self.c_or[:], d["onesrow"][:], w=[bf("c_or")])
        B.dma("sync", sl_c, self.c_orf[:], d["onesrow_f"][:], w=[bf("c_orf")])
        B.dma("sync", sl_c, self.pvs[:], d["pv"][:], w=[bf("pvs")])
        B.dma("sync", sl_c, self.pv64s[:], d["pv64"][:], w=[bf("pv64s")])
        B.dma("sync", sl_c, self.rbs[:], d["rbias"][:], w=[bf("rbs")])
        bm_s = self.sb("bm_s", (64, D))
        B.dma("sync", sl_c, bm_s[:], d["bmat"][:], w=[bf("bm_s")])

        pv, dv = self.pvc, self.dvc
        B.v(lambda v: v.tensor_scalar_add(dv("c1ts"), pv("ts"), 1.0), r=[bf("pvs")], w=[bf("dvs")])
        B.v(lambda v: v.tensor_scalar_mul(dv("cmts"), pv("ts"), -1.0), r=[bf("pvs")], w=[bf("dvs")])
        B.v(lambda v: v.tensor_scalar_max(dv("reluU"), pv("tf"), 0.0), r=[bf("pvs")], w=[bf("dvs")])
        B.v(lambda v: v.tensor_scalar(dv("omur"), pv("mur"), -1.0, 1.0, ALU.mult, ALU.add), r=[bf("pvs")], w=[bf("dvs")])
        B.v(lambda v: v.tensor_scalar(dv("omuk"), pv("muk"), -1.0, 1.0, ALU.mult, ALU.add), r=[bf("pvs")], w=[bf("dvs")])
        B.v(lambda v: v.tensor_scalar_add(dv("dp1"), pv("l1d"), 1.0), r=[bf("pvs")], w=[bf("dvs")])
        B.s(lambda s: s.activation(dv("eU"), pv("tf"), AF.Exp), r=[bf("pvs")], w=[bf("dvs")])
        B.v(lambda v: v.tensor_scalar(dv("omts"), pv("ts"), -1.0, 1.0, ALU.mult, ALU.add), r=[bf("pvs")], w=[bf("dvs")])
        B.v(lambda v: v.tensor_scalar_mul(dv("nU"), pv("tf"), -1.0), r=[bf("pvs")], w=[bf("dvs")])

        prb = self.ps("P_rb", (128, 8))
        B.t(lambda t: t.matmul(prb[:], self.c_orf[:], self.rbs[:], start=True, stop=True),
            r=[bf("c_orf"), bf("rbs")], w=[self.pb("P_rb")])
        B.v(lambda v: v.tensor_copy(self.rb_b[:], prb[:]), r=[self.pb("P_rb")], w=[bf("rb_b")])

        w = self.sb("ssw", (64, 26))
        ww = bf("ssw")
        lnr = self.pv64s[:, 0:1]
        th = self.pv64s[:, 1:2]
        pvb = bf("pv64s")
        B.s(lambda s: s.activation(w[:, 0:1], lnr, AF.Exp), r=[pvb], w=[ww])
        B.s(lambda s: s.activation(w[:, 1:2], w[:, 0:1], AF.Exp, scale=-1.0), r=[ww], w=[ww])
        B.v(lambda v: v.tensor_scalar(w[:, 2:3], th, 1.0 / TWO_PI, MAGIC, ALU.mult, ALU.add), r=[pvb], w=[ww])
        B.v(lambda v: v.tensor_scalar_add(w[:, 2:3], w[:, 2:3], -MAGIC), r=[ww], w=[ww])
        B.v(lambda v: v.scalar_tensor_tensor(w[:, 3:4], w[:, 2:3], -TWO_PI, th, ALU.mult, ALU.add), r=[pvb, ww], w=[ww])
        B.v(lambda v: v.tensor_scalar_add(w[:, 4:5], th, TWO_PI / 4.0), r=[pvb], w=[ww])
        B.v(lambda v: v.tensor_scalar(w[:, 5:6], w[:, 4:5], 1.0 / TWO_PI, MAGIC, ALU.mult, ALU.add), r=[ww], w=[ww])
        B.v(lambda v: v.tensor_scalar_add(w[:, 5:6], w[:, 5:6], -MAGIC), r=[ww], w=[ww])
        B.v(lambda v: v.scalar_tensor_tensor(w[:, 6:7], w[:, 5:6], -TWO_PI, w[:, 4:5], ALU.mult, ALU.add), r=[ww], w=[ww])
        B.s(lambda s: s.activation(w[:, 7:8], w[:, 3:4], AF.Sin), r=[ww], w=[ww])
        B.s(lambda s: s.activation(w[:, 8:9], w[:, 6:7], AF.Sin), r=[ww], w=[ww])
        B.v(lambda v: v.tensor_mul(w[:, 9:10], w[:, 1:2], w[:, 8:9]), r=[ww], w=[ww])
        B.v(lambda v: v.tensor_mul(w[:, 10:11], w[:, 1:2], w[:, 7:8]), r=[ww], w=[ww])
        B.v(lambda v: v.tensor_scalar_add(w[:, 11:12], w[:, 9:10], -1.0), r=[ww], w=[ww])
        B.v(lambda v: v.tensor_mul(w[:, 12:13], w[:, 0:1], w[:, 0:1]), r=[ww], w=[ww])
        B.v(lambda v: v.tensor_mul(w[:, 13:14], th, th), r=[pvb, ww], w=[ww])
        B.v(lambda v: v.tensor_add(w[:, 12:13], w[:, 12:13], w[:, 13:14]), r=[ww], w=[ww])
        B.v(lambda v: v.reciprocal(w[:, 13:14], w[:, 12:13]), r=[ww], w=[ww])
        B.v(lambda v: v.tensor_mul(w[:, 14:15], w[:, 11:12], w[:, 0:1]), r=[ww], w=[ww])
        B.v(lambda v: v.tensor_scalar_mul(w[:, 14:15], w[:, 14:15], -1.0), r=[ww], w=[ww])
        B.v(lambda v: v.tensor_mul(w[:, 15:16], w[:, 10:11], th), r=[pvb, ww], w=[ww])
        B.v(lambda v: v.tensor_add(w[:, 14:15], w[:, 14:15], w[:, 15:16]), r=[ww], w=[ww])
        B.v(lambda v: v.tensor_mul(w[:, 14:15], w[:, 14:15], w[:, 13:14]), r=[ww], w=[ww])
        B.v(lambda v: v.tensor_mul(w[:, 16:17], w[:, 10:11], w[:, 0:1]), r=[ww], w=[ww])
        B.v(lambda v: v.tensor_scalar_mul(w[:, 16:17], w[:, 16:17], -1.0), r=[ww], w=[ww])
        B.v(lambda v: v.tensor_mul(w[:, 17:18], w[:, 11:12], th), r=[pvb, ww], w=[ww])
        B.v(lambda v: v.tensor_sub(w[:, 16:17], w[:, 16:17], w[:, 17:18]), r=[ww], w=[ww])
        B.v(lambda v: v.tensor_mul(w[:, 16:17], w[:, 16:17], w[:, 13:14]), r=[ww], w=[ww])
        pwb = bf("pw")
        B.v(lambda v: v.tensor_copy(self.pw_re[:, 0:1], w[:, 9:10]), r=[ww], w=[pwb])
        B.v(lambda v: v.tensor_copy(self.pw_im[:, 0:1], w[:, 10:11]), r=[ww], w=[pwb])
        for j in range(11):
            a, b2 = self.pw_re[:, j:j + 1], self.pw_im[:, j:j + 1]

            w18, w19 = bf("ssw18"), bf("ssw19")
            B.v(lambda v, a=a: v.tensor_mul(w[:, 18:19], a, a), r=[pwb], w=[w18])
            B.v(lambda v, b2=b2: v.tensor_mul(w[:, 19:20], b2, b2), r=[pwb], w=[w19])
            B.v(lambda v, j=j: v.tensor_sub(self.pw_re[:, j + 1:j + 2], w[:, 18:19], w[:, 19:20]),
                r=[w18, w19], w=[pwb])
            B.v(lambda v, a=a, b2=b2: v.tensor_mul(w[:, 18:19], a, b2), r=[pwb], w=[w18])
            B.v(lambda v, j=j: v.tensor_scalar_mul(self.pw_im[:, j + 1:j + 2], w[:, 18:19], 2.0),
                r=[w18], w=[pwb])
        bb_s = self.sb("bb_s", (64, 2 * D), BF16)
        bbb = bf("bb_s")
        B.v(lambda v: v.tensor_scalar_mul(bb_s[:, 0:D], bm_s[:], w[:, 14:15]), r=[bf("bm_s"), ww], w=[bbb])
        B.v(lambda v: v.tensor_scalar_mul(bb_s[:, D:2 * D], bm_s[:], w[:, 16:17]), r=[bf("bm_s"), ww], w=[bbb])
        ptr = self.ps("P_tr0", (128, 128), BF16)
        ptrb = self.pb("P_tr0")
        bbTb = bf("bbT")
        for dd in range(NDT):
            B.t(lambda t, dd=dd: t.transpose(ptr[:, 0:64], bb_s[0:64, _sl(dd, 128)], self.ident()[0:64, 0:64]),
                r=[bbb, bf("c_bf")], w=[ptrb])
            B.t(lambda t, dd=dd: t.transpose(ptr[:, 64:128], bb_s[0:64, D + dd * 128: D + (dd + 1) * 128], self.ident()[0:64, 0:64]),
                r=[bbb, bf("c_bf")], w=[ptrb])
            B.v(lambda v, dd=dd: v.tensor_copy(self.bbT[:, _sl(dd, 128)], ptr[:]),
                r=[ptrb], w=[bbTb])
        self.end_phase()

    def phase_l0aa_A(self):
        B = self.B
        d = self.d
        bf = B.buf
        pv, dv = self.pvc, self.dvc
        self.begin_phase()
        sb, ps = self.sb, self.ps
        CW = 512
        NCH = T // CW
        NSUB = CW // 128

        xc = [sb("A_xc0", (128, NDT * CW)), sb("A_xc1", (128, NDT * CW))]
        xcb = [bf("A_xc0"), bf("A_xc1")]
        sq = sb("A_sq", (128, CW)); sqb = bf("A_sq")
        xin = sb("A_xin", (128, NDT * CW)); xinb = bf("A_xin")
        xxc = [sb("A_xxc0", (128, NDT * CW), BF16), sb("A_xxc1", (128, NDT * CW), BF16)]
        xxcb = [bf("A_xxc0"), bf("A_xxc1")]
        hbound = sb("A_hbound", (128, NDT)); hbndb = bf("A_hbound")
        rows = sb("A_rows", (1, 2 * CW)); rowsb = bf("A_rows")
        tmp = sb("A_tmp", (128, CW)); tmpb = bf("A_tmp")
        wk_s = sb("A_wk", (128, NDT * D), BF16)
        wv_s = sb("A_wv", (128, NDT * D), BF16)
        wr_s = sb("A_wr", (128, NDT * D), BF16)

        kf = sb("A_kf", (128, T)); kfb = bf("A_kf")
        vf = sb("A_vf", (128, T), BF16); vfb = bf("A_vf")
        rf = sb("A_rf", (128, T), BF16); rfb = bf("A_rf")
        ekc = sb("A_ek", (128, CW)); ekcb = bf("A_ek")
        ekbf = sb("A_ekbf", (128, CW), BF16); ekbfb = bf("A_ekbf")
        smallbf2 = [sb("A_sbf0", (128, 128), BF16), sb("A_sbf1", (128, 128), BF16)]
        smallbf2b = [bf("A_sbf0"), bf("A_sbf1")]
        zh = sb("A_zh", (128, CW)); zhb = bf("A_zh")
        dchunk = sb("A_D", (128, CW)); dchb = bf("A_D")
        lgc = sb("A_lg", (128, CW)); lgcb = bf("A_lg")
        bc = sb("A_bc", (128, CW)); bcb = bf("A_bc")
        aac = sb("A_aa", (128, CW)); aacb = bf("A_aa")
        t4 = sb("A_t4", (128, CW)); t4b = bf("A_t4")
        t5 = sb("A_t5", (128, CW)); t5b = bf("A_t5")
        xinld = sb("A_xinld", (128, CW)); xinldb = bf("A_xinld")
        x2c = sb("A_x2c", (128, CW)); x2cb = bf("A_x2c")
        cols = sb("A_cols", (128, 8)); colb = bf("A_cols")
        sub1 = sb("A_sub1", (128, 128)); sub1b = bf("A_sub1")
        sub2 = sb("A_sub2", (128, 128)); sub2b = bf("A_sub2")
        sub3 = sb("A_sub3", (128, 128)); sub3b = bf("A_sub3")

        pK = ps("A_pK", (128, CW)); pKb = self.pb("A_pK")
        pV = ps("A_pV", (128, CW)); pVb = self.pb("A_pV")
        pR = ps("A_pR", (128, CW)); pRb = self.pb("A_pR")
        pT2 = [ps("A_pT0", (128, 128), BF16), ps("A_pT1", (128, 128), BF16)]
        pT2b = [self.pb("A_pT0"), self.pb("A_pT1")]
        pC = ps("A_pC", (128, 132)); pCb = self.pb("A_pC")
        pG = ps("A_pG", (128, CW)); pGb = self.pb("A_pG")
        pH = ps("A_pH", (128, CW)); pHb = self.pb("A_pH")

        slw = B.dslot()
        for nm, dst in [("wkT", wk_s), ("wvT", wv_s), ("wrT", wr_s)]:
            B.dma("gpsimd", slw, dst.ap().rearrange("p (a c) -> p a c", a=NDT),
                  d[nm].ap().rearrange("(a p) c -> p a c", p=128), w=[bf("A_" + nm)])
        wkb, wvb, wrb = bf("A_wkT"), bf("A_wvT"), bf("A_wrT")

        B.v(lambda v: v.tensor_copy(hbound[:], pv("ts")), r=[bf("pvs")], w=[hbndb])

        cbuf = [sb("A_cb0", (128, 4 * D), BF16), sb("A_cb1", (128, 4 * D), BF16)]
        cbufb = [bf("A_cb0"), bf("A_cb1")]
        slci = [B.dslot(), B.dslot()]
        slco = [B.dslot(), B.dslot()]
        cast_tasks = []
        for e in range(8):
            for cq in range(16):
                cast_tasks.append(("w1", e, cq))
            for cq in range(8):
                cast_tasks.append(("w2", e, cq))
        self._cast_i = 0

        def emit_casts(n):
            for _ in range(n):
                if self._cast_i >= len(cast_tasks):
                    return
                which, e, cq = cast_tasks[self._cast_i]
                par = self._cast_i % 2
                self._cast_i += 1
                if which == "w1":
                    B.dma("gpsimd", slci[par],
                          cbuf[par].ap().rearrange("p (a f) -> p a f", a=8)[:, :, 0:512],
                          d["w1"][e].rearrange("(a p) f -> p a f", p=128)[:, :, cq * 512:(cq + 1) * 512],
                          w=[cbufb[par]])
                    B.dma("gpsimd", slco[par],
                          d["w1c"][e].rearrange("(a p) f -> p a f", p=128)[:, :, cq * 512:(cq + 1) * 512],
                          cbuf[par].ap().rearrange("p (a f) -> p a f", a=8)[:, :, 0:512],
                          r=[cbufb[par]], w=[bf("w1c%d" % e)])
                else:
                    B.dma("gpsimd", slci[par],
                          cbuf[par].ap().rearrange("p (a f) -> p a f", a=4),
                          d["w2"][e].rearrange("(a p) f -> p a f", p=128)[:, 4 * cq:4 * cq + 4, :],
                          w=[cbufb[par]])
                    B.dma("gpsimd", slco[par],
                          d["w2c"][e].rearrange("(a p) f -> p a f", p=128)[:, 4 * cq:4 * cq + 4, :],
                          cbuf[par].ap().rearrange("p (a f) -> p a f", a=4),
                          r=[cbufb[par]], w=[bf("w2c%d" % e)])

        slx = [B.dslot(), B.dslot()]
        slxin = B.dslot()
        slxi2 = B.dslot()
        slxxo = B.dslot()
        slxx = [B.dslot(), B.dslot()]
        slo = B.dslot()

        for c in range(NCH):
            par = c % 2
            B.dma("sync", slx[par], xc[par].ap().rearrange("p (a t) -> p a t", a=NDT),
                  d["xT"][:, _sl(c, CW)].rearrange("(a p) t -> p a t", p=128), w=[xcb[par]])
            self.ln_fm(xc[par], [xcb[par]], CW, sq, rows, pG, pGb, pH, pHb, sqb, rowsb)
            for dd in range(NDT):
                B.v(lambda v, dd=dd, par=par: v.tensor_tensor(xin[:, _sl(dd, CW)], xc[par][:, _sl(dd, CW)], pG[:], ALU.add),
                    r=[xcb[par], pGb], w=[xinb])
                B.v(lambda v, dd=dd: v.tensor_tensor(xin[:, _sl(dd, CW)], xin[:, _sl(dd, CW)], pH[:], ALU.mult),
                    r=[pHb], w=[xinb])
            B.dma("sync", slxin, d["xins"][:, _sl(c, CW)].rearrange("(a p) t -> p a t", p=128),
                  xin.ap().rearrange("p (a t) -> p a t", a=NDT),
                  r=[xinb], w=self.dram_bufs("xins", c * CW, (c + 1) * CW))
            self.ln_fm(xin, [xinb], CW, sq, rows, pG, pGb, pH, pHb, sqb, rowsb)
            for dd in range(NDT):
                B.v(lambda v, dd=dd: v.tensor_tensor(xin[:, _sl(dd, CW)], xin[:, _sl(dd, CW)], pG[:], ALU.add),
                    r=[pGb], w=[xinb])
                B.v(lambda v, dd=dd: v.tensor_tensor(xin[:, _sl(dd, CW)], xin[:, _sl(dd, CW)], pH[:], ALU.mult),
                    r=[pHb], w=[xinb])
                B.v(lambda v, dd=dd: v.tensor_scalar(xin[:, _sl(dd, CW)], xin[:, _sl(dd, CW)],
                                                     pv("ln1w")[:, dd:dd + 1], pv("ln1b")[:, dd:dd + 1],
                                                     ALU.mult, ALU.add),
                    r=[bf("pvs")], w=[xinb])
            xb = xxc[par]
            xbb = xxcb[par]
            for dd in range(NDT):
                o0 = dd * CW
                B.v(lambda v, dd=dd: v.tensor_scalar_mul(tmp[:], xin[:, _sl(dd, CW)], dv("c1ts")[:, dd:dd + 1]),
                    r=[xinb, bf("dvs")], w=[tmpb])
                B.v(lambda v, dd=dd, o0=o0, xb=xb: v.scalar_tensor_tensor(
                    xb[:, o0 + 1:o0 + CW], xin[:, dd * CW:(dd + 1) * CW - 1],
                    dv("cmts")[:, dd:dd + 1], tmp[:, 1:CW], ALU.mult, ALU.add),
                    r=[xinb, tmpb, bf("dvs")], w=[xbb])
                B.v(lambda v, dd=dd, o0=o0, xb=xb: v.scalar_tensor_tensor(
                    xb[:, o0:o0 + 1], hbound[:, dd:dd + 1],
                    dv("cmts")[:, dd:dd + 1], tmp[:, 0:1], ALU.mult, ALU.add),
                    r=[hbndb, tmpb, bf("dvs")], w=[xbb])
                B.v(lambda v, dd=dd: v.tensor_copy(hbound[:, dd:dd + 1], xin[:, (dd + 1) * CW - 1:(dd + 1) * CW]),
                    r=[xinb], w=[hbndb])
            B.dma("sync", slxxo, d["xxxs"][:, _sl(c, CW)].rearrange("(a p) t -> p a t", p=128),
                  xb.ap().rearrange("p (a t) -> p a t", a=NDT),
                  r=[xbb], w=self.dram_bufs("xxxs", c * CW, (c + 1) * CW))
            emit_casts(3)

        for cc in range(NDT):
            for qc in range(NCH):
                par = qc % 2
                B.dma("sync", slxx[par], xxc[par].ap().rearrange("p (a t) -> p a t", a=NDT),
                      d["xxxs"][:, _sl(qc, CW)].rearrange("(a p) t -> p a t", p=128),
                      r=self.dram_bufs("xxxs", qc * CW, (qc + 1) * CW), w=[xxcb[par]])
                for dd in range(NDT):
                    B.t(lambda t, cc=cc, qc=qc, dd=dd, par=par: t.matmul(
                        pK[:], wk_s[:, dd * D + cc * 128: dd * D + (cc + 1) * 128],
                        xxc[par][:, _sl(dd, CW)],
                        start=(dd == 0), stop=(dd == NDT - 1)),
                        r=[wkb, xxcb[par]], w=[pKb])
                B.v(lambda v, qc=qc: v.tensor_copy(kf[:, _sl(qc, CW)], pK[:]), r=[pKb], w=[kfb])
                for dd in range(NDT):
                    B.t(lambda t, cc=cc, qc=qc, dd=dd, par=par: t.matmul(
                        pV[:], wv_s[:, dd * D + cc * 128: dd * D + (cc + 1) * 128],
                        xxc[par][:, _sl(dd, CW)],
                        start=(dd == 0), stop=(dd == NDT - 1)),
                        r=[wvb, xxcb[par]], w=[pVb])
                B.v(lambda v, qc=qc: v.tensor_copy(vf[:, _sl(qc, CW)], pV[:]), r=[pVb], w=[vfb])
                for dd in range(NDT):
                    B.t(lambda t, cc=cc, qc=qc, dd=dd, par=par: t.matmul(
                        pR[:], wr_s[:, dd * D + cc * 128: dd * D + (cc + 1) * 128],
                        xxc[par][:, _sl(dd, CW)],
                        start=(dd == 0), stop=(dd == NDT - 1)),
                        r=[wrb, xxcb[par]], w=[pRb])
                B.s(lambda s, qc=qc: s.activation(rf[:, _sl(qc, CW)], pR[:], AF.Sigmoid),
                    r=[pRb], w=[rfb])
                emit_casts(3)
            B.v(lambda v: v.reduce_max(cols[:, 0:1], kf[:], axis=AX.X), r=[kfb], w=[colb])
            B.v(lambda v, cc=cc: v.tensor_tensor(cols[:, 0:1], cols[:, 0:1], dv("reluU")[:, cc:cc + 1], ALU.add),
                r=[bf("dvs")], w=[colb])
            B.v(lambda v: v.tensor_scalar_mul(cols[:, 1:2], cols[:, 0:1], -1.0), r=[], w=[colb])
            B.s(lambda s: s.activation(cols[:, 2:3], cols[:, 1:2], AF.Exp), r=[colb], w=[colb])
            B.v(lambda v: v.tensor_copy(cols[:, 3:4], cols[:, 2:3]), r=[], w=[colb])
            B.v(lambda v: v.memset(cols[:, 4:5], 0.0), w=[colb])
            for qc in range(NCH):
                B.s(lambda s, qc=qc: s.activation(ekc[:], kf[:, _sl(qc, CW)], AF.Exp, bias=cols[:, 1:2]),
                    r=[kfb, colb], w=[ekcb])
                B.v(lambda v: v.tensor_copy(ekbf[:], ekc[:]), r=[ekcb], w=[ekbfb])
                for j in range(NSUB):
                    jp = j % 2
                    B.t(lambda t, j=j, jp=jp: t.transpose(pT2[jp][:], ekbf[:, _sl(j, 128)], self.ident()),
                        r=[ekbfb, bf("c_bf")], w=[pT2b[jp]])
                    B.v(lambda v, jp=jp: v.tensor_copy(smallbf2[jp][:], pT2[jp][:]),
                        r=[pT2b[jp]], w=[smallbf2b[jp]])
                    B.t(lambda t, jp=jp: t.matmul(pC[:, 0:129], smallbf2[jp][:], self.tri(),
                                                  start=True, stop=True),
                        r=[smallbf2b[jp], bf("c_bf")], w=[pCb])
                    B.v(lambda v, j=j: v.tensor_scalar_add(zh[:, _sl(j, 128)], pC[:, 0:128], cols[:, 3:4]),
                        r=[pCb, colb], w=[zhb])
                    B.v(lambda v: v.tensor_tensor(cols[:, 3:4], cols[:, 3:4], pC[:, 128:129], ALU.add),
                        r=[pCb], w=[colb])
                B.s(lambda s: s.activation(zh[:], zh[:], AF.Ln), r=[], w=[zhb])
                B.v(lambda v, qc=qc: v.scalar_tensor_tensor(tmp[:], kf[:, _sl(qc, CW)], cols[:, 1:2],
                                                            zh[:], ALU.add, ALU.subtract),
                    r=[kfb, colb, zhb], w=[tmpb])
                B.s(lambda s: s.activation(dchunk[:], tmp[:], AF.Exp), r=[tmpb], w=[dchb])
                B.v(lambda v: v.tensor_scalar_max(lgc[:], tmp[:], 0.0), r=[tmpb], w=[lgcb])
                B.v(lambda v: v.tensor_scalar_min(bc[:], dchunk[:], 1.0), r=[dchb], w=[bcb])
                B.v(lambda v, qc=qc: v.tensor_mul(bc[:], bc[:], vf[:, _sl(qc, CW)]), r=[vfb], w=[bcb])
                B.v(lambda v: v.tensor_copy(cols[:, 5:6], cols[:, 4:5]), r=[colb], w=[colb])
                B.v(lambda v: v.tensor_copy(ekbf[:], lgc[:]), r=[lgcb], w=[ekbfb])
                for j in range(NSUB):
                    jp = j % 2
                    B.t(lambda t, j=j, jp=jp: t.transpose(pT2[jp][:], ekbf[:, _sl(j, 128)], self.ident()),
                        r=[ekbfb, bf("c_bf")], w=[pT2b[jp]])
                    B.v(lambda v, jp=jp: v.tensor_copy(smallbf2[jp][:], pT2[jp][:]),
                        r=[pT2b[jp]], w=[smallbf2b[jp]])
                    B.t(lambda t, jp=jp: t.matmul(pC[:, 0:129], smallbf2[jp][:], self.tri(),
                                                  start=True, stop=True),
                        r=[smallbf2b[jp], bf("c_bf")], w=[pCb])
                    B.v(lambda v, j=j: v.tensor_tensor(ekc[:, _sl(j, 128)], pC[:, 0:128],
                                                       lgc[:, _sl(j, 128)], ALU.add),
                        r=[pCb, lgcb], w=[ekcb])
                B.s(lambda s: s.activation(t4[:], ekc[:], AF.Exp, scale=-1.0), r=[ekcb], w=[t4b])
                B.s(lambda s: s.activation(t5[:], ekc[:], AF.Exp), r=[ekcb], w=[t5b])
                B.v(lambda v: v.tensor_mul(t5[:], t5[:], bc[:]), r=[bcb], w=[t5b])
                B.v(lambda v: v.tensor_copy(ekbf[:], t5[:]), r=[t5b], w=[ekbfb])
                for j in range(NSUB):
                    jp = j % 2
                    B.t(lambda t, j=j, jp=jp: t.transpose(pT2[jp][:], ekbf[:, _sl(j, 128)], self.ident()),
                        r=[ekbfb, bf("c_bf")], w=[pT2b[jp]])
                    B.v(lambda v, jp=jp: v.tensor_copy(smallbf2[jp][:], pT2[jp][:]),
                        r=[pT2b[jp]], w=[smallbf2b[jp]])
                    B.t(lambda t, jp=jp: t.matmul(pC[:, 0:129], smallbf2[jp][:], self.tri(),
                                                  start=True, stop=True),
                        r=[smallbf2b[jp], bf("c_bf")], w=[pCb])
                    B.v(lambda v, j=j: v.tensor_tensor(zh[:, _sl(j, 128)], pC[:, 0:128],
                                                       t5[:, _sl(j, 128)], ALU.add),
                        r=[pCb, t5b], w=[zhb])
                for j in range(NSUB):
                    B.v(lambda v, j=j: v.tensor_scalar_add(sub1[:], zh[:, _sl(j, 128)], cols[:, 4:5]),
                        r=[zhb, colb], w=[sub1b])
                    B.v(lambda v, j=j: v.tensor_tensor(aac[:, _sl(j, 128)], sub1[:],
                                                       t4[:, _sl(j, 128)], ALU.mult),
                        r=[sub1b, t4b], w=[aacb])
                    B.v(lambda v, j=j: v.tensor_copy(cols[:, 4:5], aac[:, j * 128 + 127: j * 128 + 128]),
                        r=[aacb], w=[colb])
                B.s(lambda s, cc=cc: s.activation(t5[:], tmp[:], AF.Sigmoid,
                                                  bias=dv("nU")[:, cc:cc + 1], scale=-1.0),
                    r=[tmpb, bf("dvs")], w=[t5b])
                B.v(lambda v, qc=qc: v.tensor_mul(t4[:], dchunk[:], vf[:, _sl(qc, CW)]),
                    r=[dchb, vfb], w=[t4b])
                B.v(lambda v, cc=cc: v.tensor_scalar_mul(t4[:], t4[:], dv("eU")[:, cc:cc + 1]),
                    r=[bf("dvs")], w=[t4b])
                B.v(lambda v: v.tensor_tensor(t4[:, 1:CW], t4[:, 1:CW], aac[:, 0:CW - 1], ALU.add),
                    r=[aacb], w=[t4b])
                B.v(lambda v: v.tensor_tensor(t4[:, 0:1], t4[:, 0:1], cols[:, 5:6], ALU.add),
                    r=[colb], w=[t4b])
                B.v(lambda v: v.tensor_mul(t4[:], t4[:], t5[:]), r=[t5b], w=[t4b])
                B.v(lambda v, qc=qc: v.tensor_mul(t4[:], t4[:], rf[:, _sl(qc, CW)]), r=[rfb], w=[t4b])
                B.dma("sync", slxi2, xinld[:], d["xins"][_sl(cc, 128), _sl(qc, CW)],
                      r=self.dram_bufs("xins", qc * CW, (qc + 1) * CW), w=[xinldb])
                B.v(lambda v: v.tensor_tensor(x2c[:], t4[:], xinld[:], ALU.add),
                    r=[t4b, xinldb], w=[x2cb])
                if qc == NCH - 1:
                    B.v(lambda v, cc=cc: v.tensor_copy(self.x2last[:, cc:cc + 1], x2c[:, CW - 1:CW]),
                        r=[x2cb], w=[bf("x2last")])
                B.dma("sync", slo, d["x2s"][_sl(cc, 128), _sl(qc, CW)], x2c[:],
                      r=[x2cb], w=self.dram_bufs("x2s", qc * CW, (qc + 1) * CW))
        emit_casts(10 ** 6)
        self.end_phase()

    def phase_l0aa_B(self):
        B = self.B
        d = self.d
        bf = B.buf
        pv, dv = self.pvc, self.dvc
        self.begin_phase()
        sb, ps = self.sb, self.ps
        CW = 256
        NCH = T // CW
        NOT = 32

        x2c = sb("B_x2c", (128, NDT * CW)); x2cb = bf("B_x2c")
        xld = sb("B_xld", (128, NDT * CW)); xldb = bf("B_xld")
        sq = sb("B_sq", (128, CW)); sqb = bf("B_sq")
        h2 = sb("B_h2", (128, NDT * CW)); h2b = bf("B_h2")
        xxx2 = sb("B_xxx2", (128, NDT * CW), BF16); xxx2b = bf("B_xxx2")
        rows = sb("B_rows", (1, 6 * CW))
        kact = sb("B_kact", (128, NOT * CW), BF16); kactb = bf("B_kact")
        rc = sb("B_rc", (128, NDT * CW)); rcb = bf("B_rc")
        outb = sb("B_out", (128, NDT * CW)); outbb = bf("B_out")
        tmp = sb("B_tmp", (128, CW)); tmpb = bf("B_tmp")
        t2 = sb("B_t2", (128, CW)); t2b = bf("B_t2")
        h2bound = sb("B_h2b", (128, NDT)); h2bndb = bf("B_h2b")
        wcv_s = sb("B_wcv", (128, NOT * D), BF16); wcvb = bf("B_wcv")
        wcr_s = sb("B_wcr", (128, NDT * D), BF16); wcrb = bf("B_wcr")
        wck_q = [sb("B_wck0", (128, NDT * D), BF16), sb("B_wck1", (128, NDT * D), BF16)]
        wckb = [bf("B_wck0"), bf("B_wck1")]

        pG = ps("B_pG", (128, CW)); pGb = self.pb("B_pG")
        pH = ps("B_pH", (128, CW)); pHb = self.pb("B_pH")
        pKC = [ps("B_pKC0", (128, CW)), ps("B_pKC1", (128, CW))]
        pKCb = [self.pb("B_pKC0"), self.pb("B_pKC1")]
        pRC = ps("B_pRC", (128, CW)); pRCb = self.pb("B_pRC")
        pVV = [ps("B_pVV0", (128, CW)), ps("B_pVV1", (128, CW))]
        pVVb = [self.pb("B_pVV0"), self.pb("B_pVV1")]

        slw = B.dslot()
        B.dma("gpsimd", slw, wcv_s.ap().rearrange("p (a c) -> p a c", a=NOT),
              d["wcvT"].ap().rearrange("(a p) c -> p a c", p=128), w=[wcvb])
        B.dma("gpsimd", slw, wcr_s.ap().rearrange("p (a c) -> p a c", a=NDT),
              d["wcrT"].ap().rearrange("(a p) c -> p a c", p=128), w=[wcrb])
        slq = [B.dslot(), B.dslot()]

        B.s(lambda s: s.square(tmp[:, 0:NDT], self.x2last[:]), r=[bf("x2last")], w=[tmpb])
        B.t(lambda t: t.matmul(pG[0:1, 0:NDT], self.ones_col_f(), self.x2last[:],
                               start=True, stop=True), r=[bf("x2last"), bf("c_f")], w=[pGb])
        B.t(lambda t: t.matmul(pH[0:1, 0:NDT], self.ones_col_f(), tmp[:, 0:NDT],
                               start=True, stop=True), r=[tmpb, bf("c_f")], w=[pHb])
        B.v(lambda v: v.reduce_sum(rows[0:1, 0:1], pG[0:1, 0:NDT], axis=AX.X), r=[pGb], w=[bf("r_nm")])
        B.v(lambda v: v.reduce_sum(rows[0:1, 1:2], pH[0:1, 0:NDT], axis=AX.X), r=[pHb], w=[bf("r_var")])
        B.v(lambda v: v.tensor_scalar_mul(rows[0:1, 2:3], rows[0:1, 0:1], 1.0 / D), r=[bf("r_nm")], w=[bf("r_mean")])
        B.v(lambda v: v.tensor_scalar_mul(rows[0:1, 0:1], rows[0:1, 0:1], -1.0 / D), r=[bf("r_nm")], w=[bf("r_nm")])
        B.v(lambda v: v.tensor_scalar_mul(rows[0:1, 3:4], rows[0:1, 1:2], 1.0 / D), r=[bf("r_var")], w=[bf("r_var")])
        B.v(lambda v: v.tensor_mul(rows[0:1, 4:5], rows[0:1, 2:3], rows[0:1, 2:3]), r=[bf("r_mean")], w=[bf("r_tmp")])
        B.v(lambda v: v.tensor_sub(rows[0:1, 3:4], rows[0:1, 3:4], rows[0:1, 4:5]), r=[bf("r_tmp")], w=[bf("r_var")])
        B.s(lambda s: s.activation(rows[0:1, 3:4], rows[0:1, 3:4], AF.Ln, bias=self.c_f[0:1, 129:130]),
            r=[bf("c_f")], w=[bf("r_var")])
        B.s(lambda s: s.activation(rows[0:1, 1:2], rows[0:1, 3:4], AF.Exp, scale=-0.5),
            r=[bf("r_var")], w=[bf("r_rstd")])
        B.t(lambda t: t.matmul(pG[:, 0:1], self.c_orf[:], rows[0:1, 0:1], start=True, stop=True),
            r=[bf("r_nm"), bf("c_orf")], w=[pGb])
        B.t(lambda t: t.matmul(pH[:, 0:1], self.c_orf[:], rows[0:1, 1:2], start=True, stop=True),
            r=[bf("r_rstd"), bf("c_orf")], w=[pHb])
        B.v(lambda v: v.tensor_copy(tmp[:, 0:1], pG[:, 0:1]), r=[pGb], w=[tmpb])
        B.v(lambda v: v.tensor_copy(tmp[:, 1:2], pH[:, 0:1]), r=[pHb], w=[bf("B_tmp1")])
        B.v(lambda v: v.tensor_scalar(h2bound[:], self.x2last[:], tmp[:, 0:1], tmp[:, 1:2],
                                      ALU.add, ALU.mult),
            r=[bf("x2last"), tmpb, bf("B_tmp1")], w=[h2bndb])
        B.v(lambda v: v.tensor_mul(h2bound[:], h2bound[:], pv("ln2w")), r=[bf("pvs")], w=[h2bndb])
        B.v(lambda v: v.tensor_add(h2bound[:], h2bound[:], pv("ln2b")), r=[bf("pvs")], w=[h2bndb])

        slx2 = B.dslot()
        slx = B.dslot()
        slo = B.dslot()
        for c in range(NCH):
            B.dma("sync", slx2, x2c.ap().rearrange("p (a t) -> p a t", a=NDT),
                  d["x2s"][:, _sl(c, CW)].rearrange("(a p) t -> p a t", p=128),
                  r=self.dram_bufs("x2s", c * CW, (c + 1) * CW), w=[x2cb])
            B.dma("sync", slx, xld.ap().rearrange("p (a t) -> p a t", a=NDT),
                  d["xT"][:, _sl(c, CW)].rearrange("(a p) t -> p a t", p=128), w=[xldb])
            self.ln_fm(x2c, [x2cb], CW, sq, rows, pG, pGb, pH, pHb, sqb, bf("B_rowsX"))
            for dd in range(NDT):
                B.v(lambda v, dd=dd: v.tensor_tensor(h2[:, _sl(dd, CW)], x2c[:, _sl(dd, CW)], pG[:, 0:CW], ALU.add),
                    r=[x2cb, pGb], w=[h2b])
                B.v(lambda v, dd=dd: v.tensor_tensor(h2[:, _sl(dd, CW)], h2[:, _sl(dd, CW)], pH[:, 0:CW], ALU.mult),
                    r=[pHb], w=[h2b])
                B.v(lambda v, dd=dd: v.tensor_scalar(h2[:, _sl(dd, CW)], h2[:, _sl(dd, CW)],
                                                     pv("ln2w")[:, dd:dd + 1], pv("ln2b")[:, dd:dd + 1],
                                                     ALU.mult, ALU.add), r=[bf("pvs")], w=[h2b])
            for dd in range(NDT):
                B.v(lambda v, dd=dd: v.tensor_scalar_mul(tmp[:], h2[:, _sl(dd, CW)], dv("omts")[:, dd:dd + 1]),
                    r=[h2b, bf("dvs")], w=[tmpb])
                B.v(lambda v, dd=dd: v.scalar_tensor_tensor(
                    xxx2[:, dd * CW + 1:(dd + 1) * CW], h2[:, dd * CW:(dd + 1) * CW - 1],
                    pv("ts")[:, dd:dd + 1], tmp[:, 1:CW], ALU.mult, ALU.add),
                    r=[h2b, tmpb, bf("pvs")], w=[xxx2b])
                B.v(lambda v, dd=dd: v.scalar_tensor_tensor(
                    xxx2[:, dd * CW:dd * CW + 1], h2bound[:, dd:dd + 1],
                    pv("ts")[:, dd:dd + 1], tmp[:, 0:1], ALU.mult, ALU.add),
                    r=[h2bndb, tmpb, bf("pvs")], w=[xxx2b])
                B.v(lambda v, dd=dd: v.tensor_copy(h2bound[:, dd:dd + 1], h2[:, (dd + 1) * CW - 1:(dd + 1) * CW]),
                    r=[h2b], w=[h2bndb])
            for q in range(4):
                B.dma("gpsimd", slq[q % 2],
                      wck_q[q % 2].ap().rearrange("p (a c) -> p a c", a=NDT),
                      d["wckT"].ap().rearrange("(a p) c -> p a c", p=128)[:, :, q * D:(q + 1) * D],
                      w=[wckb[q % 2]])
                for oq in range(8):
                    ot = q * 8 + oq
                    pk = pKC[ot % 2]
                    pkb = pKCb[ot % 2]
                    for dd in range(NDT):
                        B.t(lambda t, q=q, oq=oq, dd=dd, pk=pk: t.matmul(
                            pk[:], wck_q[q % 2][:, dd * D + oq * 128: dd * D + (oq + 1) * 128],
                            xxx2[:, _sl(dd, CW)], start=(dd == 0), stop=(dd == NDT - 1)),
                            r=[wckb[q % 2], xxx2b], w=[pkb])
                    B.v(lambda v, pk=pk: v.tensor_scalar_max(t2[:], pk[:], 0.0), r=[pkb], w=[t2b])
                    B.s(lambda s, ot=ot: s.square(kact[:, _sl(ot, CW)], t2[:]), r=[t2b], w=[kactb])
            for ct in range(NDT):
                for dd in range(NDT):
                    B.t(lambda t, ct=ct, dd=dd: t.matmul(
                        pRC[:], wcr_s[:, dd * D + ct * 128: dd * D + (ct + 1) * 128],
                        xxx2[:, _sl(dd, CW)], start=(dd == 0), stop=(dd == NDT - 1)),
                        r=[wcrb, xxx2b], w=[pRCb])
                B.s(lambda s, ct=ct: s.activation(rc[:, _sl(ct, CW)], pRC[:], AF.Sigmoid),
                    r=[pRCb], w=[rcb])
            for ct in range(NDT):
                pvv = pVV[ct % 2]
                pvvb = pVVb[ct % 2]
                for ht in range(NOT):
                    B.t(lambda t, ct=ct, ht=ht, pvv=pvv: t.matmul(
                        pvv[:], wcv_s[:, ht * D + ct * 128: ht * D + (ct + 1) * 128],
                        kact[:, _sl(ht, CW)], start=(ht == 0), stop=(ht == NOT - 1)),
                        r=[wcvb, kactb], w=[pvvb])
                B.v(lambda v, ct=ct, pvv=pvv: v.tensor_tensor(
                    outb[:, _sl(ct, CW)], rc[:, _sl(ct, CW)], pvv[:], ALU.mult),
                    r=[rcb, pvvb], w=[outbb])
                B.v(lambda v, ct=ct: v.tensor_tensor(
                    outb[:, _sl(ct, CW)], outb[:, _sl(ct, CW)], x2c[:, _sl(ct, CW)], ALU.add),
                    r=[x2cb], w=[outbb])
                B.v(lambda v, ct=ct: v.tensor_tensor(
                    outb[:, _sl(ct, CW)], outb[:, _sl(ct, CW)], xld[:, _sl(ct, CW)], ALU.add),
                    r=[xldb], w=[outbb])
            B.dma("sync", slo, d["xres"][:, _sl(c, CW)].rearrange("(a p) t -> p a t", p=128),
                  outb.ap().rearrange("p (a t) -> p a t", a=NDT),
                  r=[outbb], w=self.dram_bufs("xres", c * CW, (c + 1) * CW))
        self.end_phase()


    def phase_moe(self, call):
        B = self.B
        d = self.d
        bf = B.buf
        self.begin_phase()
        sb, ps = self.sb, self.ps
        CW = 1024
        NCH = T // CW
        pre = "M%d_" % call

        def gb(n):
            return bf(pre + n)

        xdt = [sb(pre + "xdt0", (128, CW)), sb(pre + "xdt1", (128, CW))]
        xdtb = [gb("xdt0"), gb("xdt1")]
        sq = sb(pre + "sq", (128, CW)); sqb = gb("sq")
        rows = sb(pre + "rows", (1, 2 * CW))
        xin = sb(pre + "xin", (128, NDT * CW), BF16); xinb = gb("xin")
        xinm = sb(pre + "xinm", (128, NDT * CW), BF16)
        xinmb_d = [gb("xinm%d" % dd) for dd in range(NDT)]
        yac = sb(pre + "yac", (128, NDT * CW)); yacb = gb("yac")
        act = sb(pre + "act", (128, 16 * CW), BF16); actb = gb("act")
        maskb_s = [sb(pre + "maskb0", (128, CW), BF16), sb(pre + "maskb1", (128, CW), BF16)]
        maskbb = [gb("maskb0"), gb("maskb1")]
        lgs = sb(pre + "lgs", (128, 128)); lgsb = gb("lgs")
        maskc = sb(pre + "maskc", (128, 96)); maskcb = gb("maskc")
        maskT = sb(pre + "maskT", (8, CW), BF16); maskTb = gb("maskT")
        maxpT = sb(pre + "maxpT", (1, CW)); maxpTb = gb("maxpT")
        rw_s = sb(pre + "rw", (128, NDT * 8), BF16); rwb = gb("rw")
        w1b = [sb(pre + "w1_%d" % i, (128, NDT * CW), BF16) for i in range(2)]
        w1bb = [gb("w1_%d" % i) for i in range(2)]
        w2b = [sb(pre + "w2_%d" % i, (128, 4 * CW), BF16) for i in range(3)]
        w2bb = [gb("w2_%d" % i) for i in range(3)]
        maxb = sb(pre + "maxb", (128, CW)); maxbb = gb("maxb")
        outd2 = [sb(pre + "outd2_%d" % i, (128, CW)) for i in range(2)]
        outd2b = [gb("outd2_%d" % i) for i in range(2)]

        pp = [ps(pre + "pp%d" % i, (128, CW)) for i in range(4)]
        ppb = [self.pb(pre + "pp%d" % i) for i in range(4)]

        slw1 = [B.dslot() for _ in range(4)]
        slw2 = [B.dslot() for _ in range(4)]
        slfin = [B.dslot(), B.dslot()]
        slx = [B.dslot(), B.dslot()]
        slo = B.dslot()
        slc = B.dslot()
        B.dma("gpsimd", slc, rw_s.ap().rearrange("p (a e) -> p a e", a=NDT),
              d["rw"].ap().rearrange("(a p) e -> p a e", p=128), w=[rwb])

        r_nm, r_rstd, r_var, r_mean, r_tmp = (bf("r_nm"), bf("r_rstd"), bf("r_var"),
                                              bf("r_mean"), bf("r_tmp"))
        for c in range(NCH):
            t0 = c * CW
            for dd in range(NDT):
                par = dd % 2
                B.dma("sync", slx[par], xdt[par][:], d["xres"][_sl(dd, 128), _sl(c, CW)],
                      r=self.dram_bufs("xres", t0, t0 + CW), w=[xdtb[par]])
                B.s(lambda s, par=par: s.square(sq[:], xdt[par][:]), r=[xdtb[par]], w=[sqb])
                for hf in range(2):
                    B.t(lambda t, par=par, hf=hf, dd=dd: t.matmul(
                        pp[0][0:1, _sl(hf, 512)], self.ones_col_f(), xdt[par][:, _sl(hf, 512)],
                        start=(dd == 0), stop=(dd == NDT - 1)),
                        r=[xdtb[par], bf("c_f")], w=[ppb[0]])
                    B.t(lambda t, hf=hf, dd=dd: t.matmul(
                        pp[1][0:1, _sl(hf, 512)], self.ones_col_f(), sq[:, _sl(hf, 512)],
                        start=(dd == 0), stop=(dd == NDT - 1)),
                        r=[sqb, bf("c_f")], w=[ppb[1]])
            B.v(lambda v: v.tensor_scalar_mul(rows[0:1, 0:CW], pp[0][0:1, :], -1.0 / D),
                r=[ppb[0]], w=[r_nm])
            B.v(lambda v: v.tensor_mul(rows[0:1, CW:2 * CW], rows[0:1, 0:CW],
                                       rows[0:1, 0:CW]), r=[r_nm], w=[r_rstd])
            B.v(lambda v: v.scalar_tensor_tensor(rows[0:1, CW:2 * CW], pp[1][0:1, :], 1.0 / D,
                                                 rows[0:1, CW:2 * CW], ALU.mult, ALU.subtract),
                r=[ppb[1]], w=[r_rstd])
            B.s(lambda s: s.activation(rows[0:1, CW:2 * CW], rows[0:1, CW:2 * CW],
                                       AF.Ln, bias=self.c_f[0:1, 129:130]), r=[bf("c_f")], w=[r_rstd])
            B.s(lambda s: s.activation(rows[0:1, CW:2 * CW], rows[0:1, CW:2 * CW],
                                       AF.Exp, scale=-0.5), w=[r_rstd])
            for hf in range(2):
                B.t(lambda t, hf=hf: t.matmul(pp[0][:, _sl(hf, 512)], self.c_orf[:],
                                              rows[0:1, hf * 512:hf * 512 + 512],
                                              start=True, stop=True),
                    r=[r_nm, bf("c_orf")], w=[ppb[0]])
                B.t(lambda t, hf=hf: t.matmul(pp[1][:, _sl(hf, 512)], self.c_orf[:],
                                              rows[0:1, CW + hf * 512:CW + hf * 512 + 512],
                                              start=True, stop=True),
                    r=[r_rstd, bf("c_orf")], w=[ppb[1]])
            for dd in range(NDT):
                par = dd % 2
                B.dma("sync", slx[par], xdt[par][:], d["xres"][_sl(dd, 128), _sl(c, CW)],
                      r=self.dram_bufs("xres", t0, t0 + CW), w=[xdtb[par]])
                B.v(lambda v, par=par: v.tensor_tensor(sq[:], xdt[par][:], pp[0][:], ALU.add),
                    r=[xdtb[par], ppb[0]], w=[sqb])
                B.v(lambda v, dd=dd: v.tensor_tensor(xin[:, _sl(dd, CW)], sq[:], pp[1][:], ALU.mult),
                    r=[sqb, ppb[1]], w=[xinb])
            for tt in range(8):
                for dd in range(NDT):
                    B.t(lambda t, tt=tt, dd=dd: t.matmul(
                        pp[2][:, tt * 8:(tt + 1) * 8],
                        xin[:, dd * CW + tt * 128: dd * CW + (tt + 1) * 128],
                        rw_s[:, _sl(dd, 8)], start=(dd == 0), stop=(dd == NDT - 1)),
                        r=[xinb, rwb], w=[ppb[2]])
            B.v(lambda v: v.tensor_tensor(
                lgs[:, 0:64].rearrange("p (t e) -> p t e", t=8),
                pp[2][:, 0:64].rearrange("p (t e) -> p t e", t=8),
                self.rb_b.ap().unsqueeze(1).broadcast_to([128, 8, 8]), ALU.add),
                r=[ppb[2], bf("rb_b")], w=[lgsb])

            def tree3(alu, dst_lo):
                B.v(lambda v, alu=alu: v.tensor_tensor(
                    maskc[:, 0:32].rearrange("p (q one) -> p q one", one=1),
                    lgs[:, 0:64].rearrange("p (q two) -> p q two", two=2)[:, :, 0:1],
                    lgs[:, 0:64].rearrange("p (q two) -> p q two", two=2)[:, :, 1:2], alu),
                    r=[lgsb], w=[maskcb])
                B.v(lambda v, alu=alu: v.tensor_tensor(
                    maskc[:, 32:48].rearrange("p (q one) -> p q one", one=1),
                    maskc[:, 0:32].rearrange("p (q two) -> p q two", two=2)[:, :, 0:1],
                    maskc[:, 0:32].rearrange("p (q two) -> p q two", two=2)[:, :, 1:2], alu),
                    w=[maskcb])
                B.v(lambda v, alu=alu, dst_lo=dst_lo: v.tensor_tensor(
                    maskc[:, dst_lo:dst_lo + 8].rearrange("p (q one) -> p q one", one=1),
                    maskc[:, 32:48].rearrange("p (q two) -> p q two", two=2)[:, :, 0:1],
                    maskc[:, 32:48].rearrange("p (q two) -> p q two", two=2)[:, :, 1:2], alu),
                    w=[maskcb])

            tree3(ALU.max, 48)
            B.v(lambda v: v.tensor_tensor(
                lgs[:, 0:64].rearrange("p (t e) -> p t e", t=8),
                lgs[:, 0:64].rearrange("p (t e) -> p t e", t=8),
                maskc[:, 48:56].broadcast_to([128, 8, 8]), ALU.subtract),
                r=[maskcb], w=[lgsb])
            B.s(lambda s: s.activation(lgs[:, 0:64], lgs[:, 0:64], AF.Exp), w=[lgsb])
            tree3(ALU.add, 56)
            tree3(ALU.max, 72)
            B.s(lambda s: s.activation(maskc[:, 64:72], maskc[:, 56:64], AF.Ln),
                w=[maskcb])
            B.s(lambda s: s.activation(maskc[:, 64:72], maskc[:, 64:72], AF.Exp, scale=-1.0),
                w=[maskcb])
            B.v(lambda v: v.tensor_tensor(maskc[:, 80:88], maskc[:, 72:80], maskc[:, 64:72],
                                          ALU.mult), w=[maskcb])
            B.v(lambda v: v.memset(maskc[:, 88:96], 1.0), w=[maskcb])
            for e in range(8):
                B.v(lambda v, e=e: v.tensor_tensor(
                    lgs[:, 64:128].rearrange("p (t e) -> p t e", t=8)[:, :, e:e + 1],
                    lgs[:, 0:64].rearrange("p (t e) -> p t e", t=8)[:, :, e:e + 1],
                    maskc[:, 72:80].rearrange("p (t one) -> p t one", one=1), ALU.is_ge),
                    r=[maskcb], w=[lgsb])
                B.v(lambda v, e=e: v.tensor_tensor(
                    lgs[:, 64:128].rearrange("p (t e) -> p t e", t=8)[:, :, e:e + 1],
                    lgs[:, 64:128].rearrange("p (t e) -> p t e", t=8)[:, :, e:e + 1],
                    maskc[:, 88:96].rearrange("p (t one) -> p t one", one=1), ALU.mult),
                    r=[maskcb], w=[lgsb])
                B.v(lambda v, e=e: v.tensor_tensor(
                    maskc[:, 88:96].rearrange("p (t one) -> p t one", one=1),
                    maskc[:, 88:96].rearrange("p (t one) -> p t one", one=1),
                    lgs[:, 64:128].rearrange("p (t e) -> p t e", t=8)[:, :, e:e + 1],
                    ALU.subtract), r=[lgsb], w=[maskcb])
            for tt in range(8):
                B.t(lambda t, tt=tt: t.transpose(pp[2][0:8, tt * 128:(tt + 1) * 128],
                                                 lgs[:, 64 + tt * 8:64 + (tt + 1) * 8],
                                                 self.ident_f()),
                    r=[lgsb, bf("c_f")], w=[ppb[2]])
                B.t(lambda t, tt=tt: t.transpose(pp[3][0:1, tt * 128:(tt + 1) * 128],
                                                 maskc[:, 80 + tt:81 + tt], self.ident_f()),
                    r=[maskcb, bf("c_f")], w=[ppb[3]])
            B.s(lambda s: s.copy(maskT[:], pp[2][0:8, :]), r=[ppb[2]], w=[maskTb])
            B.s(lambda s: s.copy(maxpT[:], pp[3][0:1, :]), r=[ppb[3]], w=[maxpTb])
            def emit_mask(e, pt, ptb):
                par = e % 2
                for hf in range(2):
                    B.t(lambda t, e=e, hf=hf, pt=pt: t.matmul(
                        pt[:, _sl(hf, 512)], self.selc(e), maskT[0:8, _sl(hf, 512)],
                        start=True, stop=True),
                        r=[maskTb, bf("c_bf")], w=[ptb])
                B.s(lambda s, par=par, pt=pt: s.copy(maskb_s[par][:], pt[:]),
                    r=[ptb], w=[maskbb[par]])

            def emit_xinm(e):
                par = e % 2
                for dd in range(NDT):
                    B.v(lambda v, dd=dd, par=par: v.tensor_tensor(
                        xinm[:, _sl(dd, CW)], xin[:, _sl(dd, CW)], maskb_s[par][:], ALU.mult),
                        r=[xinb, maskbb[par]], w=[xinmb_d[dd]])

            w1i = 0
            w2i = 0
            emit_mask(0, pp[3], ppb[3])
            emit_xinm(0)
            for e in range(8):
                for fh in range(2):
                    for qi in range(4):
                        q = (fh * 2 + qi) if qi < 2 else (4 + fh * 2 + qi - 2)
                        w1q = "sync" if (fh * 4 + qi) % 2 == 0 else "gpsimd"
                        B.dma(w1q, slw1[qi % 2],
                              w1b[qi % 2].ap().rearrange("p (a f) -> p a f", a=NDT),
                              d["w1c"][e].rearrange("(a p) f -> p a f", p=128)[:, :, q * CW:(q + 1) * CW],
                              r=[bf("w1c%d" % e)], w=[w1bb[qi % 2]])
                        for ft in range(8):
                            floc = (qi % 2) * 8 + ft
                            for hf in range(2):
                                slot = (ft * 2 + hf) % 4
                                pa = pp[2 + slot // 2]
                                pab = ppb[2 + slot // 2][slot % 2: slot % 2 + 1]
                                bank = _sl(slot % 2, 512)
                                for dd in range(NDT):
                                    B.t(lambda t, qi=qi, ft=ft, hf=hf, dd=dd, pa=pa, bank=bank: t.matmul(
                                        pa[:, bank],
                                        w1b[qi % 2][:, dd * CW + ft * 128: dd * CW + (ft + 1) * 128],
                                        xinm[:, dd * CW + hf * 512: dd * CW + hf * 512 + 512],
                                        start=(dd == 0), stop=(dd == NDT - 1)),
                                        r=[w1bb[qi % 2], xinmb_d[dd]], w=pab)
                                aslot = floc * CW + hf * 512
                                if qi < 2:
                                    B.s(lambda s, aslot=aslot, pa=pa, bank=bank: s.activation(
                                        act[:, aslot:aslot + 512], pa[:, bank], AF.Silu),
                                        r=pab, w=[actb])
                                else:
                                    B.v(lambda v, aslot=aslot, pa=pa, bank=bank: v.tensor_tensor(
                                        act[:, aslot:aslot + 512], act[:, aslot:aslot + 512],
                                        pa[:, bank], ALU.mult),
                                        r=pab, w=[actb])
                    if fh == 1 and e < 7:
                        emit_mask(e + 1, pp[0], ppb[0])
                        emit_xinm(e + 1)
                    for pq in range(2):
                        for q2 in range(4):
                            q = fh * 4 + q2
                            wi = w2i % 3
                            w2q = "sync" if w2i % 2 == 0 else "gpsimd"
                            w2i += 1
                            B.dma(w2q, slw2[wi],
                                  w2b[wi].ap().rearrange("p (a dcol) -> p a dcol", a=4),
                                  d["w2c"][e].rearrange("(a p) dcol -> p a dcol", p=128)[:, 4 * q:4 * q + 4, :],
                                  w=[w2bb[wi]])
                            for ft in range(4):
                                floc = q2 * 4 + ft
                                for dt in range(4):
                                    dtg = pq * 4 + dt
                                    for hf in range(2):
                                        B.t(lambda t, wi=wi, q2=q2, ft=ft, dt=dt, dtg=dtg, hf=hf, floc=floc: t.matmul(
                                            pp[dt][:, _sl(hf, 512)],
                                            w2b[wi][:, ft * CW + dtg * 128: ft * CW + (dtg + 1) * 128],
                                            act[:, floc * CW + hf * 512: floc * CW + hf * 512 + 512],
                                            start=(q2 == 0 and ft == 0), stop=(q2 == 3 and ft == 3)),
                                            r=[w2bb[wi], actb], w=[ppb[dt]])
                        for dt in range(4):
                            dtg = pq * 4 + dt
                            if e == 0 and fh == 0:
                                B.v(lambda v, dt=dt, dtg=dtg: v.tensor_copy(
                                    yac[:, _sl(dtg, CW)], pp[dt][:]), r=[ppb[dt]], w=[yacb])
                            else:
                                B.v(lambda v, dt=dt, dtg=dtg: v.tensor_tensor(
                                    yac[:, _sl(dtg, CW)], yac[:, _sl(dtg, CW)], pp[dt][:], ALU.add),
                                    r=[ppb[dt]], w=[yacb])
            for hf in range(2):
                B.t(lambda t, hf=hf: t.matmul(pp[2][:, _sl(hf, 512)], self.c_orf[:],
                                              maxpT[0:1, hf * 512:hf * 512 + 512],
                                              start=True, stop=True),
                    r=[maxpTb, bf("c_orf")], w=[ppb[2]])
            B.s(lambda s: s.copy(maxb[:], pp[2][:]), r=[ppb[2]], w=[maxbb])
            for dd in range(NDT):
                par = dd % 2
                B.dma("gpsimd", slfin[par], xdt[par][:], d["xres"][_sl(dd, 128), _sl(c, CW)],
                      r=self.dram_bufs("xres", t0, t0 + CW), w=[xdtb[par]])
                B.v(lambda v, dd=dd, par=par: v.tensor_tensor(
                    outd2[par][:], yac[:, _sl(dd, CW)], maxb[:], ALU.mult),
                    r=[yacb, maxbb], w=[outd2b[par]])
                B.v(lambda v, par=par: v.tensor_tensor(
                    outd2[par][:], outd2[par][:], xdt[par][:], ALU.add),
                    r=[xdtb[par]], w=[outd2b[par]])
                B.dma("gpsimd", slo, d["xres"][_sl(dd, 128), _sl(c, CW)], outd2[par][:],
                      r=[outd2b[par]], w=self.dram_bufs("xres", t0, t0 + CW))
        self.end_phase()


    def ln_stream(self, c, CW, hb, out_fn):
        B = self.B
        bf = B.buf
        d = self.d
        xdt, xdtb, sq, sqb = hb["xdt"], hb["xdtb"], hb["sq"], hb["sqb"]
        rows, pp0, pp0b, pp1, pp1b, slx = (hb["rows"], hb["pp0"], hb["pp0b"],
                                           hb["pp1"], hb["pp1b"], hb["slx"])
        t0 = c * CW
        nh = CW // 512
        r0, r1 = bf("r_s0"), bf("r_s1")
        for dd in range(NDT):
            par = dd % 2
            B.dma("sync", slx[par], xdt[par][:, 0:CW], d["xres"][_sl(dd, 128), _sl(c, CW)],
                  r=self.dram_bufs("xres", t0, t0 + CW), w=[xdtb[par]])
            B.s(lambda s, par=par: s.square(sq[:, 0:CW], xdt[par][:, 0:CW]),
                r=[xdtb[par]], w=[sqb])
            for hf in range(nh):
                B.t(lambda t, par=par, hf=hf, dd=dd: t.matmul(
                    pp0[0:1, _sl(hf, 512)], self.ones_col_f(), xdt[par][:, _sl(hf, 512)],
                    start=(dd == 0), stop=(dd == NDT - 1)),
                    r=[xdtb[par], bf("c_f")], w=[pp0b])
                B.t(lambda t, hf=hf, dd=dd: t.matmul(
                    pp1[0:1, _sl(hf, 512)], self.ones_col_f(), sq[:, _sl(hf, 512)],
                    start=(dd == 0), stop=(dd == NDT - 1)),
                    r=[sqb, bf("c_f")], w=[pp1b])
        B.v(lambda v: v.tensor_scalar_mul(rows[0:1, 0:CW], pp0[0:1, 0:CW], 1.0 / D),
            r=[pp0b], w=[r0])
        B.v(lambda v: v.tensor_scalar_mul(rows[0:1, CW:2 * CW], pp1[0:1, 0:CW], 1.0 / D),
            r=[pp1b], w=[r1])
        B.v(lambda v: v.tensor_mul(rows[0:1, 0:CW], rows[0:1, 0:CW], rows[0:1, 0:CW]),
            r=[], w=[r0])
        B.v(lambda v: v.tensor_sub(rows[0:1, CW:2 * CW], rows[0:1, CW:2 * CW], rows[0:1, 0:CW]),
            r=[r0], w=[r1])
        B.v(lambda v: v.tensor_scalar_mul(rows[0:1, 0:CW], pp0[0:1, 0:CW], -1.0 / D),
            r=[pp0b], w=[r0])
        B.s(lambda s: s.activation(rows[0:1, CW:2 * CW], rows[0:1, CW:2 * CW],
                                   AF.Ln, bias=self.c_f[0:1, 129:130]), r=[bf("c_f")], w=[r1])
        B.s(lambda s: s.activation(rows[0:1, CW:2 * CW], rows[0:1, CW:2 * CW],
                                   AF.Exp, scale=-0.5), r=[], w=[r1])
        for hf in range(nh):
            B.t(lambda t, hf=hf: t.matmul(pp0[:, _sl(hf, 512)], self.c_orf[:],
                                          rows[0:1, hf * 512:hf * 512 + 512],
                                          start=True, stop=True),
                r=[r0, bf("c_orf")], w=[pp0b])
            B.t(lambda t, hf=hf: t.matmul(pp1[:, _sl(hf, 512)], self.c_orf[:],
                                          rows[0:1, CW + hf * 512:CW + hf * 512 + 512],
                                          start=True, stop=True),
                r=[r1, bf("c_orf")], w=[pp1b])
        for dd in range(NDT):
            par = dd % 2
            B.dma("sync", slx[par], xdt[par][:, 0:CW], d["xres"][_sl(dd, 128), _sl(c, CW)],
                  r=self.dram_bufs("xres", t0, t0 + CW), w=[xdtb[par]])
            out_fn(dd, par)

    def phase_attn0(self):
        B = self.B
        d = self.d
        bf = B.buf
        self.begin_phase()
        sb, ps = self.sb, self.ps
        CW = 1024
        NCH = T // CW
        SC = 1.0 / (128.0 ** 0.5)

        def gb(n):
            return bf("A0_" + n)

        xdt = [sb("A0_xdt0", (128, CW)), sb("A0_xdt1", (128, CW))]
        xdtb = [gb("xdt0"), gb("xdt1")]
        sq = sb("A0_sq", (128, CW)); sqb = gb("sq")
        rows = sb("A0_rows", (1, 4 * CW))
        xin = sb("A0_xin", (128, NDT * T), BF16); xinb = gb("xin")
        qt = sb("A0_qt", (128, T), BF16); qtb = gb("qt")
        kt = sb("A0_kt", (128, T), BF16); ktb = gb("kt")
        vt = sb("A0_vt", (128, T), BF16); vtb = gb("vt")
        srow = sb("A0_srow", (128, T)); srowb = gb("srow")
        pbuf = sb("A0_pbuf", (128, T), BF16); pbufb = gb("pbuf")
        ptb_s = sb("A0_ptb", (128, T), BF16); ptbb = gb("ptb")
        attno = sb("A0_attno", (128, T), BF16); attnob = gb("attno")
        col = sb("A0_col", (128, 4)); colb = gb("col")
        outdt = sb("A0_outdt", (128, CW)); outdtb = gb("outdt")
        wq_s = sb("A0_wq", (128, NDT * 128), BF16)
        wk_s = sb("A0_wk", (128, NDT * 128), BF16)
        wv_s = sb("A0_wv", (128, NDT * 128), BF16)
        wo_s = sb("A0_wo", (128, NDT * 128), BF16)

        pG = ps("A0_pG", (128, CW)); pGb = self.pb("A0_pG")
        pH = ps("A0_pH", (128, CW)); pHb = self.pb("A0_pH")
        pS = [ps("A0_pS0", (128, 512)), ps("A0_pS1", (128, 512))]
        pSb = [self.pb("A0_pS0"), self.pb("A0_pS1")]
        pT_ = ps("A0_pT", (128, 256), BF16); pTb = self.pb("A0_pT")
        pAV = ps("A0_pAV", (128, 128)); pAVb = self.pb("A0_pAV")

        slc = B.dslot()
        for nm, dst in [("awq", wq_s), ("awk", wk_s), ("awv", wv_s)]:
            B.dma("gpsimd", slc, dst.ap().rearrange("p (a o) -> p a o", a=NDT),
                  d[nm].ap().rearrange("(a p) o -> p a o", p=128), w=[gb(nm)])
        B.dma("gpsimd", slc, wo_s.ap().rearrange("p (a o) -> p a o", a=NDT),
              d["awo"].ap().rearrange("p (a o) -> p a o", a=NDT), w=[gb("awo")])
        slx = [B.dslot(), B.dslot()]
        slo = B.dslot()
        hb = dict(xdt=xdt, xdtb=xdtb, sq=sq, sqb=sqb, rows=rows,
                  pp0=pG, pp0b=pGb, pp1=pH, pp1b=pHb, slx=slx)

        for c in range(NCH):
            def out_fn(dd, par, c=c):
                B.v(lambda v, par=par: v.tensor_tensor(sq[:], xdt[par][:], pG[:], ALU.add),
                    r=[xdtb[par], pGb], w=[sqb])
                B.v(lambda v, dd=dd, c=c: v.tensor_tensor(
                    xin[:, dd * T + c * CW: dd * T + (c + 1) * CW], sq[:], pH[:], ALU.mult),
                    r=[sqb, pHb], w=[xinb])
            self.ln_stream(c, CW, hb, out_fn)
        for qc in range(8):
            for dd in range(NDT):
                B.t(lambda t, qc=qc, dd=dd: t.matmul(
                    pS[0][:], wq_s[:, _sl(dd, 128)],
                    xin[:, dd * T + qc * 512: dd * T + (qc + 1) * 512],
                    start=(dd == 0), stop=(dd == NDT - 1)), r=[gb("awq"), xinb], w=[pSb[0]])
            B.v(lambda v, qc=qc: v.tensor_copy(qt[:, _sl(qc, 512)], pS[0][:]), r=[pSb[0]], w=[qtb])
            for dd in range(NDT):
                B.t(lambda t, qc=qc, dd=dd: t.matmul(
                    pS[1][:], wk_s[:, _sl(dd, 128)],
                    xin[:, dd * T + qc * 512: dd * T + (qc + 1) * 512],
                    start=(dd == 0), stop=(dd == NDT - 1)), r=[gb("awk"), xinb], w=[pSb[1]])
            B.v(lambda v, qc=qc: v.tensor_copy(kt[:, _sl(qc, 512)], pS[1][:]), r=[pSb[1]], w=[ktb])
        for tt in range(32):
            for dd in range(NDT):
                B.t(lambda t, tt=tt, dd=dd: t.matmul(
                    pAV[:], xin[:, dd * T + tt * 128: dd * T + (tt + 1) * 128],
                    wv_s[:, _sl(dd, 128)],
                    start=(dd == 0), stop=(dd == NDT - 1)), r=[xinb, gb("awv")], w=[pAVb])
            B.v(lambda v, tt=tt: v.tensor_copy(vt[:, _sl(tt, 128)], pAV[:]), r=[pAVb], w=[vtb])
        for tt in range(32):
            W = tt * 128 + 128
            nfull = (tt * 128) // 512
            for kc in range(nfull):
                B.t(lambda t, tt=tt, kc=kc: t.matmul(
                    pS[kc % 2][:], qt[:, _sl(tt, 128)], kt[:, _sl(kc, 512)],
                    start=True, stop=True), r=[qtb, ktb], w=[pSb[kc % 2]])
                B.s(lambda s, tt=tt, kc=kc: s.activation(
                    srow[:, _sl(kc, 512)], pS[kc % 2][:], AF.Copy, scale=SC),
                    r=[pSb[kc % 2]], w=[srowb])
            for k1 in range(nfull * 4, tt + 1):
                B.t(lambda t, tt=tt, k1=k1: t.matmul(
                    pS[k1 % 2][:, 0:128], qt[:, _sl(tt, 128)], kt[:, _sl(k1, 128)],
                    start=True, stop=True), r=[qtb, ktb], w=[pSb[k1 % 2]])
                if k1 == tt:
                    B.v(lambda v, tt=tt, k1=k1: v.scalar_tensor_tensor(
                        srow[:, _sl(k1, 128)], pS[k1 % 2][:, 0:128], SC, self.causal(),
                        ALU.mult, ALU.add), r=[pSb[k1 % 2], bf("c_f")], w=[srowb])
                else:
                    B.s(lambda s, tt=tt, k1=k1: s.activation(
                        srow[:, _sl(k1, 128)], pS[k1 % 2][:, 0:128], AF.Copy, scale=SC),
                        r=[pSb[k1 % 2]], w=[srowb])
            B.v(lambda v, W=W: v.reduce_max(col[:, 0:1], srow[:, 0:W], axis=AX.X),
                r=[srowb], w=[colb])
            B.v(lambda v: v.tensor_scalar_mul(col[:, 1:2], col[:, 0:1], -1.0), r=[colb], w=[colb])
            B.s(lambda s, W=W: s.activation(srow[:, 0:W], srow[:, 0:W], AF.Exp, bias=col[:, 1:2]),
                r=[colb], w=[srowb])
            B.v(lambda v, W=W: v.reduce_sum(col[:, 2:3], srow[:, 0:W], axis=AX.X),
                r=[srowb], w=[colb])
            B.v(lambda v: v.reciprocal(col[:, 3:4], col[:, 2:3]), r=[colb], w=[colb])
            B.v(lambda v, W=W: v.tensor_scalar_mul(pbuf[:, 0:W], srow[:, 0:W], col[:, 3:4]),
                r=[srowb, colb], w=[pbufb])
            for j in range(tt + 1):
                B.t(lambda t, j=j: t.transpose(pT_[:, _sl(j % 2, 128)], pbuf[:, _sl(j, 128)],
                                               self.ident()), r=[pbufb, bf("c_bf")], w=[pTb])
                B.v(lambda v, j=j: v.tensor_copy(ptb_s[:, _sl(j, 128)], pT_[:, _sl(j % 2, 128)]),
                    r=[pTb], w=[ptbb])
            for j in range(tt + 1):
                B.t(lambda t, tt=tt, j=j: t.matmul(
                    pAV[:], vt[:, _sl(j, 128)], ptb_s[:, _sl(j, 128)],
                    start=(j == 0), stop=(j == tt)), r=[vtb, ptbb], w=[pAVb])
            B.s(lambda s, tt=tt: s.copy(attno[:, _sl(tt, 128)], pAV[:]), r=[pAVb], w=[attnob])
        for c in range(NCH):
            t0 = c * CW
            for dd in range(NDT):
                par = dd % 2
                B.dma("sync", slx[par], xdt[par][:], d["xres"][_sl(dd, 128), _sl(c, CW)],
                      r=self.dram_bufs("xres", t0, t0 + CW), w=[xdtb[par]])
                for hf in range(2):
                    B.t(lambda t, c=c, dd=dd, hf=hf: t.matmul(
                        pS[hf][:], wo_s[:, _sl(dd, 128)],
                        attno[:, c * CW + hf * 512: c * CW + hf * 512 + 512],
                        start=True, stop=True), r=[gb("awo"), attnob], w=[pSb[hf]])
                    B.v(lambda v, par=par, hf=hf: v.tensor_tensor(
                        outdt[:, _sl(hf, 512)], xdt[par][:, _sl(hf, 512)], pS[hf][:], ALU.add),
                        r=[xdtb[par], pSb[hf]], w=[outdtb])
                B.dma("sync", slo, d["xres"][_sl(dd, 128), _sl(c, CW)], outdt[:],
                      r=[outdtb], w=self.dram_bufs("xres", t0, t0 + CW))
        self.end_phase()


    def phase_ssm(self):
        B = self.B
        d = self.d
        bf = B.buf
        pv, dv = self.pvc, self.dvc
        self.begin_phase()
        sb, ps = self.sb, self.ps
        CW = 1024
        NCH = T // CW

        def gb(n):
            return bf("S_" + n)

        xdt = [sb("S_xdt0", (128, CW)), sb("S_xdt1", (128, CW))]
        xdtb = [gb("xdt0"), gb("xdt1")]
        sq = sb("S_sq", (128, CW)); sqb = gb("sq")
        rows = sb("S_rows", (1, 4 * CW))
        xinc = sb("S_xin", (128, NDT * CW), BF16); xincb = gb("xin")
        bb_ = [sb("S_B0", (128, T)), sb("S_B1", (128, T))]
        bbb_ = [gb("B0"), gb("B1")]
        sw = sb("S_sw", (128, T)); swb = gb("sw")
        scanc = sb("S_scanc", (128, 24)); scancb = gb("scanc")
        cmat = sb("S_cmat", (64, D)); cmatb = gb("cmat")
        ybuf = sb("S_ybuf", (128, NDT * CW)); ybufb = gb("ybuf")
        outdt = sb("S_outdt", (128, CW)); outdtb = gb("outdt")

        pG = ps("S_pG", (128, CW)); pGb = self.pb("S_pG")
        pH = ps("S_pH", (128, CW)); pHb = self.pb("S_pH")
        pBu = ps("S_pBu", (128, 512)); pBub = self.pb("S_pBu")
        pY = ps("S_pY", (128, 512)); pYb = self.pb("S_pY")

        slc = B.dslot()
        B.dma("sync", slc, cmat[:], d["cmatT"][:], w=[cmatb])
        slx = [B.dslot(), B.dslot()]
        slo = B.dslot()
        hb = dict(xdt=xdt, xdtb=xdtb, sq=sq, sqb=sqb, rows=rows,
                  pp0=pG, pp0b=pGb, pp1=pH, pp1b=pHb, slx=slx)

        for j in range(12):
            B.v(lambda v, j=j: v.tensor_copy(scanc[0:64, 2 * j:2 * j + 1], self.pw_re[:, j:j + 1]),
                r=[bf("pw")], w=[scancb])
            B.v(lambda v, j=j: v.tensor_copy(scanc[64:128, 2 * j:2 * j + 1], self.pw_re[:, j:j + 1]),
                r=[bf("pw")], w=[scancb])
            B.v(lambda v, j=j: v.tensor_scalar_mul(scanc[0:64, 2 * j + 1:2 * j + 2],
                                                   self.pw_im[:, j:j + 1], -1.0),
                r=[bf("pw")], w=[scancb])
            B.v(lambda v, j=j: v.tensor_copy(scanc[64:128, 2 * j + 1:2 * j + 2], self.pw_im[:, j:j + 1]),
                r=[bf("pw")], w=[scancb])

        for c in range(NCH):
            def out_fn(dd, par, c=c):
                B.v(lambda v, par=par: v.tensor_tensor(sq[:], xdt[par][:], pG[:], ALU.add),
                    r=[xdtb[par], pGb], w=[sqb])
                B.v(lambda v, dd=dd: v.tensor_tensor(xinc[:, _sl(dd, CW)], sq[:], pH[:], ALU.mult),
                    r=[sqb, pHb], w=[xincb])
            self.ln_stream(c, CW, hb, out_fn)
            for hf in range(2):
                for dd in range(NDT):
                    B.t(lambda t, c=c, hf=hf, dd=dd: t.matmul(
                        pBu[:], self.bbT[:, _sl(dd, 128)],
                        xinc[:, dd * CW + hf * 512: dd * CW + hf * 512 + 512],
                        start=(dd == 0), stop=(dd == NDT - 1)),
                        r=[bf("bbT"), xincb], w=[pBub])
                B.v(lambda v, c=c, hf=hf: v.tensor_copy(
                    bb_[0][:, c * CW + hf * 512: c * CW + hf * 512 + 512], pBu[:]),
                    r=[pBub], w=[bbb_[0]])
        for j in range(12):
            o = 1 << j
            src_ = bb_[j % 2]
            srcb = bbb_[j % 2]
            dst = bb_[(j + 1) % 2]
            dstb = bbb_[(j + 1) % 2]
            B.v(lambda v, o=o, src_=src_: v.tensor_copy(sw[0:64, 0:T - o], src_[64:128, 0:T - o]),
                r=[srcb], w=[swb])
            B.v(lambda v, o=o, src_=src_: v.tensor_copy(sw[64:128, 0:T - o], src_[0:64, 0:T - o]),
                r=[srcb], w=[swb])
            B.v(lambda v, o=o, j=j, src_=src_, dst=dst: v.scalar_tensor_tensor(
                dst[:, o:T], src_[:, 0:T - o], scanc[:, 2 * j:2 * j + 1], src_[:, o:T],
                ALU.mult, ALU.add), r=[srcb, scancb], w=[dstb])
            B.v(lambda v, o=o, j=j, dst=dst: v.scalar_tensor_tensor(
                dst[:, o:T], sw[:, 0:T - o], scanc[:, 2 * j + 1:2 * j + 2], dst[:, o:T],
                ALU.mult, ALU.add), r=[swb, scancb], w=[dstb])
            B.v(lambda v, o=o, src_=src_, dst=dst: v.tensor_copy(dst[:, 0:o], src_[:, 0:o]),
                r=[srcb], w=[dstb])
        hfin = bb_[0]
        hfinb = bbb_[0]
        for c in range(NCH):
            for hf in range(2):
                for dd in range(NDT):
                    B.t(lambda t, c=c, hf=hf, dd=dd: t.matmul(
                        pY[:], cmat[0:64, _sl(dd, 128)],
                        hfin[0:64, c * CW + hf * 512: c * CW + hf * 512 + 512],
                        start=True, stop=True), r=[cmatb, hfinb], w=[pYb])
                    B.v(lambda v, c=c, hf=hf, dd=dd: v.tensor_copy(
                        ybuf[:, dd * CW + _sl(hf, 512).start: dd * CW + hf * 512 + 512], pY[:]),
                        r=[pYb], w=[ybufb])

            def out_fn2(dd, par, c=c):
                B.v(lambda v, par=par: v.tensor_tensor(sq[:], xdt[par][:], pG[:], ALU.add),
                    r=[xdtb[par], pGb], w=[sqb])
                B.v(lambda v: v.tensor_tensor(sq[:], sq[:], pH[:], ALU.mult),
                    r=[pHb], w=[sqb])
                B.v(lambda v, dd=dd: v.scalar_tensor_tensor(
                    ybuf[:, _sl(dd, CW)], sq[:], dv("dp1")[:, dd:dd + 1], ybuf[:, _sl(dd, CW)],
                    ALU.mult, ALU.add), r=[sqb, bf("dvs")], w=[ybufb])
            self.ln_stream(c, CW, hb, out_fn2)
            for dd in range(NDT):
                B.s(lambda s, dd=dd: s.square(sq[:], ybuf[:, _sl(dd, CW)]), r=[ybufb], w=[sqb])
                for hf in range(2):
                    B.t(lambda t, dd=dd, hf=hf: t.matmul(
                        pG[0:1, _sl(hf, 512)], self.ones_col_f(), ybuf[:, dd * CW + hf * 512: dd * CW + hf * 512 + 512],
                        start=(dd == 0), stop=(dd == NDT - 1)), r=[ybufb, bf("c_f")], w=[pGb])
                    B.t(lambda t, dd=dd, hf=hf: t.matmul(
                        pH[0:1, _sl(hf, 512)], self.ones_col_f(), sq[:, _sl(hf, 512)],
                        start=(dd == 0), stop=(dd == NDT - 1)), r=[sqb, bf("c_f")], w=[pHb])
            r_nm, r_rstd, r_var, r_mean, r_tmp = (bf("r_nm"), bf("r_rstd"), bf("r_var"),
                                                  bf("r_mean"), bf("r_tmp"))
            B.v(lambda v: v.tensor_scalar_mul(rows[0:1, 2 * CW:3 * CW], pG[0:1, :], 1.0 / D),
                r=[pGb], w=[r_mean])
            B.v(lambda v: v.tensor_scalar_mul(rows[0:1, 0:CW], pG[0:1, :], -1.0 / D),
                r=[pGb], w=[r_nm])
            B.v(lambda v: v.tensor_scalar_mul(rows[0:1, 3 * CW:4 * CW], pH[0:1, :], 1.0 / D),
                r=[pHb], w=[r_var])
            B.v(lambda v: v.tensor_mul(rows[0:1, CW:2 * CW], rows[0:1, 2 * CW:3 * CW],
                                       rows[0:1, 2 * CW:3 * CW]), r=[r_mean], w=[r_tmp])
            B.v(lambda v: v.tensor_sub(rows[0:1, 3 * CW:4 * CW], rows[0:1, 3 * CW:4 * CW],
                                       rows[0:1, CW:2 * CW]), r=[r_tmp], w=[r_var])
            B.s(lambda s: s.activation(rows[0:1, 3 * CW:4 * CW], rows[0:1, 3 * CW:4 * CW],
                                       AF.Ln, bias=self.c_f[0:1, 129:130]), r=[bf("c_f")], w=[r_var])
            B.s(lambda s: s.activation(rows[0:1, CW:2 * CW], rows[0:1, 3 * CW:4 * CW],
                                       AF.Exp, scale=-0.5), r=[r_var], w=[r_rstd])
            for hf in range(2):
                B.t(lambda t, hf=hf: t.matmul(pG[:, _sl(hf, 512)], self.c_orf[:],
                                              rows[0:1, hf * 512:hf * 512 + 512], start=True, stop=True),
                    r=[r_nm, bf("c_orf")], w=[pGb])
                B.t(lambda t, hf=hf: t.matmul(pH[:, _sl(hf, 512)], self.c_orf[:],
                                              rows[0:1, CW + hf * 512:CW + hf * 512 + 512], start=True, stop=True),
                    r=[r_rstd, bf("c_orf")], w=[pHb])
            t0 = c * CW
            for dd in range(NDT):
                par = dd % 2
                B.dma("sync", slx[par], xdt[par][:], d["xres"][_sl(dd, 128), _sl(c, CW)],
                      r=self.dram_bufs("xres", t0, t0 + CW), w=[xdtb[par]])
                B.v(lambda v, dd=dd: v.tensor_tensor(outdt[:], ybuf[:, _sl(dd, CW)], pG[:], ALU.add),
                    r=[ybufb, pGb], w=[outdtb])
                B.v(lambda v: v.tensor_tensor(outdt[:], outdt[:], pH[:], ALU.mult),
                    r=[pHb], w=[outdtb])
                B.v(lambda v, dd=dd: v.tensor_scalar(outdt[:], outdt[:], pv("l1g")[:, dd:dd + 1],
                                                     pv("l1b2")[:, dd:dd + 1], ALU.mult, ALU.add),
                    r=[bf("pvs")], w=[outdtb])
                B.v(lambda v, par=par: v.tensor_tensor(outdt[:], outdt[:], xdt[par][:], ALU.add),
                    r=[xdtb[par]], w=[outdtb])
                B.dma("sync", slo, d["xres"][_sl(dd, 128), _sl(c, CW)], outdt[:],
                      r=[outdtb], w=self.dram_bufs("xres", t0, t0 + CW))
        self.end_phase()

    def phase_cmix(self, call, final=False):
        B = self.B
        d = self.d
        bf = B.buf
        pv, dv = self.pvc, self.dvc
        self.begin_phase()
        sb, ps = self.sb, self.ps
        CW = 1024
        NCH = T // CW
        pre = "C%d_" % call

        def gb(n):
            return bf(pre + n)

        xdt = [sb(pre + "xdt0", (128, CW)), sb(pre + "xdt1", (128, CW))]
        xdtb = [gb("xdt0"), gb("xdt1")]
        sq = sb(pre + "sq", (128, CW)); sqb = gb("sq")
        rows = sb(pre + "rows", (1, 4 * CW))
        uc = sb(pre + "u", (128, NDT * CW)); ucb = gb("u")
        xr = sb(pre + "xr", (128, NDT * CW), BF16); xrb = gb("xr")
        xk = sb(pre + "xk", (128, NDT * CW), BF16); xkb = gb("xk")
        rbuf = sb(pre + "rbuf", (128, NDT * CW)); rbufb = gb("rbuf")
        kact = sb(pre + "kact", (128, NDT * CW), BF16); kactb = gb("kact")
        ubnd = sb(pre + "ubnd", (128, NDT)); ubndb = gb("ubnd")
        tmp = sb(pre + "tmp", (128, CW)); tmpb = gb("tmp")
        outdt = sb(pre + "outdt", (128, CW)); outdtb = gb("outdt")
        wr_s = sb(pre + "wr", (128, NDT * D), BF16)
        wk_s = sb(pre + "wk", (128, NDT * D), BF16)
        wv_s = sb(pre + "wv", (128, NDT * D), BF16)

        pG = ps(pre + "pG", (128, CW)); pGb = self.pb(pre + "pG")
        pH = ps(pre + "pH", (128, CW)); pHb = self.pb(pre + "pH")
        pM = [ps(pre + "pM0", (128, 512)), ps(pre + "pM1", (128, 512))]
        pMb = [self.pb(pre + "pM0"), self.pb(pre + "pM1")]

        slc = B.dslot()
        for nm, dst in [("fwrT", wr_s), ("fwkT", wk_s), ("fwvT", wv_s)]:
            B.dma("gpsimd", slc, dst.ap().rearrange("p (a c) -> p a c", a=NDT),
                  d[nm].ap().rearrange("(a p) c -> p a c", p=128), w=[gb(nm)])
        slx = [B.dslot(), B.dslot()]
        slo = B.dslot()
        hb = dict(xdt=xdt, xdtb=xdtb, sq=sq, sqb=sqb, rows=rows,
                  pp0=pG, pp0b=pGb, pp1=pH, pp1b=pHb, slx=slx)

        B.v(lambda v: v.memset(ubnd[:], 0.0), w=[ubndb])
        for c in range(NCH):
            t0 = c * CW

            def out_fn(dd, par, c=c):
                B.v(lambda v, par=par: v.tensor_tensor(sq[:], xdt[par][:], pG[:], ALU.add),
                    r=[xdtb[par], pGb], w=[sqb])
                B.v(lambda v, dd=dd: v.tensor_tensor(uc[:, _sl(dd, CW)], sq[:], pH[:], ALU.mult),
                    r=[sqb, pHb], w=[ucb])
            self.ln_stream(c, CW, hb, out_fn)
            for dd in range(NDT):
                B.v(lambda v, dd=dd: v.tensor_scalar_mul(tmp[:], uc[:, _sl(dd, CW)],
                                                         pv("mur")[:, dd:dd + 1]),
                    r=[ucb, bf("pvs")], w=[tmpb])
                B.v(lambda v, dd=dd: v.scalar_tensor_tensor(
                    xr[:, dd * CW + 1:(dd + 1) * CW], uc[:, dd * CW:(dd + 1) * CW - 1],
                    dv("omur")[:, dd:dd + 1], tmp[:, 1:CW], ALU.mult, ALU.add),
                    r=[ucb, tmpb, bf("dvs")], w=[xrb])
                B.v(lambda v, dd=dd: v.scalar_tensor_tensor(
                    xr[:, dd * CW:dd * CW + 1], ubnd[:, dd:dd + 1],
                    dv("omur")[:, dd:dd + 1], tmp[:, 0:1], ALU.mult, ALU.add),
                    r=[ubndb, tmpb, bf("dvs")], w=[xrb])
                B.v(lambda v, dd=dd: v.tensor_scalar_mul(tmp[:], uc[:, _sl(dd, CW)],
                                                         pv("muk")[:, dd:dd + 1]),
                    r=[ucb, bf("pvs")], w=[tmpb])
                B.v(lambda v, dd=dd: v.scalar_tensor_tensor(
                    xk[:, dd * CW + 1:(dd + 1) * CW], uc[:, dd * CW:(dd + 1) * CW - 1],
                    dv("omuk")[:, dd:dd + 1], tmp[:, 1:CW], ALU.mult, ALU.add),
                    r=[ucb, tmpb, bf("dvs")], w=[xkb])
                B.v(lambda v, dd=dd: v.scalar_tensor_tensor(
                    xk[:, dd * CW:dd * CW + 1], ubnd[:, dd:dd + 1],
                    dv("omuk")[:, dd:dd + 1], tmp[:, 0:1], ALU.mult, ALU.add),
                    r=[ubndb, tmpb, bf("dvs")], w=[xkb])
                B.v(lambda v, dd=dd: v.tensor_copy(ubnd[:, dd:dd + 1],
                                                   uc[:, (dd + 1) * CW - 1:(dd + 1) * CW]),
                    r=[ucb], w=[ubndb])
            for ct in range(NDT):
                for hf in range(2):
                    pm = pM[hf]
                    pmb = pMb[hf]
                    for dd in range(NDT):
                        B.t(lambda t, ct=ct, hf=hf, dd=dd, pm=pm: t.matmul(
                            pm[:], wr_s[:, dd * D + ct * 128: dd * D + (ct + 1) * 128],
                            xr[:, dd * CW + hf * 512: dd * CW + hf * 512 + 512],
                            start=(dd == 0), stop=(dd == NDT - 1)),
                            r=[gb("fwrT"), xrb], w=[pmb])
                    B.s(lambda s, ct=ct, hf=hf, pm=pm: s.activation(
                        rbuf[:, ct * CW + hf * 512: ct * CW + hf * 512 + 512], pm[:], AF.Sigmoid),
                        r=[pmb], w=[rbufb])
            for ct in range(NDT):
                for hf in range(2):
                    pm = pM[hf]
                    pmb = pMb[hf]
                    for dd in range(NDT):
                        B.t(lambda t, ct=ct, hf=hf, dd=dd, pm=pm: t.matmul(
                            pm[:], wk_s[:, dd * D + ct * 128: dd * D + (ct + 1) * 128],
                            xk[:, dd * CW + hf * 512: dd * CW + hf * 512 + 512],
                            start=(dd == 0), stop=(dd == NDT - 1)),
                            r=[gb("fwkT"), xkb], w=[pmb])
                    B.v(lambda v, pm=pm: v.tensor_scalar_max(tmp[:, 0:512], pm[:], 0.0),
                        r=[pmb], w=[tmpb])
                    B.s(lambda s, ct=ct, hf=hf: s.square(
                        kact[:, ct * CW + hf * 512: ct * CW + hf * 512 + 512], tmp[:, 0:512]),
                        r=[tmpb], w=[kactb])
            for ct in range(NDT):
                par = ct % 2
                B.dma("sync", slx[par], xdt[par][:], d["xres"][_sl(ct, 128), _sl(c, CW)],
                      r=self.dram_bufs("xres", t0, t0 + CW), w=[xdtb[par]])
                for hf in range(2):
                    pm = pM[hf]
                    pmb = pMb[hf]
                    for dd in range(NDT):
                        B.t(lambda t, ct=ct, hf=hf, dd=dd, pm=pm: t.matmul(
                            pm[:], wv_s[:, dd * D + ct * 128: dd * D + (ct + 1) * 128],
                            kact[:, dd * CW + hf * 512: dd * CW + hf * 512 + 512],
                            start=(dd == 0), stop=(dd == NDT - 1)),
                            r=[gb("fwvT"), kactb], w=[pmb])
                    B.v(lambda v, ct=ct, hf=hf, pm=pm: v.tensor_tensor(
                        outdt[:, _sl(hf, 512)], rbuf[:, ct * CW + hf * 512: ct * CW + hf * 512 + 512],
                        pm[:], ALU.mult), r=[rbufb, pmb], w=[outdtb])
                    B.v(lambda v, par=par, hf=hf: v.tensor_tensor(
                        outdt[:, _sl(hf, 512)], outdt[:, _sl(hf, 512)], xdt[par][:, _sl(hf, 512)],
                        ALU.add), r=[xdtb[par]], w=[outdtb])
                tgt = d["outT"] if final else d["xres"]
                wb_ = [bf("outT")] if final else self.dram_bufs("xres", t0, t0 + CW)
                B.dma("sync", slo, tgt[_sl(ct, 128), _sl(c, CW)], outdt[:],
                      r=[outdtb], w=wb_)
        self.end_phase()


    def phase_seed(self):
        B = self.B
        self.begin_phase()
        sl = B.dslot()
        B.dma("sync", sl, self.d["xres"][:, :], self.d["xT"][:, :],
              w=self.dram_bufs("xres", 0, T))
        self.end_phase()

    def phase_attn1(self):
        B = self.B
        d = self.d
        bf = B.buf
        pv, dv = self.pvc, self.dvc
        self.begin_phase()
        sb, ps = self.sb, self.ps
        CW = 1024
        NCH = T // CW
        SC = 1.0 / 8.0

        def gb(n):
            return bf("A1_" + n)

        xdt = [sb("A1_xdt0", (128, CW)), sb("A1_xdt1", (128, CW))]
        xdtb = [gb("xdt0"), gb("xdt1")]
        sq = sb("A1_sq", (128, CW)); sqb = gb("sq")
        rows = sb("A1_rows", (1, 2 * CW))
        xin = sb("A1_xin", (128, NDT * T), BF16); xinb = gb("xin")
        kth = sb("A1_kth", (65, T), BF16); kthb = gb("kth")
        qth = sb("A1_qth", (65, T), BF16); qthb = gb("qth")
        vaug = sb("A1_vaug", (128, 32 * 65), BF16); vaugb = gb("vaug")
        pexp = [sb("A1_pe%d" % i, (128, 512), BF16) for i in range(6)]
        pexpb = [gb("pe%d" % i) for i in range(6)]
        ounf = sb("A1_ounf", (64, 512)); ounfb = gb("ounf")
        zi = sb("A1_zi", (1, 544)); zib = gb("zi")
        oall = sb("A1_oall", (128, 4 * T), BF16); oallb = gb("oall")
        ybuf = sb("A1_ybuf", (128, NDT * 512)); ybufb = gb("ybuf")
        outdt = sb("A1_outdt", (128, 512)); outdtb = gb("outdt")
        qkbias = sb("A1_qkb", (128, 1)); qkbiasb = gb("qkb")
        bqk_s = sb("A1_bqk", (64, 16)); bqk_sb = gb("bqk")
        bvr_s = sb("A1_bvr", (1, 512)); bvr_sb = gb("bvr")
        wqk_s = sb("A1_wqk", (128, NDT * D), BF16)
        wv1_s = sb("A1_wv1", (128, NDT * 512), BF16)
        wout_s = sb("A1_wout", (128, 4 * D), BF16)

        pG = ps("A1_pG", (128, CW)); pGb = self.pb("A1_pG")
        pH = ps("A1_pH", (128, CW)); pHb = self.pb("A1_pH")
        pS = [ps("A1_pS0", (128, 512)), ps("A1_pS1", (128, 512))]
        pSb = [self.pb("A1_pS0"), self.pb("A1_pS1")]
        pO = ps("A1_pO", (65, 512)); pOb = self.pb("A1_pO")
        pZ = ps("A1_pZ", (64, 512)); pZb = self.pb("A1_pZ")

        slc = B.dslot()
        B.dma("gpsimd", slc, wqk_s.ap().rearrange("p (a o) -> p a o", a=NDT),
              d["wqk"].ap().rearrange("(a p) o -> p a o", p=128), w=[gb("wqk")])
        B.dma("gpsimd", slc, wv1_s.ap().rearrange("p (a o) -> p a o", a=NDT),
              d["wv1"].ap().rearrange("(a p) o -> p a o", p=128), w=[gb("wv1")])
        B.dma("gpsimd", slc, wout_s.ap().rearrange("p (a o) -> p a o", a=4),
              d["wout"].ap().rearrange("(a p) o -> p a o", p=128), w=[gb("wout")])
        slc2 = B.dslot()
        B.dma("sync", slc2, bqk_s[:], d["bqk"][:], w=[bqk_sb])
        B.dma("sync", slc2, bvr_s[:], d["bvr"][:], w=[bvr_sb])
        slx = [B.dslot(), B.dslot()]
        slo = B.dslot()
        hb = dict(xdt=xdt, xdtb=xdtb, sq=sq, sqb=sqb, rows=rows,
                  pp0=pG, pp0b=pGb, pp1=pH, pp1b=pHb, slx=slx)

        for c in range(NCH):
            def out_fn(dd, par, c=c):
                B.v(lambda v, par=par: v.tensor_tensor(sq[:], xdt[par][:], pG[:], ALU.add),
                    r=[xdtb[par], pGb], w=[sqb])
                B.v(lambda v, dd=dd, c=c: v.tensor_tensor(
                    xin[:, dd * T + c * CW: dd * T + (c + 1) * CW], sq[:], pH[:], ALU.mult),
                    r=[sqb, pHb], w=[xinb])
            self.ln_stream(c, CW, hb, out_fn)
        for h in range(8):
            p, hloc = h // 2, h % 2
            B.t(lambda t, h=h: t.matmul(pS[0][:, 0:64], self.c_orf[:],
                                        bvr_s[0:1, h * 64:(h + 1) * 64], start=True, stop=True),
                r=[bvr_sb, bf("c_orf")], w=[pSb[0]])
            B.v(lambda v: v.tensor_copy(sq[:, 0:64], pS[0][:, 0:64]), r=[pSb[0]], w=[sqb])
            B.v(lambda v: v.memset(vaug[:], 1.0), w=[vaugb])
            for tt in range(32):
                for dd in range(NDT):
                    B.t(lambda t, h=h, tt=tt, dd=dd: t.matmul(
                        pS[tt % 2][:, 0:64], xin[:, dd * T + tt * 128: dd * T + (tt + 1) * 128],
                        wv1_s[:, dd * 512 + h * 64: dd * 512 + (h + 1) * 64],
                        start=(dd == 0), stop=(dd == NDT - 1)),
                        r=[xinb, gb("wv1")], w=[pSb[tt % 2]])
                B.v(lambda v, tt=tt: v.tensor_tensor(
                    vaug[:, tt * 65: tt * 65 + 64], sq[:, 0:64],
                    pS[tt % 2][:, 0:64], ALU.add),
                    r=[sqb, pSb[tt % 2]], w=[vaugb])
            if True:
                B.v(lambda v, h=h: v.tensor_copy(qkbias[0:64, :], bqk_s[:, 2 * h:2 * h + 1]),
                    r=[bqk_sb], w=[qkbiasb])
                B.v(lambda v, h=h: v.tensor_copy(qkbias[64:128, :], bqk_s[:, 2 * h + 1:2 * h + 2]),
                    r=[bqk_sb], w=[qkbiasb])
                for qc in range(8):
                    for dd in range(NDT):
                        B.t(lambda t, h=h, qc=qc, dd=dd: t.matmul(
                            pS[qc % 2][:], wqk_s[:, dd * D + h * 128: dd * D + (h + 1) * 128],
                            xin[:, dd * T + qc * 512: dd * T + (qc + 1) * 512],
                            start=(dd == 0), stop=(dd == NDT - 1)),
                            r=[gb("wqk"), xinb], w=[pSb[qc % 2]])
                    B.v(lambda v, qc=qc: v.tensor_scalar_add(qth[0:64, _sl(qc, 512)],
                                                             pS[qc % 2][0:64, :], qkbias[0:64, :]),
                        r=[pSb[qc % 2], qkbiasb], w=[qthb])
                    B.v(lambda v, qc=qc: v.tensor_scalar_add(kth[0:64, _sl(qc, 512)],
                                                             pS[qc % 2][64:128, :], qkbias[64:128, :]),
                        r=[pSb[qc % 2], qkbiasb], w=[kthb])
                B.v(lambda v: v.memset(kth[64:65, :], 1.0), w=[kthb])
                for qc in range(8):
                    B.s(lambda s, qc=qc: s.square(sq[0:64, 0:512], kth[0:64, _sl(qc, 512)]),
                        r=[kthb], w=[sqb])
                    B.t(lambda t, qc=qc: t.matmul(pZ[0:1, :], self.ones_col_f()[0:64, :],
                                                  sq[0:64, 0:512], start=True, stop=True),
                        r=[sqb, bf("c_f")], w=[pZb])
                    B.v(lambda v, qc=qc: v.reduce_max(zi[0:1, qc:qc + 1], pZ[0:1, :], axis=AX.X),
                        r=[pZb], w=[zib])
                B.v(lambda v: v.reduce_max(zi[0:1, 8:9], zi[0:1, 0:8], axis=AX.X),
                    r=[], w=[zib])
                for qc in range(8):
                    B.s(lambda s, qc=qc: s.square(sq[0:64, 0:512], qth[0:64, _sl(qc, 512)]),
                        r=[qthb], w=[sqb])
                    B.t(lambda t, qc=qc: t.matmul(pZ[0:1, :], self.ones_col_f()[0:64, :],
                                                  sq[0:64, 0:512], start=True, stop=True),
                        r=[sqb, bf("c_f")], w=[pZb])
                    B.v(lambda v, qc=qc: v.tensor_single_scalar(zi[0:1, 16:528], pZ[0:1, :],
                                                                zi[0:1, 8:9], ALU.mult),
                        r=[pZb], w=[zib])
                    B.s(lambda s: s.activation(zi[0:1, 16:528], zi[0:1, 16:528], AF.Sqrt),
                        r=[], w=[zib])
                    B.v(lambda v, qc=qc: v.tensor_scalar_mul(qth[64:65, _sl(qc, 512)],
                                                             zi[0:1, 16:528], -1.0),
                        r=[zib], w=[qthb])
                if p == 0 and hloc == 0 and getattr(self, "_dbg", None) is not None:
                    self.dump("qth", qth[:], (65, T), [qthb])
                    self.dump("kth", kth[:], (65, T), [kthb])
                pSl = [pG[:, 0:512], pG[:, 512:1024], pH[:, 0:512], pH[:, 512:1024]]
                pSlb = [pGb[0:1], pGb[1:2], pHb[0:1], pHb[1:2]]
                DEP = 6

                def emit_S(kt, qc):
                    r4 = kt % 4
                    B.t(lambda t, qc=qc, kt=kt, r4=r4: t.matmul(
                        pSl[r4], kth[:, _sl(kt, 128)], qth[:, _sl(qc, 512)],
                        start=True, stop=True), r=[kthb, qthb], w=[pSlb[r4]])

                def emit_exp(kt, qc):
                    r4 = kt % 4
                    B.s(lambda s, kt=kt, r4=r4: s.activation(
                        pexp[kt % DEP][:], pSl[r4], AF.Exp, scale=SC),
                        r=[pSlb[r4]], w=[pexpb[kt % DEP]])

                def emit_AV(kt, hloc):
                    B.t(lambda t, kt=kt: t.matmul(
                        pO[:], vaug[:, kt * 65: kt * 65 + 65], pexp[kt % DEP][:],
                        start=(kt == 0), stop=(kt == 31)),
                        r=[vaugb, pexpb[kt % DEP]], w=[pOb])

                for qc in range(8):
                    for kt in range(32):
                        emit_S(kt, qc)
                        if kt >= DEP:
                            emit_AV(kt - DEP, hloc)
                        emit_exp(kt, qc)
                    for kt in range(32 - DEP, 32):
                        emit_AV(kt, hloc)
                    B.v(lambda v: v.reciprocal(zi[0:1, 16:528], pO[64:65, :]), r=[pOb], w=[zib])
                    B.t(lambda t: t.matmul(pZ[:], self.c_orf[0:1, 0:64], zi[0:1, 16:528],
                                           start=True, stop=True), r=[zib, bf("c_orf")], w=[pZb])
                    B.s(lambda s: s.copy(ounf[:], pO[0:64, :]), r=[pOb], w=[ounfb])
                    B.v(lambda v, h=h, qc=qc, hloc=hloc: v.tensor_tensor(
                        oall[hloc * 64: hloc * 64 + 64,
                             (h // 2) * T + qc * 512: (h // 2) * T + (qc + 1) * 512],
                        ounf[:], pZ[:], ALU.mult), r=[ounfb, pZb], w=[oallb])
        if getattr(self, "_dbg", None) is not None:
            self.dump("oall", oall[:], (128, 4 * T), [oallb])
            self.dump("wouts", wout_s[:], (128, 4 * D), [gb("wout")])
        for c2 in range(T // 512):
            t0 = c2 * 512
            for ot in range(NDT):
                for vd in range(4):
                    B.t(lambda t, t0=t0, ot=ot, vd=vd: t.matmul(
                        pS[ot % 2][:], wout_s[:, vd * D + ot * 128: vd * D + (ot + 1) * 128],
                        oall[:, vd * T + t0: vd * T + t0 + 512],
                        start=(vd == 0), stop=(vd == 3)),
                        r=[gb("wout"), oallb], w=[pSb[ot % 2]])
                B.v(lambda v, ot=ot: v.tensor_scalar_add(
                    ybuf[:, _sl(ot, 512)], pS[ot % 2][:], pv("bout")[:, ot:ot + 1]),
                    r=[pSb[ot % 2], bf("pvs")], w=[ybufb])
            if c2 == 0 and getattr(self, "_dbg", None) is not None:
                self.dump("ypre", ybuf[:], (128, NDT * 512), [ybufb])
            for dd in range(NDT):
                B.s(lambda s, dd=dd: s.square(sq[:, 0:512], ybuf[:, _sl(dd, 512)]),
                    r=[ybufb], w=[sqb])
                B.t(lambda t, dd=dd: t.matmul(
                    pG[0:1, 0:512], self.ones_col_f(), ybuf[:, _sl(dd, 512)],
                    start=(dd == 0), stop=(dd == NDT - 1)), r=[ybufb, bf("c_f")], w=[pGb])
                B.t(lambda t, dd=dd: t.matmul(
                    pH[0:1, 0:512], self.ones_col_f(), sq[:, 0:512],
                    start=(dd == 0), stop=(dd == NDT - 1)), r=[sqb, bf("c_f")], w=[pHb])
            r0, r1 = bf("r_s0"), bf("r_s1")
            B.v(lambda v: v.tensor_scalar_mul(rows[0:1, 0:512], pG[0:1, 0:512], 1.0 / D),
                r=[pGb], w=[r0])
            B.v(lambda v: v.tensor_scalar_mul(rows[0:1, 512:1024], pH[0:1, 0:512], 1.0 / D),
                r=[pHb], w=[r1])
            B.v(lambda v: v.tensor_mul(rows[0:1, 0:512], rows[0:1, 0:512], rows[0:1, 0:512]),
                r=[], w=[r0])
            B.v(lambda v: v.tensor_sub(rows[0:1, 512:1024], rows[0:1, 512:1024], rows[0:1, 0:512]),
                r=[r0], w=[r1])
            B.v(lambda v: v.tensor_scalar_mul(rows[0:1, 0:512], pG[0:1, 0:512], -1.0 / D),
                r=[pGb], w=[r0])
            B.s(lambda s: s.activation(rows[0:1, 512:1024], rows[0:1, 512:1024],
                                       AF.Ln, bias=self.c_f[0:1, 129:130]), r=[bf("c_f")], w=[r1])
            B.s(lambda s: s.activation(rows[0:1, 512:1024], rows[0:1, 512:1024],
                                       AF.Exp, scale=-0.5), r=[], w=[r1])
            B.t(lambda t: t.matmul(pG[:, 0:512], self.c_orf[:], rows[0:1, 0:512],
                                   start=True, stop=True), r=[r0, bf("c_orf")], w=[pGb])
            B.t(lambda t: t.matmul(pH[:, 0:512], self.c_orf[:], rows[0:1, 512:1024],
                                   start=True, stop=True), r=[r1, bf("c_orf")], w=[pHb])
            for dd in range(NDT):
                B.v(lambda v, dd=dd: v.tensor_tensor(ybuf[:, _sl(dd, 512)], ybuf[:, _sl(dd, 512)],
                                                     pG[:, 0:512], ALU.add), r=[pGb], w=[ybufb])
                B.v(lambda v, dd=dd: v.tensor_tensor(ybuf[:, _sl(dd, 512)], ybuf[:, _sl(dd, 512)],
                                                     pH[:, 0:512], ALU.mult), r=[pHb], w=[ybufb])
                B.v(lambda v, dd=dd: v.tensor_scalar(ybuf[:, _sl(dd, 512)], ybuf[:, _sl(dd, 512)],
                                                     pv("lng")[:, dd:dd + 1], pv("lnb")[:, dd:dd + 1],
                                                     ALU.mult, ALU.add), r=[bf("pvs")], w=[ybufb])

            if c2 == 0 and getattr(self, "_dbg", None) is not None:
                self.dump("ypost", ybuf[:], (128, NDT * 512), [ybufb])

            def out_fn2(dd, par, c2=c2):
                t0_ = c2 * 512
                B.v(lambda v, par=par: v.tensor_tensor(sq[:, 0:512], xdt[par][:, 0:512],
                                                       pG[:, 0:512], ALU.add),
                    r=[xdtb[par], pGb], w=[sqb])
                B.v(lambda v: v.tensor_tensor(sq[:, 0:512], sq[:, 0:512], pH[:, 0:512], ALU.mult),
                    r=[pHb], w=[sqb])
                B.v(lambda v, dd=dd: v.tensor_tensor(outdt[:], ybuf[:, _sl(dd, 512)],
                                                     sq[:, 0:512], ALU.add),
                    r=[ybufb, sqb], w=[outdtb])
                B.v(lambda v, par=par: v.scalar_tensor_tensor(outdt[:], outdt[:], RS2,
                                                              xdt[par][:, 0:512], ALU.mult, ALU.add),
                    r=[xdtb[par]], w=[outdtb])
                B.dma("sync", slo, d["xres"][_sl(dd, 128), t0_:t0_ + 512], outdt[:],
                      r=[outdtb], w=self.dram_bufs("xres", t0_, t0_ + 512))
            self.ln_stream(c2, 512, hb, out_fn2)
        self.end_phase()

    def dbg_init(self):
        self._dbg = {}
        self._dbg_slot = self.B.dslot()

    def dump(self, name, ap, shape, rbufs):
        t = self.nc.dram_tensor("dbg_" + name, list(shape), F32, kind="ExternalOutput")
        self.B.dma("gpsimd", self._dbg_slot, t[:], ap, r=rbufs, w=[self.B.buf("dbg_" + name)])

    def finish(self):
        self.B.build()
        self.es.close()



def _pvcols(v):
    return np.ascontiguousarray(v.reshape(8, 128).T.astype(np.float32))


def host_prep(inputs):
    f = np.float32
    x = np.asarray(inputs["L0_attn_a_x"], f)
    nb = ml_dtypes.bfloat16

    cb = np.zeros((128, 128 + 132 + 8 * 128), nb)
    cb[:, 0:128] = np.eye(128, dtype=nb)
    tri = np.zeros((128, 129), np.float32)
    tri[:, 128] = 1.0
    for i in range(128):
        tri[i, i + 1:128] = 1.0
    cb[:, 128:257] = tri.astype(nb)
    for e in range(8):
        cb[e, 260 + e * 128: 260 + (e + 1) * 128] = nb(1.0)
    cf = np.zeros((128, 260), f)
    causal = np.zeros((128, 128), f)
    for i in range(128):
        causal[i, i + 1:] = NEG_BIG
    cf[:, 0:128] = causal
    cf[:, 128] = 1.0
    cf[:, 129] = EPS
    cf[:, 132:260] = np.eye(128, dtype=f)
    onesrow = np.ones((1, 128), nb)
    onesrow_f = np.ones((1, 128), f)

    pv_order = ["ts", "tf", "ln1w", "ln1b", "ln2w", "ln2b",
                "l1d", "l1g", "l1b2", "mur", "muk", "bout", "lng", "lnb"]
    pv_src = {
        "ts": np.asarray(inputs["L0_attn_a_time_shift"], f).reshape(-1),
        "tf": np.asarray(inputs["L0_attn_a_time_first"], f).reshape(-1),
        "ln1w": np.asarray(inputs["L0_attn_a_ln1_w"], f),
        "ln1b": np.asarray(inputs["L0_attn_a_ln1_b"], f),
        "ln2w": np.asarray(inputs["L0_attn_a_ln2_w"], f),
        "ln2b": np.asarray(inputs["L0_attn_a_ln2_b"], f),
        "l1d": np.asarray(inputs["L1_attn_a_D"], f),
        "l1g": np.asarray(inputs["L1_attn_a_gamma"], f),
        "l1b2": np.asarray(inputs["L1_attn_a_beta"], f),
        "mur": np.asarray(inputs["L1_ffn_mu_r"], f),
        "muk": np.asarray(inputs["L1_ffn_mu_k"], f),
        "bout": np.asarray(inputs["L1_attn_b_b_out"], f),
        "lng": np.asarray(inputs["L1_attn_b_ln_gamma"], f),
        "lnb": np.asarray(inputs["L1_attn_b_ln_beta"], f),
    }
    pv = np.concatenate([_pvcols(pv_src[k]) for k in pv_order], axis=1)
    pv64 = np.stack([np.asarray(inputs["L1_attn_a_log_neg_real"], f),
                     np.asarray(inputs["L1_attn_a_imag"], f)], axis=1)

    bqk_full = np.asarray(inputs["L1_attn_b_b_qk"], f)
    bqk = np.zeros((64, 16), f)
    for h in range(8):
        idx = (h * 64 + np.arange(64))
        bqk[:, 2 * h] = bqk_full[idx * 2]
        bqk[:, 2 * h + 1] = bqk_full[idx * 2 + 1]
    bv_full = np.asarray(inputs["L1_attn_b_b_v"], f)
    bv = np.stack([bv_full[h * 64:(h + 1) * 64] for h in range(8)], axis=1)

    wqk_full = np.asarray(inputs["L1_attn_b_W_qk"], f)
    wqk = np.zeros((D, D), f)
    for h in range(8):
        idx = (h * 64 + np.arange(64))
        wqk[:, h * 128:h * 128 + 64] = wqk_full[:, idx * 2]
        wqk[:, h * 128 + 64:h * 128 + 128] = wqk_full[:, idx * 2 + 1]

    common = {
        "cb": cb, "cf": cf, "onesrow": onesrow, "onesrow_f": onesrow_f,
        "pv": pv, "pv64": pv64,
        "rbias": np.asarray(inputs["L0_ffn_router_b"], f).reshape(1, 8),
        "wkT": np.ascontiguousarray(np.asarray(inputs["L0_attn_a_Wk"], f).T),
        "wvT": np.ascontiguousarray(np.asarray(inputs["L0_attn_a_Wv"], f).T),
        "wrT": np.ascontiguousarray(np.asarray(inputs["L0_attn_a_Wr"], f).T),
        "wckT": np.ascontiguousarray(np.asarray(inputs["L0_attn_a_Wck"], f).T),
        "wcvT": np.ascontiguousarray(np.asarray(inputs["L0_attn_a_Wcv"], f).T),
        "wcrT": np.ascontiguousarray(np.asarray(inputs["L0_attn_a_Wcr"], f).T),
        "awq": np.asarray(inputs["L0_attn_b_Wq"], f),
        "awk": np.asarray(inputs["L0_attn_b_Wk"], f),
        "awv": np.asarray(inputs["L0_attn_b_Wv"], f),
        "awo": np.asarray(inputs["L0_attn_b_Wo"], f),
        "rw": np.asarray(inputs["L0_ffn_router_w"], f),
        "w1": np.asarray(inputs["L0_ffn_w1"], f),
        "w2": np.asarray(inputs["L0_ffn_w2"], f),
        "bmat": np.asarray(inputs["L1_attn_a_B_mat"], f),
        "cmatT": np.ascontiguousarray(np.asarray(inputs["L1_attn_a_C_mat"], f).T),
        "wqk": wqk, "bqk": bqk,
        "wv1": np.asarray(inputs["L1_attn_b_W_v"], f), "bv": bv,
        "bvr": np.ascontiguousarray(bv_full.reshape(1, 512)),
        "wout": np.asarray(inputs["L1_attn_b_W_out"], f),
        "fwrT": np.ascontiguousarray(np.asarray(inputs["L1_ffn_W_r"], f).T),
        "fwkT": np.ascontiguousarray(np.asarray(inputs["L1_ffn_W_k"], f).T),
        "fwvT": np.ascontiguousarray(np.asarray(inputs["L1_ffn_W_v"], f).T),
    }
    in_maps = []
    for b in range(8):
        m = dict(common)
        m["xT"] = np.ascontiguousarray(x[b].T)
        in_maps.append(m)
    return in_maps


def build_program(taps=(), phases=("P0", "P1"), debug=False, races=True):
    nc = bass.Bass("TRN2", target_bir_lowering=False, detect_race_conditions=races)
    mk = MK(nc, taps=taps)
    if debug:
        mk.dbg_init()
    mk.prologue()
    if "SEED" in phases:
        mk.phase_seed()
    if "P1" in phases:
        mk.phase_l0aa_A()
    if "P2" in phases:
        mk.phase_l0aa_B()
    if "P3" in phases:
        mk.phase_moe(0)
    if "P4" in phases:
        mk.phase_attn0()
    if "P5" in phases:
        mk.phase_moe(1)
    if "P6" in phases:
        mk.phase_ssm()
    if "P7" in phases:
        mk.phase_cmix(0)
    if "P8" in phases:
        mk.phase_attn1()
    if "P9" in phases:
        mk.phase_cmix(1, final=True)
    mk.finish()
    return nc


ALL_PHASES = ("P0", "P1", "P2", "P3", "P4", "P5", "P6", "P7", "P8", "P9")


def kernel(**inputs):
    in_maps = host_prep(inputs)
    nc = build_program(taps=(), phases=ALL_PHASES)
    res = run_bass_kernel_spmd(nc, in_maps, core_ids=list(range(8)))
    out = np.stack([np.ascontiguousarray(r["outT"].T) for r in res.results], axis=0)
    return out


# version 07 (final)
# speedup vs baseline: 1.2003x; 1.2003x over previous
import numpy as np
import ml_dtypes
from contextlib import ExitStack

import concourse.bass as bass
import concourse.mybir as mybir
from concourse.bass_utils import run_bass_kernel_spmd

F32 = mybir.dt.float32
BF16 = mybir.dt.bfloat16
AF = mybir.ActivationFunctionType
ALU = mybir.AluOpType
AX = mybir.AxisListType

D = 1024
T = 4096
NDT = 8
EPS = 1e-5
NEG_BIG = -30000.0
RS2 = 2.0 ** (-0.5)
TWO_PI = 6.283185307179586
MAGIC = 12582912.0

ENGINES = ("tensor", "vector", "scalar", "gpsimd", "sync")


def _sl(i, w):
    return slice(i * w, (i + 1) * w)


class Buf:
    __slots__ = ("name", "wt", "readers")

    def __init__(self, name):
        self.name = name
        self.wt = None
        self.readers = []


class DSlot:

    def __init__(self, idx):
        self.idx = idx
        self.n = 0


class Op:
    __slots__ = ("engine", "fn", "deps", "eidx", "dslot", "marked")

    def __init__(self, engine, fn, deps, eidx, dslot=None):
        self.engine = engine
        self.fn = fn
        self.deps = deps
        self.eidx = eidx
        self.dslot = dslot
        self.marked = False


class Bld:
    def __init__(self, nc):
        self.nc = nc
        self.ops = []
        self.ecount = {e: 0 for e in ENGINES}
        self.dslots = []
        self.bufs = {}
        self.esem = {}
        self.dsem = []
        self.segs = []
        self._slot_touched = set()

    def buf(self, name):
        if name not in self.bufs:
            self.bufs[name] = Buf(name)
        return self.bufs[name]

    def dslot(self):
        s = DSlot(len(self.dslots))
        self.dslots.append(s)
        return s

    def _dep_tok(self, t):
        return t

    @staticmethod
    def _flat(bs):
        out = []
        for b in bs:
            if isinstance(b, (list, tuple)):
                out.extend(b)
            else:
                out.append(b)
        return out

    def _mk_op(self, engine, fn, r, w, extra, dslot=None):
        r = self._flat(r)
        w = self._flat(w)
        deps = []
        for b in r:
            if b.wt is not None:
                deps.append(b.wt)
        for b in w:
            if b.wt is not None:
                deps.append(b.wt)
            deps.extend(b.readers)
        deps.extend(extra)
        gi = len(self.ops)
        eidx = self.ecount[engine]
        self.ecount[engine] += 1
        if dslot is not None:
            dslot.n += 16
            tok = ("dma", dslot.idx, dslot.n, gi)
        else:
            tok = ("eng", engine, gi)
        kept = []
        for d in deps:
            if d is None:
                continue
            if d[0] == "eng" and d[1] == engine == "tensor":
                continue
            kept.append(d)
        op = Op(engine, fn, kept, eidx, dslot)
        self.ops.append(op)
        for b in r:
            b.readers.append(tok)
        for b in w:
            b.wt = tok
            b.readers = []
        return tok

    def op(self, engine, fn, r=(), w=(), extra=()):
        return self._mk_op(engine, fn, list(r), list(w), list(extra))

    def t(self, fn, r=(), w=(), extra=()):
        return self.op("tensor", fn, r, w, extra)

    def v(self, fn, r=(), w=(), extra=()):
        return self.op("vector", fn, r, w, extra)

    def s(self, fn, r=(), w=(), extra=()):
        return self.op("scalar", fn, r, w, extra)

    def g(self, fn, r=(), w=(), extra=()):
        return self.op("gpsimd", fn, r, w, extra)

    def dma(self, queue, slot, out_ap, in_ap, r=(), w=(), extra=(), **kw):
        extra = list(extra)
        if slot.n > 0:
            extra.append(("dma", slot.idx, slot.n, -1))

        def fn(eng):
            return eng.dma_start(out=out_ap, in_=in_ap, **kw)

        self._slot_touched.add(slot.idx)
        return self._mk_op(queue, fn, list(r), list(w), extra, dslot=slot)

    def fence_dma(self, slot):
        return self._mk_op("vector", lambda v: v.engine_nop(), [], [],
                           [("dma", slot.idx, slot.n, -1)])

    def seal_phase(self):
        for si in sorted(self._slot_touched):
            s = self.dslots[si]
            self._mk_op("vector", lambda v: v.engine_nop(), [], [],
                        [("dma", s.idx, s.n, -1)])
        self._slot_touched = set()
        self.segs.append(len(self.ops))
        for b in self.bufs.values():
            b.wt = None
            b.readers = []

    def build(self):
        nc = self.nc
        need = set()
        for op in self.ops:
            for d in op.deps:
                need.add((d[0], d[1], d[3] if d[0] == "dma" else d[2]))
        for op in self.ops:
            if op.dslot is not None:
                op.marked = True
            else:
                pass
        for kind, a, gi in list(need):
            if kind == "eng" and gi >= 0:
                self.ops[gi].marked = True
        self.cum = {}
        cnt = {e: 0 for e in ENGINES}
        for gi, op in enumerate(self.ops):
            if op.dslot is None and op.marked:
                cnt[op.engine] += 1
            self.cum[gi] = cnt[op.engine]
        for e in ENGINES:
            self.esem[e] = nc.alloc_semaphore("es_" + e)
        for s in self.dslots:
            self.dsem.append(nc.alloc_semaphore("ds_%d" % s.idx))

        cum = self.cum
        esem = self.esem
        dsem = self.dsem
        if not self.segs or self.segs[-1] != len(self.ops):
            self.segs.append(len(self.ops))
        wm = {e: {} for e in ENGINES}

        def run_engine(eng_handle, ename, seg_ops):
            w = wm[ename]
            for op in seg_ops:
                if op.engine != ename:
                    continue
                for d in op.deps:
                    if d[0] == "eng":
                        tgt = cum[d[2]]
                        key = ("e", d[1])
                        if tgt > 0 and w.get(key, 0) < tgt:
                            eng_handle.wait_ge(esem[d[1]], tgt)
                            w[key] = tgt
                    else:
                        tgt = d[2]
                        key = ("d", d[1])
                        if tgt > 0 and w.get(key, 0) < tgt:
                            eng_handle.wait_ge(dsem[d[1]], tgt)
                            w[key] = tgt
                inst = op.fn(eng_handle)
                if op.dslot is not None:
                    inst.then_inc(dsem[op.dslot.idx], 16)
                elif op.marked:
                    inst.then_inc(esem[ename], 1)

        prev = 0
        for seg_end in self.segs:
            seg = self.ops[prev:seg_end]
            prev = seg_end
            if not seg:
                continue
            with nc.Block() as blk:
                blk.tensor(lambda e, s=seg: run_engine(e, "tensor", s))
                blk.vector(lambda e, s=seg: run_engine(e, "vector", s))
                blk.scalar(lambda e, s=seg: run_engine(e, "scalar", s))
                blk.gpsimd(lambda e, s=seg: run_engine(e, "gpsimd", s))
                blk.sync(lambda e, s=seg: run_engine(e, "sync", s))


class MK:
    def __init__(self, nc, taps=()):
        self.nc = nc
        self.B = Bld(nc)
        self.es = ExitStack()
        self.pes = None
        self.taps = set(taps)
        self._psum_cur = 0
        self.psb = {}
        self._dram()
        self._global_sbuf()

    def sb(self, name, shape, dtype=F32):
        es = self.pes if self.pes is not None else self.es
        return es.enter_context(self.nc.sbuf_tensor(name, list(shape), dtype))

    def ps(self, name, shape, dtype=F32):
        es = self.pes if self.pes is not None else self.es
        h = es.enter_context(self.nc.psum_tensor(name, list(shape), dtype))
        nb = 2 if dtype == BF16 else 4
        per_part = int(np.prod(shape[1:])) * nb
        nbanks = -(-per_part // 2048)
        bufs = [self.B.buf("PSB%d" % (self._psum_cur + i)) for i in range(nbanks)]
        self._psum_cur += nbanks
        assert self._psum_cur <= 8, "PSUM overflow in phase"
        self.psb[name] = bufs
        return h

    def pb(self, name):
        return self.psb[name]

    def begin_phase(self):
        assert self.pes is None
        self.pes = ExitStack()
        self._psum_cur = 0
        self.psb = {}

    def end_phase(self):
        self.B.seal_phase()
        self.pes.close()
        self.pes = None

    def _dram(self):
        nc = self.nc
        d = {}

        def dd(name, shape, dtype=F32, kind="ExternalInput"):
            d[name] = nc.dram_tensor(name, list(shape), dtype, kind=kind)

        def scr(name, shape):
            dd(name, shape, F32, "ExternalOutput" if name in self.taps else "Internal")

        dd("xT", (D, T))
        dd("outT", (D, T), F32, "ExternalOutput")
        scr("xres", (D, T))
        scr("x2s", (D, T))
        scr("xins", (D, T))
        dd("xxxs", (D, T), BF16, "Internal")
        dd("cb", (128, 128 + 132 + 8 * 128), BF16)
        dd("cf", (128, 260), F32)
        dd("onesrow", (1, 128), BF16)
        dd("onesrow_f", (1, 128), F32)
        self.PV = ["ts", "tf", "ln1w", "ln1b", "ln2w", "ln2b",
                   "l1d", "l1g", "l1b2", "mur", "muk", "bout", "lng", "lnb"]
        dd("pv", (128, 8 * len(self.PV)))
        dd("pv64", (64, 2))
        dd("rbias", (1, 8))
        dd("wkT", (D, D)); dd("wvT", (D, D)); dd("wrT", (D, D))
        dd("wckT", (D, 4 * D)); dd("wcvT", (4 * D, D)); dd("wcrT", (D, D))
        dd("wckc", (D, 4 * D), BF16, "Internal")
        dd("awq", (D, 128)); dd("awk", (D, 128)); dd("awv", (D, 128)); dd("awo", (128, D))
        dd("rw", (D, 8))
        dd("w1", (8, D, 8 * D))
        dd("w2", (8, 4 * D, D))
        dd("w1c", (8, D, 8 * D), BF16, "Internal")
        dd("w2c", (8, 4 * D, D), BF16, "Internal")
        dd("bmat", (64, D)); dd("cmatT", (64, D))
        dd("wqk", (D, D)); dd("bqk", (64, 16))
        dd("wv1", (D, 512)); dd("bv", (64, 8)); dd("bvr", (1, 512))
        dd("wout", (512, D))
        dd("fwrT", (D, D)); dd("fwkT", (D, D)); dd("fwvT", (D, D))
        self.d = d
        self.xres_b = [self.B.buf("xres%d" % i) for i in range(T // 256)]
        self.x2s_b = [self.B.buf("x2s%d" % i) for i in range(T // 256)]
        self.xins_b = [self.B.buf("xins%d" % i) for i in range(T // 256)]
        self.xxxs_b = [self.B.buf("xxxs%d" % i) for i in range(T // 256)]

    def dram_bufs(self, which, t0, t1):
        lst = {"xres": self.xres_b, "x2s": self.x2s_b, "xins": self.xins_b,
               "xxxs": self.xxxs_b}[which]
        return lst[t0 // 256:(t1 + 255) // 256]

    def _global_sbuf(self):
        sb = self.sb
        self.c_bf = sb("c_bf", (128, 128 + 132 + 8 * 128), BF16)
        self.c_f = sb("c_f", (128, 260), F32)
        self.c_or = sb("c_or", (1, 128), BF16)
        self.c_orf = sb("c_orf", (1, 128), F32)
        self.pvs = sb("pvs", (128, 8 * len(self.PV)))
        self.pv64s = sb("pv64s", (64, 2))
        self.rbs = sb("rbs", (1, 8))
        self.DV = ["c1ts", "cmts", "eU", "reluU", "omur", "omuk", "dp1", "omts", "nU"]
        self.dvs = sb("dvs", (128, 8 * len(self.DV)))
        self.rb_b = sb("rb_b", (128, 8))
        self.pw_re = sb("pw_re", (64, 12))
        self.pw_im = sb("pw_im", (64, 12))
        self.bbT = sb("bbT", (128, NDT * 128), BF16)
        self.x2last = sb("x2last", (128, NDT))

    def ident(self):
        return self.c_bf[:, 0:128]

    def tri(self):
        return self.c_bf[:, 128:257]

    def selc(self, e):
        return self.c_bf[0:8, 260 + e * 128: 260 + (e + 1) * 128]

    def causal(self):
        return self.c_f[:, 0:128]

    def ones_col_f(self):
        return self.c_f[:, 128:129]

    def ident_f(self):
        return self.c_f[:, 132:260]

    def pvc(self, name):
        i = self.PV.index(name)
        return self.pvs[:, 8 * i: 8 * (i + 1)]

    def dvc(self, name):
        i = self.DV.index(name)
        return self.dvs[:, 8 * i: 8 * (i + 1)]

    def ln_fm(self, xap, src_bufs, ncols, sq, rows, pG, pGb, pH, pHb, sqb, rowsb):
        B = self.B
        bf = B.buf
        CW = ncols
        for dd in range(NDT):
            B.s(lambda s, dd=dd: s.square(sq[:, 0:CW], xap[:, _sl(dd, CW)]),
                r=src_bufs, w=[sqb])
            B.t(lambda t, dd=dd: t.matmul(pG[0:1, 0:CW], self.ones_col_f(),
                                          xap[:, _sl(dd, CW)],
                                          start=(dd == 0), stop=(dd == NDT - 1)),
                r=src_bufs + [bf("c_f")], w=[pGb])
            B.t(lambda t, dd=dd: t.matmul(pH[0:1, 0:CW], self.ones_col_f(),
                                          sq[:, 0:CW],
                                          start=(dd == 0), stop=(dd == NDT - 1)),
                r=[sqb, bf("c_f")], w=[pHb])

        r0, r1 = bf("r_s0"), bf("r_s1")
        B.v(lambda v: v.tensor_scalar_mul(rows[0:1, 0:CW], pG[0:1, 0:CW], 1.0 / D),
            r=[pGb], w=[r0])
        B.v(lambda v: v.tensor_scalar_mul(rows[0:1, CW:2 * CW], pH[0:1, 0:CW], 1.0 / D),
            r=[pHb], w=[r1])
        B.v(lambda v: v.tensor_mul(rows[0:1, 0:CW], rows[0:1, 0:CW], rows[0:1, 0:CW]),
            r=[], w=[r0])
        B.v(lambda v: v.tensor_sub(rows[0:1, CW:2 * CW], rows[0:1, CW:2 * CW], rows[0:1, 0:CW]),
            r=[r0], w=[r1])
        B.v(lambda v: v.tensor_scalar_mul(rows[0:1, 0:CW], pG[0:1, 0:CW], -1.0 / D),
            r=[pGb], w=[r0])
        B.s(lambda s: s.activation(rows[0:1, CW:2 * CW], rows[0:1, CW:2 * CW], AF.Ln,
                                   bias=self.c_f[0:1, 129:130]), r=[bf("c_f")], w=[r1])
        B.s(lambda s: s.activation(rows[0:1, CW:2 * CW], rows[0:1, CW:2 * CW], AF.Exp,
                                   scale=-0.5), r=[], w=[r1])
        B.t(lambda t: t.matmul(pG[:, 0:CW], self.c_orf[:], rows[0:1, 0:CW], start=True, stop=True),
            r=[r0, bf("c_orf")], w=[pGb])
        B.t(lambda t: t.matmul(pH[:, 0:CW], self.c_orf[:], rows[0:1, CW:2 * CW], start=True, stop=True),
            r=[r1, bf("c_orf")], w=[pHb])

    def prologue(self):
        B = self.B
        d = self.d
        bf = B.buf
        self.begin_phase()
        sl_c = B.dslot()
        B.dma("sync", sl_c, self.c_bf[:], d["cb"][:], w=[bf("c_bf")])
        B.dma("sync", sl_c, self.c_f[:], d["cf"][:], w=[bf("c_f")])
        B.dma("sync", sl_c, self.c_or[:], d["onesrow"][:], w=[bf("c_or")])
        B.dma("sync", sl_c, self.c_orf[:], d["onesrow_f"][:], w=[bf("c_orf")])
        B.dma("sync", sl_c, self.pvs[:], d["pv"][:], w=[bf("pvs")])
        B.dma("sync", sl_c, self.pv64s[:], d["pv64"][:], w=[bf("pv64s")])
        B.dma("sync", sl_c, self.rbs[:], d["rbias"][:], w=[bf("rbs")])
        bm_s = self.sb("bm_s", (64, D))
        B.dma("sync", sl_c, bm_s[:], d["bmat"][:], w=[bf("bm_s")])

        pv, dv = self.pvc, self.dvc
        B.v(lambda v: v.tensor_scalar_add(dv("c1ts"), pv("ts"), 1.0), r=[bf("pvs")], w=[bf("dvs")])
        B.v(lambda v: v.tensor_scalar_mul(dv("cmts"), pv("ts"), -1.0), r=[bf("pvs")], w=[bf("dvs")])
        B.v(lambda v: v.tensor_scalar_max(dv("reluU"), pv("tf"), 0.0), r=[bf("pvs")], w=[bf("dvs")])
        B.v(lambda v: v.tensor_scalar(dv("omur"), pv("mur"), -1.0, 1.0, ALU.mult, ALU.add), r=[bf("pvs")], w=[bf("dvs")])
        B.v(lambda v: v.tensor_scalar(dv("omuk"), pv("muk"), -1.0, 1.0, ALU.mult, ALU.add), r=[bf("pvs")], w=[bf("dvs")])
        B.v(lambda v: v.tensor_scalar_add(dv("dp1"), pv("l1d"), 1.0), r=[bf("pvs")], w=[bf("dvs")])
        B.s(lambda s: s.activation(dv("eU"), pv("tf"), AF.Exp), r=[bf("pvs")], w=[bf("dvs")])
        B.v(lambda v: v.tensor_scalar(dv("omts"), pv("ts"), -1.0, 1.0, ALU.mult, ALU.add), r=[bf("pvs")], w=[bf("dvs")])
        B.v(lambda v: v.tensor_scalar_mul(dv("nU"), pv("tf"), -1.0), r=[bf("pvs")], w=[bf("dvs")])

        prb = self.ps("P_rb", (128, 8))
        B.t(lambda t: t.matmul(prb[:], self.c_orf[:], self.rbs[:], start=True, stop=True),
            r=[bf("c_orf"), bf("rbs")], w=[self.pb("P_rb")])
        B.v(lambda v: v.tensor_copy(self.rb_b[:], prb[:]), r=[self.pb("P_rb")], w=[bf("rb_b")])

        w = self.sb("ssw", (64, 26))
        ww = bf("ssw")
        lnr = self.pv64s[:, 0:1]
        th = self.pv64s[:, 1:2]
        pvb = bf("pv64s")
        B.s(lambda s: s.activation(w[:, 0:1], lnr, AF.Exp), r=[pvb], w=[ww])
        B.s(lambda s: s.activation(w[:, 1:2], w[:, 0:1], AF.Exp, scale=-1.0), r=[ww], w=[ww])
        B.v(lambda v: v.tensor_scalar(w[:, 2:3], th, 1.0 / TWO_PI, MAGIC, ALU.mult, ALU.add), r=[pvb], w=[ww])
        B.v(lambda v: v.tensor_scalar_add(w[:, 2:3], w[:, 2:3], -MAGIC), r=[ww], w=[ww])
        B.v(lambda v: v.scalar_tensor_tensor(w[:, 3:4], w[:, 2:3], -TWO_PI, th, ALU.mult, ALU.add), r=[pvb, ww], w=[ww])
        B.v(lambda v: v.tensor_scalar_add(w[:, 4:5], th, TWO_PI / 4.0), r=[pvb], w=[ww])
        B.v(lambda v: v.tensor_scalar(w[:, 5:6], w[:, 4:5], 1.0 / TWO_PI, MAGIC, ALU.mult, ALU.add), r=[ww], w=[ww])
        B.v(lambda v: v.tensor_scalar_add(w[:, 5:6], w[:, 5:6], -MAGIC), r=[ww], w=[ww])
        B.v(lambda v: v.scalar_tensor_tensor(w[:, 6:7], w[:, 5:6], -TWO_PI, w[:, 4:5], ALU.mult, ALU.add), r=[ww], w=[ww])
        B.s(lambda s: s.activation(w[:, 7:8], w[:, 3:4], AF.Sin), r=[ww], w=[ww])
        B.s(lambda s: s.activation(w[:, 8:9], w[:, 6:7], AF.Sin), r=[ww], w=[ww])
        B.v(lambda v: v.tensor_mul(w[:, 9:10], w[:, 1:2], w[:, 8:9]), r=[ww], w=[ww])
        B.v(lambda v: v.tensor_mul(w[:, 10:11], w[:, 1:2], w[:, 7:8]), r=[ww], w=[ww])
        B.v(lambda v: v.tensor_scalar_add(w[:, 11:12], w[:, 9:10], -1.0), r=[ww], w=[ww])
        B.v(lambda v: v.tensor_mul(w[:, 12:13], w[:, 0:1], w[:, 0:1]), r=[ww], w=[ww])
        B.v(lambda v: v.tensor_mul(w[:, 13:14], th, th), r=[pvb, ww], w=[ww])
        B.v(lambda v: v.tensor_add(w[:, 12:13], w[:, 12:13], w[:, 13:14]), r=[ww], w=[ww])
        B.v(lambda v: v.reciprocal(w[:, 13:14], w[:, 12:13]), r=[ww], w=[ww])
        B.v(lambda v: v.tensor_mul(w[:, 14:15], w[:, 11:12], w[:, 0:1]), r=[ww], w=[ww])
        B.v(lambda v: v.tensor_scalar_mul(w[:, 14:15], w[:, 14:15], -1.0), r=[ww], w=[ww])
        B.v(lambda v: v.tensor_mul(w[:, 15:16], w[:, 10:11], th), r=[pvb, ww], w=[ww])
        B.v(lambda v: v.tensor_add(w[:, 14:15], w[:, 14:15], w[:, 15:16]), r=[ww], w=[ww])
        B.v(lambda v: v.tensor_mul(w[:, 14:15], w[:, 14:15], w[:, 13:14]), r=[ww], w=[ww])
        B.v(lambda v: v.tensor_mul(w[:, 16:17], w[:, 10:11], w[:, 0:1]), r=[ww], w=[ww])
        B.v(lambda v: v.tensor_scalar_mul(w[:, 16:17], w[:, 16:17], -1.0), r=[ww], w=[ww])
        B.v(lambda v: v.tensor_mul(w[:, 17:18], w[:, 11:12], th), r=[pvb, ww], w=[ww])
        B.v(lambda v: v.tensor_sub(w[:, 16:17], w[:, 16:17], w[:, 17:18]), r=[ww], w=[ww])
        B.v(lambda v: v.tensor_mul(w[:, 16:17], w[:, 16:17], w[:, 13:14]), r=[ww], w=[ww])
        pwb = bf("pw")
        B.v(lambda v: v.tensor_copy(self.pw_re[:, 0:1], w[:, 9:10]), r=[ww], w=[pwb])
        B.v(lambda v: v.tensor_copy(self.pw_im[:, 0:1], w[:, 10:11]), r=[ww], w=[pwb])
        for j in range(11):
            a, b2 = self.pw_re[:, j:j + 1], self.pw_im[:, j:j + 1]

            w18, w19 = bf("ssw18"), bf("ssw19")
            B.v(lambda v, a=a: v.tensor_mul(w[:, 18:19], a, a), r=[pwb], w=[w18])
            B.v(lambda v, b2=b2: v.tensor_mul(w[:, 19:20], b2, b2), r=[pwb], w=[w19])
            B.v(lambda v, j=j: v.tensor_sub(self.pw_re[:, j + 1:j + 2], w[:, 18:19], w[:, 19:20]),
                r=[w18, w19], w=[pwb])
            B.v(lambda v, a=a, b2=b2: v.tensor_mul(w[:, 18:19], a, b2), r=[pwb], w=[w18])
            B.v(lambda v, j=j: v.tensor_scalar_mul(self.pw_im[:, j + 1:j + 2], w[:, 18:19], 2.0),
                r=[w18], w=[pwb])
        bb_s = self.sb("bb_s", (64, 2 * D), BF16)
        bbb = bf("bb_s")
        B.v(lambda v: v.tensor_scalar_mul(bb_s[:, 0:D], bm_s[:], w[:, 14:15]), r=[bf("bm_s"), ww], w=[bbb])
        B.v(lambda v: v.tensor_scalar_mul(bb_s[:, D:2 * D], bm_s[:], w[:, 16:17]), r=[bf("bm_s"), ww], w=[bbb])
        ptr = self.ps("P_tr0", (128, 128), BF16)
        ptrb = self.pb("P_tr0")
        bbTb = bf("bbT")
        for dd in range(NDT):
            B.t(lambda t, dd=dd: t.transpose(ptr[:, 0:64], bb_s[0:64, _sl(dd, 128)], self.ident()[0:64, 0:64]),
                r=[bbb, bf("c_bf")], w=[ptrb])
            B.t(lambda t, dd=dd: t.transpose(ptr[:, 64:128], bb_s[0:64, D + dd * 128: D + (dd + 1) * 128], self.ident()[0:64, 0:64]),
                r=[bbb, bf("c_bf")], w=[ptrb])
            B.v(lambda v, dd=dd: v.tensor_copy(self.bbT[:, _sl(dd, 128)], ptr[:]),
                r=[ptrb], w=[bbTb])
        self.end_phase()

    def phase_l0aa_A(self):
        B = self.B
        d = self.d
        bf = B.buf
        pv, dv = self.pvc, self.dvc
        self.begin_phase()
        sb, ps = self.sb, self.ps
        CW = 512
        NCH = T // CW
        NSUB = CW // 128

        xc = [sb("A_xc0", (128, NDT * CW)), sb("A_xc1", (128, NDT * CW))]
        xcb = [bf("A_xc0"), bf("A_xc1")]
        sq = sb("A_sq", (128, CW)); sqb = bf("A_sq")
        xin = sb("A_xin", (128, NDT * CW)); xinb = bf("A_xin")
        xxc = [sb("A_xxc0", (128, NDT * CW), BF16), sb("A_xxc1", (128, NDT * CW), BF16)]
        xxcb = [bf("A_xxc0"), bf("A_xxc1")]
        hbound = sb("A_hbound", (128, NDT)); hbndb = bf("A_hbound")
        rows = sb("A_rows", (1, 2 * CW)); rowsb = bf("A_rows")
        tmp = sb("A_tmp", (128, CW)); tmpb = bf("A_tmp")
        wk_s = sb("A_wk", (128, NDT * D), BF16)
        wv_s = sb("A_wv", (128, NDT * D), BF16)
        wr_s = sb("A_wr", (128, NDT * D), BF16)

        kf = sb("A_kf", (128, T)); kfb = bf("A_kf")
        vf = sb("A_vf", (128, T), BF16); vfb = bf("A_vf")
        rf = sb("A_rf", (128, T), BF16); rfb = bf("A_rf")
        ekc = sb("A_ek", (128, CW)); ekcb = bf("A_ek")
        ekbf = sb("A_ekbf", (128, CW), BF16); ekbfb = bf("A_ekbf")
        smallbf2 = [sb("A_sbf0", (128, 128), BF16), sb("A_sbf1", (128, 128), BF16)]
        smallbf2b = [bf("A_sbf0"), bf("A_sbf1")]
        zh = sb("A_zh", (128, CW)); zhb = bf("A_zh")
        dchunk = sb("A_D", (128, CW)); dchb = bf("A_D")
        lgc = sb("A_lg", (128, CW)); lgcb = bf("A_lg")
        bc = sb("A_bc", (128, CW)); bcb = bf("A_bc")
        aac = sb("A_aa", (128, CW)); aacb = bf("A_aa")
        t4 = sb("A_t4", (128, CW)); t4b = bf("A_t4")
        t5 = sb("A_t5", (128, CW)); t5b = bf("A_t5")
        xinld = sb("A_xinld", (128, CW)); xinldb = bf("A_xinld")
        x2c = sb("A_x2c", (128, CW)); x2cb = bf("A_x2c")
        cols = sb("A_cols", (128, 8)); colb = bf("A_cols")
        sub1 = sb("A_sub1", (128, 128)); sub1b = bf("A_sub1")
        sub2 = sb("A_sub2", (128, 128)); sub2b = bf("A_sub2")
        sub3 = sb("A_sub3", (128, 128)); sub3b = bf("A_sub3")

        pK = ps("A_pK", (128, CW)); pKb = self.pb("A_pK")
        pV = ps("A_pV", (128, CW)); pVb = self.pb("A_pV")
        pR = ps("A_pR", (128, CW)); pRb = self.pb("A_pR")
        pT2 = [ps("A_pT0", (128, 128), BF16), ps("A_pT1", (128, 128), BF16)]
        pT2b = [self.pb("A_pT0"), self.pb("A_pT1")]
        pC = ps("A_pC", (128, 132)); pCb = self.pb("A_pC")
        pG = ps("A_pG", (128, CW)); pGb = self.pb("A_pG")
        pH = ps("A_pH", (128, CW)); pHb = self.pb("A_pH")

        slw = B.dslot()
        for nm, dst in [("wkT", wk_s), ("wvT", wv_s), ("wrT", wr_s)]:
            B.dma("gpsimd", slw, dst.ap().rearrange("p (a c) -> p a c", a=NDT),
                  d[nm].ap().rearrange("(a p) c -> p a c", p=128), w=[bf("A_" + nm)])
        wkb, wvb, wrb = bf("A_wkT"), bf("A_wvT"), bf("A_wrT")

        B.v(lambda v: v.tensor_copy(hbound[:], pv("ts")), r=[bf("pvs")], w=[hbndb])

        cbuf = [sb("A_cb0", (128, 4 * D), BF16), sb("A_cb1", (128, 4 * D), BF16)]
        cbufb = [bf("A_cb0"), bf("A_cb1")]
        slci = [B.dslot(), B.dslot()]
        slco = [B.dslot(), B.dslot()]
        cast_tasks = []
        for cq in range(8):
            cast_tasks.append(("wck", 0, cq))
        for e in range(8):
            for cq in range(16):
                cast_tasks.append(("w1", e, cq))
            for cq in range(8):
                cast_tasks.append(("w2", e, cq))
        self._cast_i = 0

        def emit_casts(n):
            for _ in range(n):
                if self._cast_i >= len(cast_tasks):
                    return
                which, e, cq = cast_tasks[self._cast_i]
                par = self._cast_i % 2
                self._cast_i += 1
                if which == "wck":
                    B.dma("gpsimd", slci[par],
                          cbuf[par].ap().rearrange("p (a f) -> p a f", a=8)[:, :, 0:512],
                          d["wckT"].ap().rearrange("(a p) f -> p a f", p=128)[:, :, cq * 512:(cq + 1) * 512],
                          w=[cbufb[par]])
                    B.dma("gpsimd", slco[par],
                          d["wckc"].ap().rearrange("(a p) f -> p a f", p=128)[:, :, cq * 512:(cq + 1) * 512],
                          cbuf[par].ap().rearrange("p (a f) -> p a f", a=8)[:, :, 0:512],
                          r=[cbufb[par]], w=[bf("wckc")])
                elif which == "w1":
                    B.dma("gpsimd", slci[par],
                          cbuf[par].ap().rearrange("p (a f) -> p a f", a=8)[:, :, 0:512],
                          d["w1"][e].rearrange("(a p) f -> p a f", p=128)[:, :, cq * 512:(cq + 1) * 512],
                          w=[cbufb[par]])
                    B.dma("gpsimd", slco[par],
                          d["w1c"][e].rearrange("(a p) f -> p a f", p=128)[:, :, cq * 512:(cq + 1) * 512],
                          cbuf[par].ap().rearrange("p (a f) -> p a f", a=8)[:, :, 0:512],
                          r=[cbufb[par]], w=[bf("w1c%d" % e)])
                else:
                    B.dma("gpsimd", slci[par],
                          cbuf[par].ap().rearrange("p (a f) -> p a f", a=4),
                          d["w2"][e].rearrange("(a p) f -> p a f", p=128)[:, 4 * cq:4 * cq + 4, :],
                          w=[cbufb[par]])
                    B.dma("gpsimd", slco[par],
                          d["w2c"][e].rearrange("(a p) f -> p a f", p=128)[:, 4 * cq:4 * cq + 4, :],
                          cbuf[par].ap().rearrange("p (a f) -> p a f", a=4),
                          r=[cbufb[par]], w=[bf("w2c%d" % e)])

        slx = [B.dslot(), B.dslot()]
        slxin = B.dslot()
        slxi2 = B.dslot()
        slxxo = B.dslot()
        slxx = [B.dslot(), B.dslot()]
        slo = B.dslot()

        for c in range(NCH):
            par = c % 2
            B.dma("sync", slx[par], xc[par].ap().rearrange("p (a t) -> p a t", a=NDT),
                  d["xT"][:, _sl(c, CW)].rearrange("(a p) t -> p a t", p=128), w=[xcb[par]])
            self.ln_fm(xc[par], [xcb[par]], CW, sq, rows, pG, pGb, pH, pHb, sqb, rowsb)
            for dd in range(NDT):
                B.v(lambda v, dd=dd, par=par: v.tensor_tensor(xin[:, _sl(dd, CW)], xc[par][:, _sl(dd, CW)], pG[:], ALU.add),
                    r=[xcb[par], pGb], w=[xinb])
                B.v(lambda v, dd=dd: v.tensor_tensor(xin[:, _sl(dd, CW)], xin[:, _sl(dd, CW)], pH[:], ALU.mult),
                    r=[pHb], w=[xinb])
            B.dma("sync", slxin, d["xins"][:, _sl(c, CW)].rearrange("(a p) t -> p a t", p=128),
                  xin.ap().rearrange("p (a t) -> p a t", a=NDT),
                  r=[xinb], w=self.dram_bufs("xins", c * CW, (c + 1) * CW))
            self.ln_fm(xin, [xinb], CW, sq, rows, pG, pGb, pH, pHb, sqb, rowsb)
            for dd in range(NDT):
                B.v(lambda v, dd=dd: v.tensor_tensor(xin[:, _sl(dd, CW)], xin[:, _sl(dd, CW)], pG[:], ALU.add),
                    r=[pGb], w=[xinb])
                B.v(lambda v, dd=dd: v.tensor_tensor(xin[:, _sl(dd, CW)], xin[:, _sl(dd, CW)], pH[:], ALU.mult),
                    r=[pHb], w=[xinb])
                B.v(lambda v, dd=dd: v.tensor_scalar(xin[:, _sl(dd, CW)], xin[:, _sl(dd, CW)],
                                                     pv("ln1w")[:, dd:dd + 1], pv("ln1b")[:, dd:dd + 1],
                                                     ALU.mult, ALU.add),
                    r=[bf("pvs")], w=[xinb])
            xb = xxc[par]
            xbb = xxcb[par]
            for dd in range(NDT):
                o0 = dd * CW
                B.v(lambda v, dd=dd: v.tensor_scalar_mul(tmp[:], xin[:, _sl(dd, CW)], dv("c1ts")[:, dd:dd + 1]),
                    r=[xinb, bf("dvs")], w=[tmpb])
                B.v(lambda v, dd=dd, o0=o0, xb=xb: v.scalar_tensor_tensor(
                    xb[:, o0 + 1:o0 + CW], xin[:, dd * CW:(dd + 1) * CW - 1],
                    dv("cmts")[:, dd:dd + 1], tmp[:, 1:CW], ALU.mult, ALU.add),
                    r=[xinb, tmpb, bf("dvs")], w=[xbb])
                B.v(lambda v, dd=dd, o0=o0, xb=xb: v.scalar_tensor_tensor(
                    xb[:, o0:o0 + 1], hbound[:, dd:dd + 1],
                    dv("cmts")[:, dd:dd + 1], tmp[:, 0:1], ALU.mult, ALU.add),
                    r=[hbndb, tmpb, bf("dvs")], w=[xbb])
                B.v(lambda v, dd=dd: v.tensor_copy(hbound[:, dd:dd + 1], xin[:, (dd + 1) * CW - 1:(dd + 1) * CW]),
                    r=[xinb], w=[hbndb])
            B.dma("sync", slxxo, d["xxxs"][:, _sl(c, CW)].rearrange("(a p) t -> p a t", p=128),
                  xb.ap().rearrange("p (a t) -> p a t", a=NDT),
                  r=[xbb], w=self.dram_bufs("xxxs", c * CW, (c + 1) * CW))
            emit_casts(3)

        for cc in range(NDT):
            for qc in range(NCH):
                par = qc % 2
                B.dma("sync", slxx[par], xxc[par].ap().rearrange("p (a t) -> p a t", a=NDT),
                      d["xxxs"][:, _sl(qc, CW)].rearrange("(a p) t -> p a t", p=128),
                      r=self.dram_bufs("xxxs", qc * CW, (qc + 1) * CW), w=[xxcb[par]])
                for dd in range(NDT):
                    B.t(lambda t, cc=cc, qc=qc, dd=dd, par=par: t.matmul(
                        pK[:], wk_s[:, dd * D + cc * 128: dd * D + (cc + 1) * 128],
                        xxc[par][:, _sl(dd, CW)],
                        start=(dd == 0), stop=(dd == NDT - 1)),
                        r=[wkb, xxcb[par]], w=[pKb])
                B.v(lambda v, qc=qc: v.tensor_copy(kf[:, _sl(qc, CW)], pK[:]), r=[pKb], w=[kfb])
                for dd in range(NDT):
                    B.t(lambda t, cc=cc, qc=qc, dd=dd, par=par: t.matmul(
                        pV[:], wv_s[:, dd * D + cc * 128: dd * D + (cc + 1) * 128],
                        xxc[par][:, _sl(dd, CW)],
                        start=(dd == 0), stop=(dd == NDT - 1)),
                        r=[wvb, xxcb[par]], w=[pVb])
                B.v(lambda v, qc=qc: v.tensor_copy(vf[:, _sl(qc, CW)], pV[:]), r=[pVb], w=[vfb])
                for dd in range(NDT):
                    B.t(lambda t, cc=cc, qc=qc, dd=dd, par=par: t.matmul(
                        pR[:], wr_s[:, dd * D + cc * 128: dd * D + (cc + 1) * 128],
                        xxc[par][:, _sl(dd, CW)],
                        start=(dd == 0), stop=(dd == NDT - 1)),
                        r=[wrb, xxcb[par]], w=[pRb])
                B.s(lambda s, qc=qc: s.activation(rf[:, _sl(qc, CW)], pR[:], AF.Sigmoid),
                    r=[pRb], w=[rfb])
                emit_casts(3)
            B.v(lambda v: v.reduce_max(cols[:, 0:1], kf[:], axis=AX.X), r=[kfb], w=[colb])
            B.v(lambda v, cc=cc: v.tensor_tensor(cols[:, 0:1], cols[:, 0:1], dv("reluU")[:, cc:cc + 1], ALU.add),
                r=[bf("dvs")], w=[colb])
            B.v(lambda v: v.tensor_scalar_mul(cols[:, 1:2], cols[:, 0:1], -1.0), r=[], w=[colb])
            B.s(lambda s: s.activation(cols[:, 2:3], cols[:, 1:2], AF.Exp), r=[colb], w=[colb])
            B.v(lambda v: v.tensor_copy(cols[:, 3:4], cols[:, 2:3]), r=[], w=[colb])
            B.v(lambda v: v.memset(cols[:, 4:5], 0.0), w=[colb])
            for qc in range(NCH):
                B.s(lambda s, qc=qc: s.activation(ekc[:], kf[:, _sl(qc, CW)], AF.Exp, bias=cols[:, 1:2]),
                    r=[kfb, colb], w=[ekcb])
                B.v(lambda v: v.tensor_copy(ekbf[:], ekc[:]), r=[ekcb], w=[ekbfb])
                for j in range(NSUB):
                    jp = j % 2
                    B.t(lambda t, j=j, jp=jp: t.transpose(pT2[jp][:], ekbf[:, _sl(j, 128)], self.ident()),
                        r=[ekbfb, bf("c_bf")], w=[pT2b[jp]])
                    B.v(lambda v, jp=jp: v.tensor_copy(smallbf2[jp][:], pT2[jp][:]),
                        r=[pT2b[jp]], w=[smallbf2b[jp]])
                    B.t(lambda t, jp=jp: t.matmul(pC[:, 0:129], smallbf2[jp][:], self.tri(),
                                                  start=True, stop=True),
                        r=[smallbf2b[jp], bf("c_bf")], w=[pCb])
                    B.v(lambda v, j=j: v.tensor_scalar_add(zh[:, _sl(j, 128)], pC[:, 0:128], cols[:, 3:4]),
                        r=[pCb, colb], w=[zhb])
                    B.v(lambda v: v.tensor_tensor(cols[:, 3:4], cols[:, 3:4], pC[:, 128:129], ALU.add),
                        r=[pCb], w=[colb])
                B.s(lambda s: s.activation(zh[:], zh[:], AF.Ln), r=[], w=[zhb])
                B.v(lambda v, qc=qc: v.scalar_tensor_tensor(tmp[:], kf[:, _sl(qc, CW)], cols[:, 1:2],
                                                            zh[:], ALU.add, ALU.subtract),
                    r=[kfb, colb, zhb], w=[tmpb])
                B.s(lambda s: s.activation(dchunk[:], tmp[:], AF.Exp), r=[tmpb], w=[dchb])
                B.v(lambda v: v.tensor_scalar_max(lgc[:], tmp[:], 0.0), r=[tmpb], w=[lgcb])
                B.v(lambda v: v.tensor_scalar_min(bc[:], dchunk[:], 1.0), r=[dchb], w=[bcb])
                B.v(lambda v, qc=qc: v.tensor_mul(bc[:], bc[:], vf[:, _sl(qc, CW)]), r=[vfb], w=[bcb])
                B.v(lambda v: v.tensor_copy(cols[:, 5:6], cols[:, 4:5]), r=[colb], w=[colb])
                B.v(lambda v: v.tensor_copy(ekbf[:], lgc[:]), r=[lgcb], w=[ekbfb])
                for j in range(NSUB):
                    jp = j % 2
                    B.t(lambda t, j=j, jp=jp: t.transpose(pT2[jp][:], ekbf[:, _sl(j, 128)], self.ident()),
                        r=[ekbfb, bf("c_bf")], w=[pT2b[jp]])
                    B.v(lambda v, jp=jp: v.tensor_copy(smallbf2[jp][:], pT2[jp][:]),
                        r=[pT2b[jp]], w=[smallbf2b[jp]])
                    B.t(lambda t, jp=jp: t.matmul(pC[:, 0:129], smallbf2[jp][:], self.tri(),
                                                  start=True, stop=True),
                        r=[smallbf2b[jp], bf("c_bf")], w=[pCb])
                    B.v(lambda v, j=j: v.tensor_tensor(ekc[:, _sl(j, 128)], pC[:, 0:128],
                                                       lgc[:, _sl(j, 128)], ALU.add),
                        r=[pCb, lgcb], w=[ekcb])
                B.s(lambda s: s.activation(t4[:], ekc[:], AF.Exp, scale=-1.0), r=[ekcb], w=[t4b])
                B.s(lambda s: s.activation(t5[:], ekc[:], AF.Exp), r=[ekcb], w=[t5b])
                B.v(lambda v: v.tensor_mul(t5[:], t5[:], bc[:]), r=[bcb], w=[t5b])
                B.v(lambda v: v.tensor_copy(ekbf[:], t5[:]), r=[t5b], w=[ekbfb])
                for j in range(NSUB):
                    jp = j % 2
                    B.t(lambda t, j=j, jp=jp: t.transpose(pT2[jp][:], ekbf[:, _sl(j, 128)], self.ident()),
                        r=[ekbfb, bf("c_bf")], w=[pT2b[jp]])
                    B.v(lambda v, jp=jp: v.tensor_copy(smallbf2[jp][:], pT2[jp][:]),
                        r=[pT2b[jp]], w=[smallbf2b[jp]])
                    B.t(lambda t, jp=jp: t.matmul(pC[:, 0:129], smallbf2[jp][:], self.tri(),
                                                  start=True, stop=True),
                        r=[smallbf2b[jp], bf("c_bf")], w=[pCb])
                    B.v(lambda v, j=j: v.tensor_tensor(zh[:, _sl(j, 128)], pC[:, 0:128],
                                                       t5[:, _sl(j, 128)], ALU.add),
                        r=[pCb, t5b], w=[zhb])
                for j in range(NSUB):
                    B.v(lambda v, j=j: v.tensor_scalar_add(sub1[:], zh[:, _sl(j, 128)], cols[:, 4:5]),
                        r=[zhb, colb], w=[sub1b])
                    B.v(lambda v, j=j: v.tensor_tensor(aac[:, _sl(j, 128)], sub1[:],
                                                       t4[:, _sl(j, 128)], ALU.mult),
                        r=[sub1b, t4b], w=[aacb])
                    B.v(lambda v, j=j: v.tensor_copy(cols[:, 4:5], aac[:, j * 128 + 127: j * 128 + 128]),
                        r=[aacb], w=[colb])
                B.s(lambda s, cc=cc: s.activation(t5[:], tmp[:], AF.Sigmoid,
                                                  bias=dv("nU")[:, cc:cc + 1], scale=-1.0),
                    r=[tmpb, bf("dvs")], w=[t5b])
                B.v(lambda v, qc=qc: v.tensor_mul(t4[:], dchunk[:], vf[:, _sl(qc, CW)]),
                    r=[dchb, vfb], w=[t4b])
                B.v(lambda v, cc=cc: v.tensor_scalar_mul(t4[:], t4[:], dv("eU")[:, cc:cc + 1]),
                    r=[bf("dvs")], w=[t4b])
                B.v(lambda v: v.tensor_tensor(t4[:, 1:CW], t4[:, 1:CW], aac[:, 0:CW - 1], ALU.add),
                    r=[aacb], w=[t4b])
                B.v(lambda v: v.tensor_tensor(t4[:, 0:1], t4[:, 0:1], cols[:, 5:6], ALU.add),
                    r=[colb], w=[t4b])
                B.v(lambda v: v.tensor_mul(t4[:], t4[:], t5[:]), r=[t5b], w=[t4b])
                B.v(lambda v, qc=qc: v.tensor_mul(t4[:], t4[:], rf[:, _sl(qc, CW)]), r=[rfb], w=[t4b])
                B.dma("sync", slxi2, xinld[:], d["xins"][_sl(cc, 128), _sl(qc, CW)],
                      r=self.dram_bufs("xins", qc * CW, (qc + 1) * CW), w=[xinldb])
                B.v(lambda v: v.tensor_tensor(x2c[:], t4[:], xinld[:], ALU.add),
                    r=[t4b, xinldb], w=[x2cb])
                if qc == NCH - 1:
                    B.v(lambda v, cc=cc: v.tensor_copy(self.x2last[:, cc:cc + 1], x2c[:, CW - 1:CW]),
                        r=[x2cb], w=[bf("x2last")])
                B.dma("sync", slo, d["x2s"][_sl(cc, 128), _sl(qc, CW)], x2c[:],
                      r=[x2cb], w=self.dram_bufs("x2s", qc * CW, (qc + 1) * CW))
        emit_casts(10 ** 6)
        self.end_phase()

    def phase_l0aa_B(self):
        B = self.B
        d = self.d
        bf = B.buf
        pv, dv = self.pvc, self.dvc
        self.begin_phase()
        sb, ps = self.sb, self.ps
        CW = 256
        NCH = T // CW
        NOT = 32

        x2c = [sb("B_x2c0", (128, NDT * CW)), sb("B_x2c1", (128, NDT * CW))]
        x2cb = [bf("B_x2c0"), bf("B_x2c1")]
        xl2 = [sb("B_xl%d" % i, (128, CW)) for i in range(4)]
        xlb = [bf("B_xl%d" % i) for i in range(4)]
        sq = sb("B_sq", (128, CW)); sqb = bf("B_sq")
        h2 = sb("B_h2", (128, NDT * CW)); h2b = bf("B_h2")
        xxx2 = [sb("B_xxx20", (128, NDT * CW), BF16), sb("B_xxx21", (128, NDT * CW), BF16)]
        xxx2b = [bf("B_xxx20"), bf("B_xxx21")]
        rows = sb("B_rows", (1, 2 * CW))
        kact = sb("B_kact", (128, NOT * CW), BF16); kactb = bf("B_kact")
        outb = sb("B_out", (128, NDT * CW)); outbb = bf("B_out")
        tmp = sb("B_tmp", (128, CW)); tmpb = bf("B_tmp")
        t2 = sb("B_t2", (128, CW)); t2b = bf("B_t2")
        h2bound = sb("B_h2b", (128, NDT)); h2bndb = bf("B_h2b")
        wcv_s = sb("B_wcv", (128, NOT * D), BF16); wcvb = bf("B_wcv")
        wcr_s = sb("B_wcr", (128, NDT * D), BF16); wcrb = bf("B_wcr")
        wck_q = [sb("B_wck0", (128, NDT * D), BF16), sb("B_wck1", (128, NDT * D), BF16)]
        wckb = [bf("B_wck0"), bf("B_wck1")]

        pG = ps("B_pG", (128, CW)); pGb = self.pb("B_pG")
        pH = ps("B_pH", (128, CW)); pHb = self.pb("B_pH")
        pKC = [ps("B_pKC0", (128, CW)), ps("B_pKC1", (128, CW))]
        pKCb = [self.pb("B_pKC0"), self.pb("B_pKC1")]
        pRC = ps("B_pRC", (128, CW)); pRCb = self.pb("B_pRC")
        pVV = [ps("B_pVV%d" % i, (128, CW)) for i in range(3)]
        pVVb = [self.pb("B_pVV%d" % i) for i in range(3)]

        slw = B.dslot()
        B.dma("gpsimd", slw, wcv_s.ap().rearrange("p (a c) -> p a c", a=NOT),
              d["wcvT"].ap().rearrange("(a p) c -> p a c", p=128), w=[wcvb])
        B.dma("gpsimd", slw, wcr_s.ap().rearrange("p (a c) -> p a c", a=NDT),
              d["wcrT"].ap().rearrange("(a p) c -> p a c", p=128), w=[wcrb])
        slq = [B.dslot(), B.dslot()]

        B.s(lambda s: s.square(tmp[:, 0:NDT], self.x2last[:]), r=[bf("x2last")], w=[tmpb])
        B.t(lambda t: t.matmul(pG[0:1, 0:NDT], self.ones_col_f(), self.x2last[:],
                               start=True, stop=True), r=[bf("x2last"), bf("c_f")], w=[pGb])
        B.t(lambda t: t.matmul(pH[0:1, 0:NDT], self.ones_col_f(), tmp[:, 0:NDT],
                               start=True, stop=True), r=[tmpb, bf("c_f")], w=[pHb])
        B.v(lambda v: v.reduce_sum(rows[0:1, 0:1], pG[0:1, 0:NDT], axis=AX.X), r=[pGb], w=[bf("r_nm")])
        B.v(lambda v: v.reduce_sum(rows[0:1, 1:2], pH[0:1, 0:NDT], axis=AX.X), r=[pHb], w=[bf("r_var")])
        B.v(lambda v: v.tensor_scalar_mul(rows[0:1, 2:3], rows[0:1, 0:1], 1.0 / D), r=[bf("r_nm")], w=[bf("r_mean")])
        B.v(lambda v: v.tensor_scalar_mul(rows[0:1, 0:1], rows[0:1, 0:1], -1.0 / D), r=[bf("r_nm")], w=[bf("r_nm")])
        B.v(lambda v: v.tensor_scalar_mul(rows[0:1, 3:4], rows[0:1, 1:2], 1.0 / D), r=[bf("r_var")], w=[bf("r_var")])
        B.v(lambda v: v.tensor_mul(rows[0:1, 4:5], rows[0:1, 2:3], rows[0:1, 2:3]), r=[bf("r_mean")], w=[bf("r_tmp")])
        B.v(lambda v: v.tensor_sub(rows[0:1, 3:4], rows[0:1, 3:4], rows[0:1, 4:5]), r=[bf("r_tmp")], w=[bf("r_var")])
        B.s(lambda s: s.activation(rows[0:1, 3:4], rows[0:1, 3:4], AF.Ln, bias=self.c_f[0:1, 129:130]),
            r=[bf("c_f")], w=[bf("r_var")])
        B.s(lambda s: s.activation(rows[0:1, 1:2], rows[0:1, 3:4], AF.Exp, scale=-0.5),
            r=[bf("r_var")], w=[bf("r_rstd")])
        B.t(lambda t: t.matmul(pG[:, 0:1], self.c_orf[:], rows[0:1, 0:1], start=True, stop=True),
            r=[bf("r_nm"), bf("c_orf")], w=[pGb])
        B.t(lambda t: t.matmul(pH[:, 0:1], self.c_orf[:], rows[0:1, 1:2], start=True, stop=True),
            r=[bf("r_rstd"), bf("c_orf")], w=[pHb])
        B.v(lambda v: v.tensor_copy(tmp[:, 0:1], pG[:, 0:1]), r=[pGb], w=[tmpb])
        B.v(lambda v: v.tensor_copy(tmp[:, 1:2], pH[:, 0:1]), r=[pHb], w=[bf("B_tmp1")])
        B.v(lambda v: v.tensor_scalar(h2bound[:], self.x2last[:], tmp[:, 0:1], tmp[:, 1:2],
                                      ALU.add, ALU.mult),
            r=[bf("x2last"), tmpb, bf("B_tmp1")], w=[h2bndb])
        B.v(lambda v: v.tensor_mul(h2bound[:], h2bound[:], pv("ln2w")), r=[bf("pvs")], w=[h2bndb])
        B.v(lambda v: v.tensor_add(h2bound[:], h2bound[:], pv("ln2b")), r=[bf("pvs")], w=[h2bndb])

        slx2 = B.dslot()
        slxl = [B.dslot() for _ in range(4)]
        slo = B.dslot()

        def prep_ln(c1):
            par1 = c1 % 2
            self.ln_fm(x2c[par1], [x2cb[par1]], CW, sq, rows, pG, pGb, pH, pHb, sqb,
                       bf("B_rowsX"))

        def prep_h2(c1):
            par1 = c1 % 2
            for dd in range(NDT):
                B.v(lambda v, dd=dd, par1=par1: v.tensor_tensor(
                    h2[:, _sl(dd, CW)], x2c[par1][:, _sl(dd, CW)], pG[:, 0:CW], ALU.add),
                    r=[x2cb[par1], pGb], w=[h2b])
                B.v(lambda v, dd=dd: v.tensor_tensor(h2[:, _sl(dd, CW)], h2[:, _sl(dd, CW)],
                                                     pH[:, 0:CW], ALU.mult),
                    r=[pHb], w=[h2b])
                B.v(lambda v, dd=dd: v.tensor_scalar(h2[:, _sl(dd, CW)], h2[:, _sl(dd, CW)],
                                                     pv("ln2w")[:, dd:dd + 1], pv("ln2b")[:, dd:dd + 1],
                                                     ALU.mult, ALU.add), r=[bf("pvs")], w=[h2b])
            for dd in range(NDT):
                B.v(lambda v, dd=dd: v.tensor_scalar_mul(tmp[:], h2[:, _sl(dd, CW)], dv("omts")[:, dd:dd + 1]),
                    r=[h2b, bf("dvs")], w=[tmpb])
                B.v(lambda v, dd=dd, par1=par1: v.scalar_tensor_tensor(
                    xxx2[par1][:, dd * CW + 1:(dd + 1) * CW], h2[:, dd * CW:(dd + 1) * CW - 1],
                    pv("ts")[:, dd:dd + 1], tmp[:, 1:CW], ALU.mult, ALU.add),
                    r=[h2b, tmpb, bf("pvs")], w=[xxx2b[par1]])
                B.v(lambda v, dd=dd, par1=par1: v.scalar_tensor_tensor(
                    xxx2[par1][:, dd * CW:dd * CW + 1], h2bound[:, dd:dd + 1],
                    pv("ts")[:, dd:dd + 1], tmp[:, 0:1], ALU.mult, ALU.add),
                    r=[h2bndb, tmpb, bf("pvs")], w=[xxx2b[par1]])
                B.v(lambda v, dd=dd: v.tensor_copy(h2bound[:, dd:dd + 1], h2[:, (dd + 1) * CW - 1:(dd + 1) * CW]),
                    r=[h2b], w=[h2bndb])

        B.dma("gpsimd", slx2, x2c[0].ap().rearrange("p (a t) -> p a t", a=NDT),
              d["x2s"][:, _sl(0, CW)].rearrange("(a p) t -> p a t", p=128),
              r=self.dram_bufs("x2s", 0, CW), w=[x2cb[0]])
        prep_ln(0)
        prep_h2(0)

        for c in range(NCH):
            par = c % 2
            c1 = c + 1
            par1 = c1 % 2
            B.dma("gpsimd", slq[0],
                  wck_q[0].ap().rearrange("p (a c) -> p a c", a=NDT),
                  d["wckc"].ap().rearrange("(a p) c -> p a c", p=128)[:, :, 0:D],
                  r=[bf("wckc")], w=[wckb[0]])
            for i in range(4):
                B.dma("sync", slxl[i], xl2[i][:], d["xT"][_sl(i, 128), _sl(c, CW)],
                      w=[xlb[i]])
            if c1 < NCH:
                B.dma("gpsimd", slx2, x2c[par1].ap().rearrange("p (a t) -> p a t", a=NDT),
                      d["x2s"][:, _sl(c1, CW)].rearrange("(a p) t -> p a t", p=128),
                      r=self.dram_bufs("x2s", c1 * CW, (c1 + 1) * CW), w=[x2cb[par1]])
            for q in range(4):
                if q > 0:
                    B.dma("gpsimd" if q % 2 == 0 else "sync", slq[q % 2],
                          wck_q[q % 2].ap().rearrange("p (a c) -> p a c", a=NDT),
                          d["wckc"].ap().rearrange("(a p) c -> p a c", p=128)[:, :, q * D:(q + 1) * D],
                          r=[bf("wckc")], w=[wckb[q % 2]])
                for oq in range(8):
                    ot = q * 8 + oq
                    pk = pKC[ot % 2]
                    pkb = pKCb[ot % 2]
                    for dd in range(NDT):
                        B.t(lambda t, q=q, oq=oq, dd=dd, pk=pk, par=par: t.matmul(
                            pk[:], wck_q[q % 2][:, dd * D + oq * 128: dd * D + (oq + 1) * 128],
                            xxx2[par][:, _sl(dd, CW)], start=(dd == 0), stop=(dd == NDT - 1)),
                            r=[wckb[q % 2], xxx2b[par]], w=[pkb])
                    B.v(lambda v, pk=pk: v.tensor_scalar_max(t2[:], pk[:], 0.0), r=[pkb], w=[t2b])
                    B.s(lambda s, ot=ot: s.square(kact[:, _sl(ot, CW)], t2[:]), r=[t2b], w=[kactb])
            if c1 < NCH:
                prep_ln(c1)
                prep_h2(c1)
            for ct in range(NDT):
                for dd in range(NDT):
                    B.t(lambda t, ct=ct, dd=dd, par=par: t.matmul(
                        pRC[:], wcr_s[:, dd * D + ct * 128: dd * D + (ct + 1) * 128],
                        xxx2[par][:, _sl(dd, CW)], start=(dd == 0), stop=(dd == NDT - 1)),
                        r=[wcrb, xxx2b[par]], w=[pRCb])
                pvv = pVV[ct % 3]
                pvvb = pVVb[ct % 3]
                for ht in range(NOT):
                    B.t(lambda t, ct=ct, ht=ht, pvv=pvv: t.matmul(
                        pvv[:], wcv_s[:, ht * D + ct * 128: ht * D + (ct + 1) * 128],
                        kact[:, _sl(ht, CW)], start=(ht == 0), stop=(ht == NOT - 1)),
                        r=[wcvb, kactb], w=[pvvb])
                B.s(lambda s: s.activation(t2[:], pRC[:], AF.Sigmoid), r=[pRCb], w=[t2b])
                B.v(lambda v, ct=ct, pvv=pvv: v.tensor_tensor(
                    outb[:, _sl(ct, CW)], t2[:], pvv[:], ALU.mult),
                    r=[t2b, pvvb], w=[outbb])
                B.v(lambda v, ct=ct, par=par: v.tensor_tensor(
                    outb[:, _sl(ct, CW)], outb[:, _sl(ct, CW)], x2c[par][:, _sl(ct, CW)], ALU.add),
                    r=[x2cb[par]], w=[outbb])
                B.v(lambda v, ct=ct: v.tensor_tensor(
                    outb[:, _sl(ct, CW)], outb[:, _sl(ct, CW)], xl2[ct % 4][:], ALU.add),
                    r=[xlb[ct % 4]], w=[outbb])
                if ct < 4:
                    i = ct + 4
                    B.dma("sync", slxl[i % 4], xl2[i % 4][:], d["xT"][_sl(i, 128), _sl(c, CW)],
                          w=[xlb[i % 4]])
            B.dma("sync", slo, d["xres"][:, _sl(c, CW)].rearrange("(a p) t -> p a t", p=128),
                  outb.ap().rearrange("p (a t) -> p a t", a=NDT),
                  r=[outbb], w=self.dram_bufs("xres", c * CW, (c + 1) * CW))
        self.end_phase()


    def phase_moe(self, call):
        B = self.B
        d = self.d
        bf = B.buf
        self.begin_phase()
        sb, ps = self.sb, self.ps
        CW = 1024
        NCH = T // CW
        pre = "M%d_" % call

        def gb(n):
            return bf(pre + n)

        xdt = [sb(pre + "xdt0", (128, CW)), sb(pre + "xdt1", (128, CW))]
        xdtb = [gb("xdt0"), gb("xdt1")]
        sq = sb(pre + "sq", (128, CW)); sqb = gb("sq")
        rows = sb(pre + "rows", (1, 2 * CW))
        xin = sb(pre + "xin", (128, NDT * CW), BF16); xinb = gb("xin")
        xinm = sb(pre + "xinm", (128, NDT * CW), BF16)
        xinmb_d = [gb("xinm%d" % dd) for dd in range(NDT)]
        yac = sb(pre + "yac", (128, NDT * CW)); yacb = gb("yac")
        act = sb(pre + "act", (128, 16 * CW), BF16); actb = gb("act")
        maskb_s = [sb(pre + "maskb0", (128, CW), BF16), sb(pre + "maskb1", (128, CW), BF16)]
        maskbb = [gb("maskb0"), gb("maskb1")]
        lgs = sb(pre + "lgs", (128, 128)); lgsb = gb("lgs")
        maskc = sb(pre + "maskc", (128, 96)); maskcb = gb("maskc")
        maskT = sb(pre + "maskT", (8, CW), BF16); maskTb = gb("maskT")
        maxpT = sb(pre + "maxpT", (1, CW)); maxpTb = gb("maxpT")
        rw_s = sb(pre + "rw", (128, NDT * 8), BF16); rwb = gb("rw")
        w1b = [sb(pre + "w1_%d" % i, (128, NDT * CW), BF16) for i in range(2)]
        w1bb = [gb("w1_%d" % i) for i in range(2)]
        w2b = [sb(pre + "w2_%d" % i, (128, 4 * CW), BF16) for i in range(3)]
        w2bb = [gb("w2_%d" % i) for i in range(3)]
        maxb = sb(pre + "maxb", (128, CW)); maxbb = gb("maxb")
        outd2 = [sb(pre + "outd2_%d" % i, (128, CW)) for i in range(2)]
        outd2b = [gb("outd2_%d" % i) for i in range(2)]

        pp = [ps(pre + "pp%d" % i, (128, CW)) for i in range(4)]
        ppb = [self.pb(pre + "pp%d" % i) for i in range(4)]

        slw1 = [B.dslot() for _ in range(4)]
        slw2 = [B.dslot() for _ in range(4)]
        slfin = [B.dslot(), B.dslot()]
        slx = [B.dslot(), B.dslot()]
        slo = B.dslot()
        slc = B.dslot()
        B.dma("gpsimd", slc, rw_s.ap().rearrange("p (a e) -> p a e", a=NDT),
              d["rw"].ap().rearrange("(a p) e -> p a e", p=128), w=[rwb])

        r_nm, r_rstd, r_var, r_mean, r_tmp = (bf("r_nm"), bf("r_rstd"), bf("r_var"),
                                              bf("r_mean"), bf("r_tmp"))
        for c in range(NCH):
            t0 = c * CW
            for dd in range(NDT):
                par = dd % 2
                B.dma("sync", slx[par], xdt[par][:], d["xres"][_sl(dd, 128), _sl(c, CW)],
                      r=self.dram_bufs("xres", t0, t0 + CW), w=[xdtb[par]])
                B.s(lambda s, par=par: s.square(sq[:], xdt[par][:]), r=[xdtb[par]], w=[sqb])
                for hf in range(2):
                    B.t(lambda t, par=par, hf=hf, dd=dd: t.matmul(
                        pp[0][0:1, _sl(hf, 512)], self.ones_col_f(), xdt[par][:, _sl(hf, 512)],
                        start=(dd == 0), stop=(dd == NDT - 1)),
                        r=[xdtb[par], bf("c_f")], w=[ppb[0]])
                    B.t(lambda t, hf=hf, dd=dd: t.matmul(
                        pp[1][0:1, _sl(hf, 512)], self.ones_col_f(), sq[:, _sl(hf, 512)],
                        start=(dd == 0), stop=(dd == NDT - 1)),
                        r=[sqb, bf("c_f")], w=[ppb[1]])
            B.v(lambda v: v.tensor_scalar_mul(rows[0:1, 0:CW], pp[0][0:1, :], -1.0 / D),
                r=[ppb[0]], w=[r_nm])
            B.v(lambda v: v.tensor_mul(rows[0:1, CW:2 * CW], rows[0:1, 0:CW],
                                       rows[0:1, 0:CW]), r=[r_nm], w=[r_rstd])
            B.v(lambda v: v.scalar_tensor_tensor(rows[0:1, CW:2 * CW], pp[1][0:1, :], 1.0 / D,
                                                 rows[0:1, CW:2 * CW], ALU.mult, ALU.subtract),
                r=[ppb[1]], w=[r_rstd])
            B.s(lambda s: s.activation(rows[0:1, CW:2 * CW], rows[0:1, CW:2 * CW],
                                       AF.Ln, bias=self.c_f[0:1, 129:130]), r=[bf("c_f")], w=[r_rstd])
            B.s(lambda s: s.activation(rows[0:1, CW:2 * CW], rows[0:1, CW:2 * CW],
                                       AF.Exp, scale=-0.5), w=[r_rstd])
            for hf in range(2):
                B.t(lambda t, hf=hf: t.matmul(pp[0][:, _sl(hf, 512)], self.c_orf[:],
                                              rows[0:1, hf * 512:hf * 512 + 512],
                                              start=True, stop=True),
                    r=[r_nm, bf("c_orf")], w=[ppb[0]])
                B.t(lambda t, hf=hf: t.matmul(pp[1][:, _sl(hf, 512)], self.c_orf[:],
                                              rows[0:1, CW + hf * 512:CW + hf * 512 + 512],
                                              start=True, stop=True),
                    r=[r_rstd, bf("c_orf")], w=[ppb[1]])
            for dd in range(NDT):
                par = dd % 2
                B.dma("sync", slx[par], xdt[par][:], d["xres"][_sl(dd, 128), _sl(c, CW)],
                      r=self.dram_bufs("xres", t0, t0 + CW), w=[xdtb[par]])
                B.v(lambda v, par=par: v.tensor_tensor(sq[:], xdt[par][:], pp[0][:], ALU.add),
                    r=[xdtb[par], ppb[0]], w=[sqb])
                B.v(lambda v, dd=dd: v.tensor_tensor(xin[:, _sl(dd, CW)], sq[:], pp[1][:], ALU.mult),
                    r=[sqb, ppb[1]], w=[xinb])
            for tt in range(8):
                for dd in range(NDT):
                    B.t(lambda t, tt=tt, dd=dd: t.matmul(
                        pp[2][:, tt * 8:(tt + 1) * 8],
                        xin[:, dd * CW + tt * 128: dd * CW + (tt + 1) * 128],
                        rw_s[:, _sl(dd, 8)], start=(dd == 0), stop=(dd == NDT - 1)),
                        r=[xinb, rwb], w=[ppb[2]])
            B.v(lambda v: v.tensor_tensor(
                lgs[:, 0:64].rearrange("p (t e) -> p t e", t=8),
                pp[2][:, 0:64].rearrange("p (t e) -> p t e", t=8),
                self.rb_b.ap().unsqueeze(1).broadcast_to([128, 8, 8]), ALU.add),
                r=[ppb[2], bf("rb_b")], w=[lgsb])

            def tree3(alu, dst_lo):
                B.v(lambda v, alu=alu: v.tensor_tensor(
                    maskc[:, 0:32].rearrange("p (q one) -> p q one", one=1),
                    lgs[:, 0:64].rearrange("p (q two) -> p q two", two=2)[:, :, 0:1],
                    lgs[:, 0:64].rearrange("p (q two) -> p q two", two=2)[:, :, 1:2], alu),
                    r=[lgsb], w=[maskcb])
                B.v(lambda v, alu=alu: v.tensor_tensor(
                    maskc[:, 32:48].rearrange("p (q one) -> p q one", one=1),
                    maskc[:, 0:32].rearrange("p (q two) -> p q two", two=2)[:, :, 0:1],
                    maskc[:, 0:32].rearrange("p (q two) -> p q two", two=2)[:, :, 1:2], alu),
                    w=[maskcb])
                B.v(lambda v, alu=alu, dst_lo=dst_lo: v.tensor_tensor(
                    maskc[:, dst_lo:dst_lo + 8].rearrange("p (q one) -> p q one", one=1),
                    maskc[:, 32:48].rearrange("p (q two) -> p q two", two=2)[:, :, 0:1],
                    maskc[:, 32:48].rearrange("p (q two) -> p q two", two=2)[:, :, 1:2], alu),
                    w=[maskcb])

            tree3(ALU.max, 48)
            B.v(lambda v: v.tensor_tensor(
                lgs[:, 0:64].rearrange("p (t e) -> p t e", t=8),
                lgs[:, 0:64].rearrange("p (t e) -> p t e", t=8),
                maskc[:, 48:56].broadcast_to([128, 8, 8]), ALU.subtract),
                r=[maskcb], w=[lgsb])
            B.s(lambda s: s.activation(lgs[:, 0:64], lgs[:, 0:64], AF.Exp), w=[lgsb])
            tree3(ALU.add, 56)
            tree3(ALU.max, 72)
            B.s(lambda s: s.activation(maskc[:, 64:72], maskc[:, 56:64], AF.Ln),
                w=[maskcb])
            B.s(lambda s: s.activation(maskc[:, 64:72], maskc[:, 64:72], AF.Exp, scale=-1.0),
                w=[maskcb])
            B.v(lambda v: v.tensor_tensor(maskc[:, 80:88], maskc[:, 72:80], maskc[:, 64:72],
                                          ALU.mult), w=[maskcb])
            B.v(lambda v: v.memset(maskc[:, 88:96], 1.0), w=[maskcb])
            for e in range(8):
                B.v(lambda v, e=e: v.tensor_tensor(
                    lgs[:, 64:128].rearrange("p (t e) -> p t e", t=8)[:, :, e:e + 1],
                    lgs[:, 0:64].rearrange("p (t e) -> p t e", t=8)[:, :, e:e + 1],
                    maskc[:, 72:80].rearrange("p (t one) -> p t one", one=1), ALU.is_ge),
                    r=[maskcb], w=[lgsb])
                B.v(lambda v, e=e: v.tensor_tensor(
                    lgs[:, 64:128].rearrange("p (t e) -> p t e", t=8)[:, :, e:e + 1],
                    lgs[:, 64:128].rearrange("p (t e) -> p t e", t=8)[:, :, e:e + 1],
                    maskc[:, 88:96].rearrange("p (t one) -> p t one", one=1), ALU.mult),
                    r=[maskcb], w=[lgsb])
                B.v(lambda v, e=e: v.tensor_tensor(
                    maskc[:, 88:96].rearrange("p (t one) -> p t one", one=1),
                    maskc[:, 88:96].rearrange("p (t one) -> p t one", one=1),
                    lgs[:, 64:128].rearrange("p (t e) -> p t e", t=8)[:, :, e:e + 1],
                    ALU.subtract), r=[lgsb], w=[maskcb])
            for tt in range(8):
                B.t(lambda t, tt=tt: t.transpose(pp[2][0:8, tt * 128:(tt + 1) * 128],
                                                 lgs[:, 64 + tt * 8:64 + (tt + 1) * 8],
                                                 self.ident_f()),
                    r=[lgsb, bf("c_f")], w=[ppb[2]])
                B.t(lambda t, tt=tt: t.transpose(pp[3][0:1, tt * 128:(tt + 1) * 128],
                                                 maskc[:, 80 + tt:81 + tt], self.ident_f()),
                    r=[maskcb, bf("c_f")], w=[ppb[3]])
            B.s(lambda s: s.copy(maskT[:], pp[2][0:8, :]), r=[ppb[2]], w=[maskTb])
            B.s(lambda s: s.copy(maxpT[:], pp[3][0:1, :]), r=[ppb[3]], w=[maxpTb])
            def emit_mask(e, pt, ptb):
                par = e % 2
                for hf in range(2):
                    B.t(lambda t, e=e, hf=hf, pt=pt: t.matmul(
                        pt[:, _sl(hf, 512)], self.selc(e), maskT[0:8, _sl(hf, 512)],
                        start=True, stop=True),
                        r=[maskTb, bf("c_bf")], w=[ptb])
                B.s(lambda s, par=par, pt=pt: s.copy(maskb_s[par][:], pt[:]),
                    r=[ptb], w=[maskbb[par]])

            def emit_xinm(e):
                par = e % 2
                for dd in range(NDT):
                    B.v(lambda v, dd=dd, par=par: v.tensor_tensor(
                        xinm[:, _sl(dd, CW)], xin[:, _sl(dd, CW)], maskb_s[par][:], ALU.mult),
                        r=[xinb, maskbb[par]], w=[xinmb_d[dd]])

            w1i = 0
            w2i = 0
            emit_mask(0, pp[3], ppb[3])
            emit_xinm(0)
            for e in range(8):
                for fh in range(2):
                    for qi in range(4):
                        q = (fh * 2 + qi) if qi < 2 else (4 + fh * 2 + qi - 2)
                        w1q = "sync" if (fh * 4 + qi) % 2 == 0 else "gpsimd"
                        B.dma(w1q, slw1[qi % 2],
                              w1b[qi % 2].ap().rearrange("p (a f) -> p a f", a=NDT),
                              d["w1c"][e].rearrange("(a p) f -> p a f", p=128)[:, :, q * CW:(q + 1) * CW],
                              r=[bf("w1c%d" % e)], w=[w1bb[qi % 2]])
                        for ft in range(8):
                            floc = (qi % 2) * 8 + ft
                            for hf in range(2):
                                slot = (ft * 2 + hf) % 4
                                pa = pp[2 + slot // 2]
                                pab = ppb[2 + slot // 2][slot % 2: slot % 2 + 1]
                                bank = _sl(slot % 2, 512)
                                for dd in range(NDT):
                                    B.t(lambda t, qi=qi, ft=ft, hf=hf, dd=dd, pa=pa, bank=bank: t.matmul(
                                        pa[:, bank],
                                        w1b[qi % 2][:, dd * CW + ft * 128: dd * CW + (ft + 1) * 128],
                                        xinm[:, dd * CW + hf * 512: dd * CW + hf * 512 + 512],
                                        start=(dd == 0), stop=(dd == NDT - 1)),
                                        r=[w1bb[qi % 2], xinmb_d[dd]], w=pab)
                                aslot = floc * CW + hf * 512
                                if qi < 2:
                                    B.s(lambda s, aslot=aslot, pa=pa, bank=bank: s.activation(
                                        act[:, aslot:aslot + 512], pa[:, bank], AF.Silu),
                                        r=pab, w=[actb])
                                else:
                                    B.v(lambda v, aslot=aslot, pa=pa, bank=bank: v.tensor_tensor(
                                        act[:, aslot:aslot + 512], act[:, aslot:aslot + 512],
                                        pa[:, bank], ALU.mult),
                                        r=pab, w=[actb])
                    if fh == 1 and e < 7:
                        emit_mask(e + 1, pp[0], ppb[0])
                        emit_xinm(e + 1)
                    for pq in range(2):
                        for q2 in range(4):
                            q = fh * 4 + q2
                            wi = w2i % 3
                            w2q = "sync" if w2i % 2 == 0 else "gpsimd"
                            w2i += 1
                            B.dma(w2q, slw2[wi],
                                  w2b[wi].ap().rearrange("p (a dcol) -> p a dcol", a=4),
                                  d["w2c"][e].rearrange("(a p) dcol -> p a dcol", p=128)[:, 4 * q:4 * q + 4, :],
                                  w=[w2bb[wi]])
                            for ft in range(4):
                                floc = q2 * 4 + ft
                                for dt in range(4):
                                    dtg = pq * 4 + dt
                                    for hf in range(2):
                                        B.t(lambda t, wi=wi, q2=q2, ft=ft, dt=dt, dtg=dtg, hf=hf, floc=floc: t.matmul(
                                            pp[dt][:, _sl(hf, 512)],
                                            w2b[wi][:, ft * CW + dtg * 128: ft * CW + (dtg + 1) * 128],
                                            act[:, floc * CW + hf * 512: floc * CW + hf * 512 + 512],
                                            start=(q2 == 0 and ft == 0), stop=(q2 == 3 and ft == 3)),
                                            r=[w2bb[wi], actb], w=[ppb[dt]])
                        for dt in range(4):
                            dtg = pq * 4 + dt
                            if e == 0 and fh == 0:
                                B.v(lambda v, dt=dt, dtg=dtg: v.tensor_copy(
                                    yac[:, _sl(dtg, CW)], pp[dt][:]), r=[ppb[dt]], w=[yacb])
                            else:
                                B.v(lambda v, dt=dt, dtg=dtg: v.tensor_tensor(
                                    yac[:, _sl(dtg, CW)], yac[:, _sl(dtg, CW)], pp[dt][:], ALU.add),
                                    r=[ppb[dt]], w=[yacb])
            for hf in range(2):
                B.t(lambda t, hf=hf: t.matmul(pp[2][:, _sl(hf, 512)], self.c_orf[:],
                                              maxpT[0:1, hf * 512:hf * 512 + 512],
                                              start=True, stop=True),
                    r=[maxpTb, bf("c_orf")], w=[ppb[2]])
            B.s(lambda s: s.copy(maxb[:], pp[2][:]), r=[ppb[2]], w=[maxbb])
            for dd in range(NDT):
                par = dd % 2
                B.dma("gpsimd", slfin[par], xdt[par][:], d["xres"][_sl(dd, 128), _sl(c, CW)],
                      r=self.dram_bufs("xres", t0, t0 + CW), w=[xdtb[par]])
                B.v(lambda v, dd=dd, par=par: v.tensor_tensor(
                    outd2[par][:], yac[:, _sl(dd, CW)], maxb[:], ALU.mult),
                    r=[yacb, maxbb], w=[outd2b[par]])
                B.v(lambda v, par=par: v.tensor_tensor(
                    outd2[par][:], outd2[par][:], xdt[par][:], ALU.add),
                    r=[xdtb[par]], w=[outd2b[par]])
                B.dma("gpsimd", slo, d["xres"][_sl(dd, 128), _sl(c, CW)], outd2[par][:],
                      r=[outd2b[par]], w=self.dram_bufs("xres", t0, t0 + CW))
        self.end_phase()


    def ln_stream(self, c, CW, hb, out_fn):
        B = self.B
        bf = B.buf
        d = self.d
        xdt, xdtb, sq, sqb = hb["xdt"], hb["xdtb"], hb["sq"], hb["sqb"]
        rows, pp0, pp0b, pp1, pp1b, slx = (hb["rows"], hb["pp0"], hb["pp0b"],
                                           hb["pp1"], hb["pp1b"], hb["slx"])
        t0 = c * CW
        nh = CW // 512
        r0, r1 = bf("r_s0"), bf("r_s1")
        for dd in range(NDT):
            par = dd % 2
            B.dma("sync", slx[par], xdt[par][:, 0:CW], d["xres"][_sl(dd, 128), _sl(c, CW)],
                  r=self.dram_bufs("xres", t0, t0 + CW), w=[xdtb[par]])
            B.s(lambda s, par=par: s.square(sq[:, 0:CW], xdt[par][:, 0:CW]),
                r=[xdtb[par]], w=[sqb])
            for hf in range(nh):
                B.t(lambda t, par=par, hf=hf, dd=dd: t.matmul(
                    pp0[0:1, _sl(hf, 512)], self.ones_col_f(), xdt[par][:, _sl(hf, 512)],
                    start=(dd == 0), stop=(dd == NDT - 1)),
                    r=[xdtb[par], bf("c_f")], w=[pp0b])
                B.t(lambda t, hf=hf, dd=dd: t.matmul(
                    pp1[0:1, _sl(hf, 512)], self.ones_col_f(), sq[:, _sl(hf, 512)],
                    start=(dd == 0), stop=(dd == NDT - 1)),
                    r=[sqb, bf("c_f")], w=[pp1b])
        B.v(lambda v: v.tensor_scalar_mul(rows[0:1, 0:CW], pp0[0:1, 0:CW], 1.0 / D),
            r=[pp0b], w=[r0])
        B.v(lambda v: v.tensor_scalar_mul(rows[0:1, CW:2 * CW], pp1[0:1, 0:CW], 1.0 / D),
            r=[pp1b], w=[r1])
        B.v(lambda v: v.tensor_mul(rows[0:1, 0:CW], rows[0:1, 0:CW], rows[0:1, 0:CW]),
            r=[], w=[r0])
        B.v(lambda v: v.tensor_sub(rows[0:1, CW:2 * CW], rows[0:1, CW:2 * CW], rows[0:1, 0:CW]),
            r=[r0], w=[r1])
        B.v(lambda v: v.tensor_scalar_mul(rows[0:1, 0:CW], pp0[0:1, 0:CW], -1.0 / D),
            r=[pp0b], w=[r0])
        B.s(lambda s: s.activation(rows[0:1, CW:2 * CW], rows[0:1, CW:2 * CW],
                                   AF.Ln, bias=self.c_f[0:1, 129:130]), r=[bf("c_f")], w=[r1])
        B.s(lambda s: s.activation(rows[0:1, CW:2 * CW], rows[0:1, CW:2 * CW],
                                   AF.Exp, scale=-0.5), r=[], w=[r1])
        for hf in range(nh):
            B.t(lambda t, hf=hf: t.matmul(pp0[:, _sl(hf, 512)], self.c_orf[:],
                                          rows[0:1, hf * 512:hf * 512 + 512],
                                          start=True, stop=True),
                r=[r0, bf("c_orf")], w=[pp0b])
            B.t(lambda t, hf=hf: t.matmul(pp1[:, _sl(hf, 512)], self.c_orf[:],
                                          rows[0:1, CW + hf * 512:CW + hf * 512 + 512],
                                          start=True, stop=True),
                r=[r1, bf("c_orf")], w=[pp1b])
        for dd in range(NDT):
            par = dd % 2
            B.dma("sync", slx[par], xdt[par][:, 0:CW], d["xres"][_sl(dd, 128), _sl(c, CW)],
                  r=self.dram_bufs("xres", t0, t0 + CW), w=[xdtb[par]])
            out_fn(dd, par)

    def phase_attn0(self):
        B = self.B
        d = self.d
        bf = B.buf
        self.begin_phase()
        sb, ps = self.sb, self.ps
        CW = 1024
        NCH = T // CW
        SC = 1.0 / (128.0 ** 0.5)

        def gb(n):
            return bf("A0_" + n)

        xdt = [sb("A0_xdt0", (128, CW)), sb("A0_xdt1", (128, CW))]
        xdtb = [gb("xdt0"), gb("xdt1")]
        sq = sb("A0_sq", (128, CW)); sqb = gb("sq")
        rows = sb("A0_rows", (1, 4 * CW))
        xin = sb("A0_xin", (128, NDT * T), BF16); xinb = gb("xin")
        qt = sb("A0_qt", (128, T), BF16); qtb = gb("qt")
        kt = sb("A0_kt", (128, T), BF16); ktb = gb("kt")
        vt = sb("A0_vt", (128, T), BF16); vtb = gb("vt")
        srow = sb("A0_srow", (128, T)); srowb = gb("srow")
        pbuf = sb("A0_pbuf", (128, T), BF16); pbufb = gb("pbuf")
        ptb_s = sb("A0_ptb", (128, T), BF16); ptbb = gb("ptb")
        attno = sb("A0_attno", (128, T), BF16); attnob = gb("attno")
        col = sb("A0_col", (128, 4)); colb = gb("col")
        outdt = sb("A0_outdt", (128, CW)); outdtb = gb("outdt")
        wq_s = sb("A0_wq", (128, NDT * 128), BF16)
        wk_s = sb("A0_wk", (128, NDT * 128), BF16)
        wv_s = sb("A0_wv", (128, NDT * 128), BF16)
        wo_s = sb("A0_wo", (128, NDT * 128), BF16)

        pG = ps("A0_pG", (128, CW)); pGb = self.pb("A0_pG")
        pH = ps("A0_pH", (128, CW)); pHb = self.pb("A0_pH")
        pS = [ps("A0_pS0", (128, 512)), ps("A0_pS1", (128, 512))]
        pSb = [self.pb("A0_pS0"), self.pb("A0_pS1")]
        pT_ = ps("A0_pT", (128, 256), BF16); pTb = self.pb("A0_pT")
        pAV = ps("A0_pAV", (128, 128)); pAVb = self.pb("A0_pAV")

        slc = B.dslot()
        for nm, dst in [("awq", wq_s), ("awk", wk_s), ("awv", wv_s)]:
            B.dma("gpsimd", slc, dst.ap().rearrange("p (a o) -> p a o", a=NDT),
                  d[nm].ap().rearrange("(a p) o -> p a o", p=128), w=[gb(nm)])
        B.dma("gpsimd", slc, wo_s.ap().rearrange("p (a o) -> p a o", a=NDT),
              d["awo"].ap().rearrange("p (a o) -> p a o", a=NDT), w=[gb("awo")])
        slx = [B.dslot(), B.dslot()]
        slo = B.dslot()
        hb = dict(xdt=xdt, xdtb=xdtb, sq=sq, sqb=sqb, rows=rows,
                  pp0=pG, pp0b=pGb, pp1=pH, pp1b=pHb, slx=slx)

        for c in range(NCH):
            def out_fn(dd, par, c=c):
                B.v(lambda v, par=par: v.tensor_tensor(sq[:], xdt[par][:], pG[:], ALU.add),
                    r=[xdtb[par], pGb], w=[sqb])
                B.v(lambda v, dd=dd, c=c: v.tensor_tensor(
                    xin[:, dd * T + c * CW: dd * T + (c + 1) * CW], sq[:], pH[:], ALU.mult),
                    r=[sqb, pHb], w=[xinb])
            self.ln_stream(c, CW, hb, out_fn)
        for qc in range(8):
            for dd in range(NDT):
                B.t(lambda t, qc=qc, dd=dd: t.matmul(
                    pS[0][:], wq_s[:, _sl(dd, 128)],
                    xin[:, dd * T + qc * 512: dd * T + (qc + 1) * 512],
                    start=(dd == 0), stop=(dd == NDT - 1)), r=[gb("awq"), xinb], w=[pSb[0]])
            B.v(lambda v, qc=qc: v.tensor_copy(qt[:, _sl(qc, 512)], pS[0][:]), r=[pSb[0]], w=[qtb])
            for dd in range(NDT):
                B.t(lambda t, qc=qc, dd=dd: t.matmul(
                    pS[1][:], wk_s[:, _sl(dd, 128)],
                    xin[:, dd * T + qc * 512: dd * T + (qc + 1) * 512],
                    start=(dd == 0), stop=(dd == NDT - 1)), r=[gb("awk"), xinb], w=[pSb[1]])
            B.v(lambda v, qc=qc: v.tensor_copy(kt[:, _sl(qc, 512)], pS[1][:]), r=[pSb[1]], w=[ktb])
        for tt in range(32):
            for dd in range(NDT):
                B.t(lambda t, tt=tt, dd=dd: t.matmul(
                    pAV[:], xin[:, dd * T + tt * 128: dd * T + (tt + 1) * 128],
                    wv_s[:, _sl(dd, 128)],
                    start=(dd == 0), stop=(dd == NDT - 1)), r=[xinb, gb("awv")], w=[pAVb])
            B.v(lambda v, tt=tt: v.tensor_copy(vt[:, _sl(tt, 128)], pAV[:]), r=[pAVb], w=[vtb])
        for tt in range(32):
            W = tt * 128 + 128
            nfull = (tt * 128) // 512
            for kc in range(nfull):
                B.t(lambda t, tt=tt, kc=kc: t.matmul(
                    pS[kc % 2][:], qt[:, _sl(tt, 128)], kt[:, _sl(kc, 512)],
                    start=True, stop=True), r=[qtb, ktb], w=[pSb[kc % 2]])
                B.s(lambda s, tt=tt, kc=kc: s.activation(
                    srow[:, _sl(kc, 512)], pS[kc % 2][:], AF.Copy, scale=SC),
                    r=[pSb[kc % 2]], w=[srowb])
            for k1 in range(nfull * 4, tt + 1):
                B.t(lambda t, tt=tt, k1=k1: t.matmul(
                    pS[k1 % 2][:, 0:128], qt[:, _sl(tt, 128)], kt[:, _sl(k1, 128)],
                    start=True, stop=True), r=[qtb, ktb], w=[pSb[k1 % 2]])
                if k1 == tt:
                    B.v(lambda v, tt=tt, k1=k1: v.scalar_tensor_tensor(
                        srow[:, _sl(k1, 128)], pS[k1 % 2][:, 0:128], SC, self.causal(),
                        ALU.mult, ALU.add), r=[pSb[k1 % 2], bf("c_f")], w=[srowb])
                else:
                    B.s(lambda s, tt=tt, k1=k1: s.activation(
                        srow[:, _sl(k1, 128)], pS[k1 % 2][:, 0:128], AF.Copy, scale=SC),
                        r=[pSb[k1 % 2]], w=[srowb])
            B.v(lambda v, W=W: v.reduce_max(col[:, 0:1], srow[:, 0:W], axis=AX.X),
                r=[srowb], w=[colb])
            B.v(lambda v: v.tensor_scalar_mul(col[:, 1:2], col[:, 0:1], -1.0), r=[colb], w=[colb])
            B.s(lambda s, W=W: s.activation(srow[:, 0:W], srow[:, 0:W], AF.Exp, bias=col[:, 1:2]),
                r=[colb], w=[srowb])
            B.v(lambda v, W=W: v.reduce_sum(col[:, 2:3], srow[:, 0:W], axis=AX.X),
                r=[srowb], w=[colb])
            B.v(lambda v: v.reciprocal(col[:, 3:4], col[:, 2:3]), r=[colb], w=[colb])
            B.v(lambda v, W=W: v.tensor_scalar_mul(pbuf[:, 0:W], srow[:, 0:W], col[:, 3:4]),
                r=[srowb, colb], w=[pbufb])
            for j in range(tt + 1):
                B.t(lambda t, j=j: t.transpose(pT_[:, _sl(j % 2, 128)], pbuf[:, _sl(j, 128)],
                                               self.ident()), r=[pbufb, bf("c_bf")], w=[pTb])
                B.v(lambda v, j=j: v.tensor_copy(ptb_s[:, _sl(j, 128)], pT_[:, _sl(j % 2, 128)]),
                    r=[pTb], w=[ptbb])
            for j in range(tt + 1):
                B.t(lambda t, tt=tt, j=j: t.matmul(
                    pAV[:], vt[:, _sl(j, 128)], ptb_s[:, _sl(j, 128)],
                    start=(j == 0), stop=(j == tt)), r=[vtb, ptbb], w=[pAVb])
            B.s(lambda s, tt=tt: s.copy(attno[:, _sl(tt, 128)], pAV[:]), r=[pAVb], w=[attnob])
        for c in range(NCH):
            t0 = c * CW
            for dd in range(NDT):
                par = dd % 2
                B.dma("sync", slx[par], xdt[par][:], d["xres"][_sl(dd, 128), _sl(c, CW)],
                      r=self.dram_bufs("xres", t0, t0 + CW), w=[xdtb[par]])
                for hf in range(2):
                    B.t(lambda t, c=c, dd=dd, hf=hf: t.matmul(
                        pS[hf][:], wo_s[:, _sl(dd, 128)],
                        attno[:, c * CW + hf * 512: c * CW + hf * 512 + 512],
                        start=True, stop=True), r=[gb("awo"), attnob], w=[pSb[hf]])
                    B.v(lambda v, par=par, hf=hf: v.tensor_tensor(
                        outdt[:, _sl(hf, 512)], xdt[par][:, _sl(hf, 512)], pS[hf][:], ALU.add),
                        r=[xdtb[par], pSb[hf]], w=[outdtb])
                B.dma("sync", slo, d["xres"][_sl(dd, 128), _sl(c, CW)], outdt[:],
                      r=[outdtb], w=self.dram_bufs("xres", t0, t0 + CW))
        self.end_phase()


    def phase_ssm(self):
        B = self.B
        d = self.d
        bf = B.buf
        pv, dv = self.pvc, self.dvc
        self.begin_phase()
        sb, ps = self.sb, self.ps
        CW = 1024
        NCH = T // CW

        def gb(n):
            return bf("S_" + n)

        xdt = [sb("S_xdt0", (128, CW)), sb("S_xdt1", (128, CW))]
        xdtb = [gb("xdt0"), gb("xdt1")]
        sq = sb("S_sq", (128, CW)); sqb = gb("sq")
        rows = sb("S_rows", (1, 4 * CW))
        xinc = sb("S_xin", (128, NDT * CW), BF16); xincb = gb("xin")
        bb_ = [sb("S_B0", (128, T)), sb("S_B1", (128, T))]
        bbb_ = [gb("B0"), gb("B1")]
        sw = sb("S_sw", (128, T)); swb = gb("sw")
        scanc = sb("S_scanc", (128, 24)); scancb = gb("scanc")
        cmat = sb("S_cmat", (64, D)); cmatb = gb("cmat")
        ybuf = sb("S_ybuf", (128, NDT * CW)); ybufb = gb("ybuf")
        outdt = sb("S_outdt", (128, CW)); outdtb = gb("outdt")

        pG = ps("S_pG", (128, CW)); pGb = self.pb("S_pG")
        pH = ps("S_pH", (128, CW)); pHb = self.pb("S_pH")
        pBu = ps("S_pBu", (128, 512)); pBub = self.pb("S_pBu")
        pY = ps("S_pY", (128, 512)); pYb = self.pb("S_pY")

        slc = B.dslot()
        B.dma("sync", slc, cmat[:], d["cmatT"][:], w=[cmatb])
        slx = [B.dslot(), B.dslot()]
        slo = B.dslot()
        hb = dict(xdt=xdt, xdtb=xdtb, sq=sq, sqb=sqb, rows=rows,
                  pp0=pG, pp0b=pGb, pp1=pH, pp1b=pHb, slx=slx)

        for j in range(12):
            B.v(lambda v, j=j: v.tensor_copy(scanc[0:64, 2 * j:2 * j + 1], self.pw_re[:, j:j + 1]),
                r=[bf("pw")], w=[scancb])
            B.v(lambda v, j=j: v.tensor_copy(scanc[64:128, 2 * j:2 * j + 1], self.pw_re[:, j:j + 1]),
                r=[bf("pw")], w=[scancb])
            B.v(lambda v, j=j: v.tensor_scalar_mul(scanc[0:64, 2 * j + 1:2 * j + 2],
                                                   self.pw_im[:, j:j + 1], -1.0),
                r=[bf("pw")], w=[scancb])
            B.v(lambda v, j=j: v.tensor_copy(scanc[64:128, 2 * j + 1:2 * j + 2], self.pw_im[:, j:j + 1]),
                r=[bf("pw")], w=[scancb])

        for c in range(NCH):
            def out_fn(dd, par, c=c):
                B.v(lambda v, par=par: v.tensor_tensor(sq[:], xdt[par][:], pG[:], ALU.add),
                    r=[xdtb[par], pGb], w=[sqb])
                B.v(lambda v, dd=dd: v.tensor_tensor(xinc[:, _sl(dd, CW)], sq[:], pH[:], ALU.mult),
                    r=[sqb, pHb], w=[xincb])
            self.ln_stream(c, CW, hb, out_fn)
            for hf in range(2):
                for dd in range(NDT):
                    B.t(lambda t, c=c, hf=hf, dd=dd: t.matmul(
                        pBu[:], self.bbT[:, _sl(dd, 128)],
                        xinc[:, dd * CW + hf * 512: dd * CW + hf * 512 + 512],
                        start=(dd == 0), stop=(dd == NDT - 1)),
                        r=[bf("bbT"), xincb], w=[pBub])
                B.v(lambda v, c=c, hf=hf: v.tensor_copy(
                    bb_[0][:, c * CW + hf * 512: c * CW + hf * 512 + 512], pBu[:]),
                    r=[pBub], w=[bbb_[0]])
        for j in range(12):
            o = 1 << j
            src_ = bb_[j % 2]
            srcb = bbb_[j % 2]
            dst = bb_[(j + 1) % 2]
            dstb = bbb_[(j + 1) % 2]
            B.v(lambda v, o=o, src_=src_: v.tensor_copy(sw[0:64, 0:T - o], src_[64:128, 0:T - o]),
                r=[srcb], w=[swb])
            B.v(lambda v, o=o, src_=src_: v.tensor_copy(sw[64:128, 0:T - o], src_[0:64, 0:T - o]),
                r=[srcb], w=[swb])
            B.v(lambda v, o=o, j=j, src_=src_, dst=dst: v.scalar_tensor_tensor(
                dst[:, o:T], src_[:, 0:T - o], scanc[:, 2 * j:2 * j + 1], src_[:, o:T],
                ALU.mult, ALU.add), r=[srcb, scancb], w=[dstb])
            B.v(lambda v, o=o, j=j, dst=dst: v.scalar_tensor_tensor(
                dst[:, o:T], sw[:, 0:T - o], scanc[:, 2 * j + 1:2 * j + 2], dst[:, o:T],
                ALU.mult, ALU.add), r=[swb, scancb], w=[dstb])
            B.v(lambda v, o=o, src_=src_, dst=dst: v.tensor_copy(dst[:, 0:o], src_[:, 0:o]),
                r=[srcb], w=[dstb])
        hfin = bb_[0]
        hfinb = bbb_[0]
        for c in range(NCH):
            for hf in range(2):
                for dd in range(NDT):
                    B.t(lambda t, c=c, hf=hf, dd=dd: t.matmul(
                        pY[:], cmat[0:64, _sl(dd, 128)],
                        hfin[0:64, c * CW + hf * 512: c * CW + hf * 512 + 512],
                        start=True, stop=True), r=[cmatb, hfinb], w=[pYb])
                    B.v(lambda v, c=c, hf=hf, dd=dd: v.tensor_copy(
                        ybuf[:, dd * CW + _sl(hf, 512).start: dd * CW + hf * 512 + 512], pY[:]),
                        r=[pYb], w=[ybufb])

            def out_fn2(dd, par, c=c):
                B.v(lambda v, par=par: v.tensor_tensor(sq[:], xdt[par][:], pG[:], ALU.add),
                    r=[xdtb[par], pGb], w=[sqb])
                B.v(lambda v: v.tensor_tensor(sq[:], sq[:], pH[:], ALU.mult),
                    r=[pHb], w=[sqb])
                B.v(lambda v, dd=dd: v.scalar_tensor_tensor(
                    ybuf[:, _sl(dd, CW)], sq[:], dv("dp1")[:, dd:dd + 1], ybuf[:, _sl(dd, CW)],
                    ALU.mult, ALU.add), r=[sqb, bf("dvs")], w=[ybufb])
            self.ln_stream(c, CW, hb, out_fn2)
            for dd in range(NDT):
                B.s(lambda s, dd=dd: s.square(sq[:], ybuf[:, _sl(dd, CW)]), r=[ybufb], w=[sqb])
                for hf in range(2):
                    B.t(lambda t, dd=dd, hf=hf: t.matmul(
                        pG[0:1, _sl(hf, 512)], self.ones_col_f(), ybuf[:, dd * CW + hf * 512: dd * CW + hf * 512 + 512],
                        start=(dd == 0), stop=(dd == NDT - 1)), r=[ybufb, bf("c_f")], w=[pGb])
                    B.t(lambda t, dd=dd, hf=hf: t.matmul(
                        pH[0:1, _sl(hf, 512)], self.ones_col_f(), sq[:, _sl(hf, 512)],
                        start=(dd == 0), stop=(dd == NDT - 1)), r=[sqb, bf("c_f")], w=[pHb])
            r_nm, r_rstd, r_var, r_mean, r_tmp = (bf("r_nm"), bf("r_rstd"), bf("r_var"),
                                                  bf("r_mean"), bf("r_tmp"))
            B.v(lambda v: v.tensor_scalar_mul(rows[0:1, 2 * CW:3 * CW], pG[0:1, :], 1.0 / D),
                r=[pGb], w=[r_mean])
            B.v(lambda v: v.tensor_scalar_mul(rows[0:1, 0:CW], pG[0:1, :], -1.0 / D),
                r=[pGb], w=[r_nm])
            B.v(lambda v: v.tensor_scalar_mul(rows[0:1, 3 * CW:4 * CW], pH[0:1, :], 1.0 / D),
                r=[pHb], w=[r_var])
            B.v(lambda v: v.tensor_mul(rows[0:1, CW:2 * CW], rows[0:1, 2 * CW:3 * CW],
                                       rows[0:1, 2 * CW:3 * CW]), r=[r_mean], w=[r_tmp])
            B.v(lambda v: v.tensor_sub(rows[0:1, 3 * CW:4 * CW], rows[0:1, 3 * CW:4 * CW],
                                       rows[0:1, CW:2 * CW]), r=[r_tmp], w=[r_var])
            B.s(lambda s: s.activation(rows[0:1, 3 * CW:4 * CW], rows[0:1, 3 * CW:4 * CW],
                                       AF.Ln, bias=self.c_f[0:1, 129:130]), r=[bf("c_f")], w=[r_var])
            B.s(lambda s: s.activation(rows[0:1, CW:2 * CW], rows[0:1, 3 * CW:4 * CW],
                                       AF.Exp, scale=-0.5), r=[r_var], w=[r_rstd])
            for hf in range(2):
                B.t(lambda t, hf=hf: t.matmul(pG[:, _sl(hf, 512)], self.c_orf[:],
                                              rows[0:1, hf * 512:hf * 512 + 512], start=True, stop=True),
                    r=[r_nm, bf("c_orf")], w=[pGb])
                B.t(lambda t, hf=hf: t.matmul(pH[:, _sl(hf, 512)], self.c_orf[:],
                                              rows[0:1, CW + hf * 512:CW + hf * 512 + 512], start=True, stop=True),
                    r=[r_rstd, bf("c_orf")], w=[pHb])
            t0 = c * CW
            for dd in range(NDT):
                par = dd % 2
                B.dma("sync", slx[par], xdt[par][:], d["xres"][_sl(dd, 128), _sl(c, CW)],
                      r=self.dram_bufs("xres", t0, t0 + CW), w=[xdtb[par]])
                B.v(lambda v, dd=dd: v.tensor_tensor(outdt[:], ybuf[:, _sl(dd, CW)], pG[:], ALU.add),
                    r=[ybufb, pGb], w=[outdtb])
                B.v(lambda v: v.tensor_tensor(outdt[:], outdt[:], pH[:], ALU.mult),
                    r=[pHb], w=[outdtb])
                B.v(lambda v, dd=dd: v.tensor_scalar(outdt[:], outdt[:], pv("l1g")[:, dd:dd + 1],
                                                     pv("l1b2")[:, dd:dd + 1], ALU.mult, ALU.add),
                    r=[bf("pvs")], w=[outdtb])
                B.v(lambda v, par=par: v.tensor_tensor(outdt[:], outdt[:], xdt[par][:], ALU.add),
                    r=[xdtb[par]], w=[outdtb])
                B.dma("sync", slo, d["xres"][_sl(dd, 128), _sl(c, CW)], outdt[:],
                      r=[outdtb], w=self.dram_bufs("xres", t0, t0 + CW))
        self.end_phase()

    def phase_cmix(self, call, final=False):
        B = self.B
        d = self.d
        bf = B.buf
        pv, dv = self.pvc, self.dvc
        self.begin_phase()
        sb, ps = self.sb, self.ps
        CW = 1024
        NCH = T // CW
        pre = "C%d_" % call

        def gb(n):
            return bf(pre + n)

        xdt = [sb(pre + "xdt0", (128, CW)), sb(pre + "xdt1", (128, CW))]
        xdtb = [gb("xdt0"), gb("xdt1")]
        sq = sb(pre + "sq", (128, CW)); sqb = gb("sq")
        rows = sb(pre + "rows", (1, 4 * CW))
        uc = sb(pre + "u", (128, NDT * CW)); ucb = gb("u")
        xr = sb(pre + "xr", (128, NDT * CW), BF16); xrb = gb("xr")
        xk = sb(pre + "xk", (128, NDT * CW), BF16); xkb = gb("xk")
        rbuf = sb(pre + "rbuf", (128, NDT * CW)); rbufb = gb("rbuf")
        kact = sb(pre + "kact", (128, NDT * CW), BF16); kactb = gb("kact")
        ubnd = sb(pre + "ubnd", (128, NDT)); ubndb = gb("ubnd")
        tmp = sb(pre + "tmp", (128, CW)); tmpb = gb("tmp")
        outdt = sb(pre + "outdt", (128, CW)); outdtb = gb("outdt")
        wr_s = sb(pre + "wr", (128, NDT * D), BF16)
        wk_s = sb(pre + "wk", (128, NDT * D), BF16)
        wv_s = sb(pre + "wv", (128, NDT * D), BF16)

        pG = ps(pre + "pG", (128, CW)); pGb = self.pb(pre + "pG")
        pH = ps(pre + "pH", (128, CW)); pHb = self.pb(pre + "pH")
        pM = [ps(pre + "pM0", (128, 512)), ps(pre + "pM1", (128, 512))]
        pMb = [self.pb(pre + "pM0"), self.pb(pre + "pM1")]

        slc = B.dslot()
        for nm, dst in [("fwrT", wr_s), ("fwkT", wk_s), ("fwvT", wv_s)]:
            B.dma("gpsimd", slc, dst.ap().rearrange("p (a c) -> p a c", a=NDT),
                  d[nm].ap().rearrange("(a p) c -> p a c", p=128), w=[gb(nm)])
        slx = [B.dslot(), B.dslot()]
        slo = B.dslot()
        hb = dict(xdt=xdt, xdtb=xdtb, sq=sq, sqb=sqb, rows=rows,
                  pp0=pG, pp0b=pGb, pp1=pH, pp1b=pHb, slx=slx)

        B.v(lambda v: v.memset(ubnd[:], 0.0), w=[ubndb])
        for c in range(NCH):
            t0 = c * CW

            def out_fn(dd, par, c=c):
                B.v(lambda v, par=par: v.tensor_tensor(sq[:], xdt[par][:], pG[:], ALU.add),
                    r=[xdtb[par], pGb], w=[sqb])
                B.v(lambda v, dd=dd: v.tensor_tensor(uc[:, _sl(dd, CW)], sq[:], pH[:], ALU.mult),
                    r=[sqb, pHb], w=[ucb])
            self.ln_stream(c, CW, hb, out_fn)
            for dd in range(NDT):
                B.v(lambda v, dd=dd: v.tensor_scalar_mul(tmp[:], uc[:, _sl(dd, CW)],
                                                         pv("mur")[:, dd:dd + 1]),
                    r=[ucb, bf("pvs")], w=[tmpb])
                B.v(lambda v, dd=dd: v.scalar_tensor_tensor(
                    xr[:, dd * CW + 1:(dd + 1) * CW], uc[:, dd * CW:(dd + 1) * CW - 1],
                    dv("omur")[:, dd:dd + 1], tmp[:, 1:CW], ALU.mult, ALU.add),
                    r=[ucb, tmpb, bf("dvs")], w=[xrb])
                B.v(lambda v, dd=dd: v.scalar_tensor_tensor(
                    xr[:, dd * CW:dd * CW + 1], ubnd[:, dd:dd + 1],
                    dv("omur")[:, dd:dd + 1], tmp[:, 0:1], ALU.mult, ALU.add),
                    r=[ubndb, tmpb, bf("dvs")], w=[xrb])
                B.v(lambda v, dd=dd: v.tensor_scalar_mul(tmp[:], uc[:, _sl(dd, CW)],
                                                         pv("muk")[:, dd:dd + 1]),
                    r=[ucb, bf("pvs")], w=[tmpb])
                B.v(lambda v, dd=dd: v.scalar_tensor_tensor(
                    xk[:, dd * CW + 1:(dd + 1) * CW], uc[:, dd * CW:(dd + 1) * CW - 1],
                    dv("omuk")[:, dd:dd + 1], tmp[:, 1:CW], ALU.mult, ALU.add),
                    r=[ucb, tmpb, bf("dvs")], w=[xkb])
                B.v(lambda v, dd=dd: v.scalar_tensor_tensor(
                    xk[:, dd * CW:dd * CW + 1], ubnd[:, dd:dd + 1],
                    dv("omuk")[:, dd:dd + 1], tmp[:, 0:1], ALU.mult, ALU.add),
                    r=[ubndb, tmpb, bf("dvs")], w=[xkb])
                B.v(lambda v, dd=dd: v.tensor_copy(ubnd[:, dd:dd + 1],
                                                   uc[:, (dd + 1) * CW - 1:(dd + 1) * CW]),
                    r=[ucb], w=[ubndb])
            for ct in range(NDT):
                for hf in range(2):
                    pm = pM[hf]
                    pmb = pMb[hf]
                    for dd in range(NDT):
                        B.t(lambda t, ct=ct, hf=hf, dd=dd, pm=pm: t.matmul(
                            pm[:], wr_s[:, dd * D + ct * 128: dd * D + (ct + 1) * 128],
                            xr[:, dd * CW + hf * 512: dd * CW + hf * 512 + 512],
                            start=(dd == 0), stop=(dd == NDT - 1)),
                            r=[gb("fwrT"), xrb], w=[pmb])
                    B.s(lambda s, ct=ct, hf=hf, pm=pm: s.activation(
                        rbuf[:, ct * CW + hf * 512: ct * CW + hf * 512 + 512], pm[:], AF.Sigmoid),
                        r=[pmb], w=[rbufb])
            for ct in range(NDT):
                for hf in range(2):
                    pm = pM[hf]
                    pmb = pMb[hf]
                    for dd in range(NDT):
                        B.t(lambda t, ct=ct, hf=hf, dd=dd, pm=pm: t.matmul(
                            pm[:], wk_s[:, dd * D + ct * 128: dd * D + (ct + 1) * 128],
                            xk[:, dd * CW + hf * 512: dd * CW + hf * 512 + 512],
                            start=(dd == 0), stop=(dd == NDT - 1)),
                            r=[gb("fwkT"), xkb], w=[pmb])
                    B.v(lambda v, pm=pm: v.tensor_scalar_max(tmp[:, 0:512], pm[:], 0.0),
                        r=[pmb], w=[tmpb])
                    B.s(lambda s, ct=ct, hf=hf: s.square(
                        kact[:, ct * CW + hf * 512: ct * CW + hf * 512 + 512], tmp[:, 0:512]),
                        r=[tmpb], w=[kactb])
            for ct in range(NDT):
                par = ct % 2
                B.dma("sync", slx[par], xdt[par][:], d["xres"][_sl(ct, 128), _sl(c, CW)],
                      r=self.dram_bufs("xres", t0, t0 + CW), w=[xdtb[par]])
                for hf in range(2):
                    pm = pM[hf]
                    pmb = pMb[hf]
                    for dd in range(NDT):
                        B.t(lambda t, ct=ct, hf=hf, dd=dd, pm=pm: t.matmul(
                            pm[:], wv_s[:, dd * D + ct * 128: dd * D + (ct + 1) * 128],
                            kact[:, dd * CW + hf * 512: dd * CW + hf * 512 + 512],
                            start=(dd == 0), stop=(dd == NDT - 1)),
                            r=[gb("fwvT"), kactb], w=[pmb])
                    B.v(lambda v, ct=ct, hf=hf, pm=pm: v.tensor_tensor(
                        outdt[:, _sl(hf, 512)], rbuf[:, ct * CW + hf * 512: ct * CW + hf * 512 + 512],
                        pm[:], ALU.mult), r=[rbufb, pmb], w=[outdtb])
                    B.v(lambda v, par=par, hf=hf: v.tensor_tensor(
                        outdt[:, _sl(hf, 512)], outdt[:, _sl(hf, 512)], xdt[par][:, _sl(hf, 512)],
                        ALU.add), r=[xdtb[par]], w=[outdtb])
                tgt = d["outT"] if final else d["xres"]
                wb_ = [bf("outT")] if final else self.dram_bufs("xres", t0, t0 + CW)
                B.dma("sync", slo, tgt[_sl(ct, 128), _sl(c, CW)], outdt[:],
                      r=[outdtb], w=wb_)
        self.end_phase()


    def phase_seed(self):
        B = self.B
        self.begin_phase()
        sl = B.dslot()
        B.dma("sync", sl, self.d["xres"][:, :], self.d["xT"][:, :],
              w=self.dram_bufs("xres", 0, T))
        self.end_phase()

    def phase_attn1(self):
        B = self.B
        d = self.d
        bf = B.buf
        pv, dv = self.pvc, self.dvc
        self.begin_phase()
        sb, ps = self.sb, self.ps
        CW = 1024
        NCH = T // CW
        SC = 1.0 / 8.0

        def gb(n):
            return bf("A1_" + n)

        xdt = [sb("A1_xdt0", (128, CW)), sb("A1_xdt1", (128, CW))]
        xdtb = [gb("xdt0"), gb("xdt1")]
        sq = sb("A1_sq", (128, CW)); sqb = gb("sq")
        rows = sb("A1_rows", (1, 2 * CW))
        xin = sb("A1_xin", (128, NDT * T), BF16); xinb = gb("xin")
        kth = sb("A1_kth", (65, T), BF16); kthb = gb("kth")
        qth = sb("A1_qth", (65, T), BF16); qthb = gb("qth")
        vaug = sb("A1_vaug", (128, 32 * 65), BF16); vaugb = gb("vaug")
        pexp = [sb("A1_pe%d" % i, (128, 512), BF16) for i in range(6)]
        pexpb = [gb("pe%d" % i) for i in range(6)]
        ounf = sb("A1_ounf", (64, 512)); ounfb = gb("ounf")
        zi = sb("A1_zi", (1, 544)); zib = gb("zi")
        oall = sb("A1_oall", (128, 4 * T), BF16); oallb = gb("oall")
        ybuf = sb("A1_ybuf", (128, NDT * 512)); ybufb = gb("ybuf")
        outdt = sb("A1_outdt", (128, 512)); outdtb = gb("outdt")
        qkbias = sb("A1_qkb", (128, 1)); qkbiasb = gb("qkb")
        bqk_s = sb("A1_bqk", (64, 16)); bqk_sb = gb("bqk")
        bvr_s = sb("A1_bvr", (1, 512)); bvr_sb = gb("bvr")
        wqk_s = sb("A1_wqk", (128, NDT * D), BF16)
        wv1_s = sb("A1_wv1", (128, NDT * 512), BF16)
        wout_s = sb("A1_wout", (128, 4 * D), BF16)

        pG = ps("A1_pG", (128, CW)); pGb = self.pb("A1_pG")
        pH = ps("A1_pH", (128, CW)); pHb = self.pb("A1_pH")
        pS = [ps("A1_pS0", (128, 512)), ps("A1_pS1", (128, 512))]
        pSb = [self.pb("A1_pS0"), self.pb("A1_pS1")]
        pO = ps("A1_pO", (65, 512)); pOb = self.pb("A1_pO")
        pZ = ps("A1_pZ", (64, 512)); pZb = self.pb("A1_pZ")

        slc = B.dslot()
        B.dma("gpsimd", slc, wqk_s.ap().rearrange("p (a o) -> p a o", a=NDT),
              d["wqk"].ap().rearrange("(a p) o -> p a o", p=128), w=[gb("wqk")])
        B.dma("gpsimd", slc, wv1_s.ap().rearrange("p (a o) -> p a o", a=NDT),
              d["wv1"].ap().rearrange("(a p) o -> p a o", p=128), w=[gb("wv1")])
        B.dma("gpsimd", slc, wout_s.ap().rearrange("p (a o) -> p a o", a=4),
              d["wout"].ap().rearrange("(a p) o -> p a o", p=128), w=[gb("wout")])
        slc2 = B.dslot()
        B.dma("sync", slc2, bqk_s[:], d["bqk"][:], w=[bqk_sb])
        B.dma("sync", slc2, bvr_s[:], d["bvr"][:], w=[bvr_sb])
        slx = [B.dslot(), B.dslot()]
        slo = B.dslot()
        hb = dict(xdt=xdt, xdtb=xdtb, sq=sq, sqb=sqb, rows=rows,
                  pp0=pG, pp0b=pGb, pp1=pH, pp1b=pHb, slx=slx)

        for c in range(NCH):
            def out_fn(dd, par, c=c):
                B.v(lambda v, par=par: v.tensor_tensor(sq[:], xdt[par][:], pG[:], ALU.add),
                    r=[xdtb[par], pGb], w=[sqb])
                B.v(lambda v, dd=dd, c=c: v.tensor_tensor(
                    xin[:, dd * T + c * CW: dd * T + (c + 1) * CW], sq[:], pH[:], ALU.mult),
                    r=[sqb, pHb], w=[xinb])
            self.ln_stream(c, CW, hb, out_fn)
        for h in range(8):
            p, hloc = h // 2, h % 2
            B.t(lambda t, h=h: t.matmul(pS[0][:, 0:64], self.c_orf[:],
                                        bvr_s[0:1, h * 64:(h + 1) * 64], start=True, stop=True),
                r=[bvr_sb, bf("c_orf")], w=[pSb[0]])
            B.v(lambda v: v.tensor_copy(sq[:, 0:64], pS[0][:, 0:64]), r=[pSb[0]], w=[sqb])
            B.v(lambda v: v.memset(vaug[:], 1.0), w=[vaugb])
            for tt in range(32):
                for dd in range(NDT):
                    B.t(lambda t, h=h, tt=tt, dd=dd: t.matmul(
                        pS[tt % 2][:, 0:64], xin[:, dd * T + tt * 128: dd * T + (tt + 1) * 128],
                        wv1_s[:, dd * 512 + h * 64: dd * 512 + (h + 1) * 64],
                        start=(dd == 0), stop=(dd == NDT - 1)),
                        r=[xinb, gb("wv1")], w=[pSb[tt % 2]])
                B.v(lambda v, tt=tt: v.tensor_tensor(
                    vaug[:, tt * 65: tt * 65 + 64], sq[:, 0:64],
                    pS[tt % 2][:, 0:64], ALU.add),
                    r=[sqb, pSb[tt % 2]], w=[vaugb])
            if True:
                B.v(lambda v, h=h: v.tensor_copy(qkbias[0:64, :], bqk_s[:, 2 * h:2 * h + 1]),
                    r=[bqk_sb], w=[qkbiasb])
                B.v(lambda v, h=h: v.tensor_copy(qkbias[64:128, :], bqk_s[:, 2 * h + 1:2 * h + 2]),
                    r=[bqk_sb], w=[qkbiasb])
                for qc in range(8):
                    for dd in range(NDT):
                        B.t(lambda t, h=h, qc=qc, dd=dd: t.matmul(
                            pS[qc % 2][:], wqk_s[:, dd * D + h * 128: dd * D + (h + 1) * 128],
                            xin[:, dd * T + qc * 512: dd * T + (qc + 1) * 512],
                            start=(dd == 0), stop=(dd == NDT - 1)),
                            r=[gb("wqk"), xinb], w=[pSb[qc % 2]])
                    B.v(lambda v, qc=qc: v.tensor_scalar_add(qth[0:64, _sl(qc, 512)],
                                                             pS[qc % 2][0:64, :], qkbias[0:64, :]),
                        r=[pSb[qc % 2], qkbiasb], w=[qthb])
                    B.v(lambda v, qc=qc: v.tensor_scalar_add(kth[0:64, _sl(qc, 512)],
                                                             pS[qc % 2][64:128, :], qkbias[64:128, :]),
                        r=[pSb[qc % 2], qkbiasb], w=[kthb])
                B.v(lambda v: v.memset(kth[64:65, :], 1.0), w=[kthb])
                for qc in range(8):
                    B.s(lambda s, qc=qc: s.square(sq[0:64, 0:512], kth[0:64, _sl(qc, 512)]),
                        r=[kthb], w=[sqb])
                    B.t(lambda t, qc=qc: t.matmul(pZ[0:1, :], self.ones_col_f()[0:64, :],
                                                  sq[0:64, 0:512], start=True, stop=True),
                        r=[sqb, bf("c_f")], w=[pZb])
                    B.v(lambda v, qc=qc: v.reduce_max(zi[0:1, qc:qc + 1], pZ[0:1, :], axis=AX.X),
                        r=[pZb], w=[zib])
                B.v(lambda v: v.reduce_max(zi[0:1, 8:9], zi[0:1, 0:8], axis=AX.X),
                    r=[], w=[zib])
                for qc in range(8):
                    B.s(lambda s, qc=qc: s.square(sq[0:64, 0:512], qth[0:64, _sl(qc, 512)]),
                        r=[qthb], w=[sqb])
                    B.t(lambda t, qc=qc: t.matmul(pZ[0:1, :], self.ones_col_f()[0:64, :],
                                                  sq[0:64, 0:512], start=True, stop=True),
                        r=[sqb, bf("c_f")], w=[pZb])
                    B.v(lambda v, qc=qc: v.tensor_single_scalar(zi[0:1, 16:528], pZ[0:1, :],
                                                                zi[0:1, 8:9], ALU.mult),
                        r=[pZb], w=[zib])
                    B.s(lambda s: s.activation(zi[0:1, 16:528], zi[0:1, 16:528], AF.Sqrt),
                        r=[], w=[zib])
                    B.v(lambda v, qc=qc: v.tensor_scalar_mul(qth[64:65, _sl(qc, 512)],
                                                             zi[0:1, 16:528], -1.0),
                        r=[zib], w=[qthb])
                if p == 0 and hloc == 0 and getattr(self, "_dbg", None) is not None:
                    self.dump("qth", qth[:], (65, T), [qthb])
                    self.dump("kth", kth[:], (65, T), [kthb])
                pSl = [pG[:, 0:512], pG[:, 512:1024], pH[:, 0:512], pH[:, 512:1024]]
                pSlb = [pGb[0:1], pGb[1:2], pHb[0:1], pHb[1:2]]
                DEP = 6

                def emit_S(kt, qc):
                    r4 = kt % 4
                    B.t(lambda t, qc=qc, kt=kt, r4=r4: t.matmul(
                        pSl[r4], kth[:, _sl(kt, 128)], qth[:, _sl(qc, 512)],
                        start=True, stop=True), r=[kthb, qthb], w=[pSlb[r4]])

                def emit_exp(kt, qc):
                    r4 = kt % 4
                    B.s(lambda s, kt=kt, r4=r4: s.activation(
                        pexp[kt % DEP][:], pSl[r4], AF.Exp, scale=SC),
                        r=[pSlb[r4]], w=[pexpb[kt % DEP]])

                def emit_AV(kt, hloc):
                    B.t(lambda t, kt=kt: t.matmul(
                        pO[:], vaug[:, kt * 65: kt * 65 + 65], pexp[kt % DEP][:],
                        start=(kt == 0), stop=(kt == 31)),
                        r=[vaugb, pexpb[kt % DEP]], w=[pOb])

                for qc in range(8):
                    for kt in range(32):
                        emit_S(kt, qc)
                        if kt >= DEP:
                            emit_AV(kt - DEP, hloc)
                        emit_exp(kt, qc)
                    for kt in range(32 - DEP, 32):
                        emit_AV(kt, hloc)
                    B.v(lambda v: v.reciprocal(zi[0:1, 16:528], pO[64:65, :]), r=[pOb], w=[zib])
                    B.t(lambda t: t.matmul(pZ[:], self.c_orf[0:1, 0:64], zi[0:1, 16:528],
                                           start=True, stop=True), r=[zib, bf("c_orf")], w=[pZb])
                    B.s(lambda s: s.copy(ounf[:], pO[0:64, :]), r=[pOb], w=[ounfb])
                    B.v(lambda v, h=h, qc=qc, hloc=hloc: v.tensor_tensor(
                        oall[hloc * 64: hloc * 64 + 64,
                             (h // 2) * T + qc * 512: (h // 2) * T + (qc + 1) * 512],
                        ounf[:], pZ[:], ALU.mult), r=[ounfb, pZb], w=[oallb])
        if getattr(self, "_dbg", None) is not None:
            self.dump("oall", oall[:], (128, 4 * T), [oallb])
            self.dump("wouts", wout_s[:], (128, 4 * D), [gb("wout")])
        for c2 in range(T // 512):
            t0 = c2 * 512
            for ot in range(NDT):
                for vd in range(4):
                    B.t(lambda t, t0=t0, ot=ot, vd=vd: t.matmul(
                        pS[ot % 2][:], wout_s[:, vd * D + ot * 128: vd * D + (ot + 1) * 128],
                        oall[:, vd * T + t0: vd * T + t0 + 512],
                        start=(vd == 0), stop=(vd == 3)),
                        r=[gb("wout"), oallb], w=[pSb[ot % 2]])
                B.v(lambda v, ot=ot: v.tensor_scalar_add(
                    ybuf[:, _sl(ot, 512)], pS[ot % 2][:], pv("bout")[:, ot:ot + 1]),
                    r=[pSb[ot % 2], bf("pvs")], w=[ybufb])
            if c2 == 0 and getattr(self, "_dbg", None) is not None:
                self.dump("ypre", ybuf[:], (128, NDT * 512), [ybufb])
            for dd in range(NDT):
                B.s(lambda s, dd=dd: s.square(sq[:, 0:512], ybuf[:, _sl(dd, 512)]),
                    r=[ybufb], w=[sqb])
                B.t(lambda t, dd=dd: t.matmul(
                    pG[0:1, 0:512], self.ones_col_f(), ybuf[:, _sl(dd, 512)],
                    start=(dd == 0), stop=(dd == NDT - 1)), r=[ybufb, bf("c_f")], w=[pGb])
                B.t(lambda t, dd=dd: t.matmul(
                    pH[0:1, 0:512], self.ones_col_f(), sq[:, 0:512],
                    start=(dd == 0), stop=(dd == NDT - 1)), r=[sqb, bf("c_f")], w=[pHb])
            r0, r1 = bf("r_s0"), bf("r_s1")
            B.v(lambda v: v.tensor_scalar_mul(rows[0:1, 0:512], pG[0:1, 0:512], 1.0 / D),
                r=[pGb], w=[r0])
            B.v(lambda v: v.tensor_scalar_mul(rows[0:1, 512:1024], pH[0:1, 0:512], 1.0 / D),
                r=[pHb], w=[r1])
            B.v(lambda v: v.tensor_mul(rows[0:1, 0:512], rows[0:1, 0:512], rows[0:1, 0:512]),
                r=[], w=[r0])
            B.v(lambda v: v.tensor_sub(rows[0:1, 512:1024], rows[0:1, 512:1024], rows[0:1, 0:512]),
                r=[r0], w=[r1])
            B.v(lambda v: v.tensor_scalar_mul(rows[0:1, 0:512], pG[0:1, 0:512], -1.0 / D),
                r=[pGb], w=[r0])
            B.s(lambda s: s.activation(rows[0:1, 512:1024], rows[0:1, 512:1024],
                                       AF.Ln, bias=self.c_f[0:1, 129:130]), r=[bf("c_f")], w=[r1])
            B.s(lambda s: s.activation(rows[0:1, 512:1024], rows[0:1, 512:1024],
                                       AF.Exp, scale=-0.5), r=[], w=[r1])
            B.t(lambda t: t.matmul(pG[:, 0:512], self.c_orf[:], rows[0:1, 0:512],
                                   start=True, stop=True), r=[r0, bf("c_orf")], w=[pGb])
            B.t(lambda t: t.matmul(pH[:, 0:512], self.c_orf[:], rows[0:1, 512:1024],
                                   start=True, stop=True), r=[r1, bf("c_orf")], w=[pHb])
            for dd in range(NDT):
                B.v(lambda v, dd=dd: v.tensor_tensor(ybuf[:, _sl(dd, 512)], ybuf[:, _sl(dd, 512)],
                                                     pG[:, 0:512], ALU.add), r=[pGb], w=[ybufb])
                B.v(lambda v, dd=dd: v.tensor_tensor(ybuf[:, _sl(dd, 512)], ybuf[:, _sl(dd, 512)],
                                                     pH[:, 0:512], ALU.mult), r=[pHb], w=[ybufb])
                B.v(lambda v, dd=dd: v.tensor_scalar(ybuf[:, _sl(dd, 512)], ybuf[:, _sl(dd, 512)],
                                                     pv("lng")[:, dd:dd + 1], pv("lnb")[:, dd:dd + 1],
                                                     ALU.mult, ALU.add), r=[bf("pvs")], w=[ybufb])

            if c2 == 0 and getattr(self, "_dbg", None) is not None:
                self.dump("ypost", ybuf[:], (128, NDT * 512), [ybufb])

            def out_fn2(dd, par, c2=c2):
                t0_ = c2 * 512
                B.v(lambda v, par=par: v.tensor_tensor(sq[:, 0:512], xdt[par][:, 0:512],
                                                       pG[:, 0:512], ALU.add),
                    r=[xdtb[par], pGb], w=[sqb])
                B.v(lambda v: v.tensor_tensor(sq[:, 0:512], sq[:, 0:512], pH[:, 0:512], ALU.mult),
                    r=[pHb], w=[sqb])
                B.v(lambda v, dd=dd: v.tensor_tensor(outdt[:], ybuf[:, _sl(dd, 512)],
                                                     sq[:, 0:512], ALU.add),
                    r=[ybufb, sqb], w=[outdtb])
                B.v(lambda v, par=par: v.scalar_tensor_tensor(outdt[:], outdt[:], RS2,
                                                              xdt[par][:, 0:512], ALU.mult, ALU.add),
                    r=[xdtb[par]], w=[outdtb])
                B.dma("sync", slo, d["xres"][_sl(dd, 128), t0_:t0_ + 512], outdt[:],
                      r=[outdtb], w=self.dram_bufs("xres", t0_, t0_ + 512))
            self.ln_stream(c2, 512, hb, out_fn2)
        self.end_phase()

    def dbg_init(self):
        self._dbg = {}
        self._dbg_slot = self.B.dslot()

    def dump(self, name, ap, shape, rbufs):
        t = self.nc.dram_tensor("dbg_" + name, list(shape), F32, kind="ExternalOutput")
        self.B.dma("gpsimd", self._dbg_slot, t[:], ap, r=rbufs, w=[self.B.buf("dbg_" + name)])

    def finish(self):
        self.B.build()
        self.es.close()



def _pvcols(v):
    return np.ascontiguousarray(v.reshape(8, 128).T.astype(np.float32))


def host_prep(inputs):
    f = np.float32
    x = np.asarray(inputs["L0_attn_a_x"], f)
    nb = ml_dtypes.bfloat16

    cb = np.zeros((128, 128 + 132 + 8 * 128), nb)
    cb[:, 0:128] = np.eye(128, dtype=nb)
    tri = np.zeros((128, 129), np.float32)
    tri[:, 128] = 1.0
    for i in range(128):
        tri[i, i + 1:128] = 1.0
    cb[:, 128:257] = tri.astype(nb)
    for e in range(8):
        cb[e, 260 + e * 128: 260 + (e + 1) * 128] = nb(1.0)
    cf = np.zeros((128, 260), f)
    causal = np.zeros((128, 128), f)
    for i in range(128):
        causal[i, i + 1:] = NEG_BIG
    cf[:, 0:128] = causal
    cf[:, 128] = 1.0
    cf[:, 129] = EPS
    cf[:, 132:260] = np.eye(128, dtype=f)
    onesrow = np.ones((1, 128), nb)
    onesrow_f = np.ones((1, 128), f)

    pv_order = ["ts", "tf", "ln1w", "ln1b", "ln2w", "ln2b",
                "l1d", "l1g", "l1b2", "mur", "muk", "bout", "lng", "lnb"]
    pv_src = {
        "ts": np.asarray(inputs["L0_attn_a_time_shift"], f).reshape(-1),
        "tf": np.asarray(inputs["L0_attn_a_time_first"], f).reshape(-1),
        "ln1w": np.asarray(inputs["L0_attn_a_ln1_w"], f),
        "ln1b": np.asarray(inputs["L0_attn_a_ln1_b"], f),
        "ln2w": np.asarray(inputs["L0_attn_a_ln2_w"], f),
        "ln2b": np.asarray(inputs["L0_attn_a_ln2_b"], f),
        "l1d": np.asarray(inputs["L1_attn_a_D"], f),
        "l1g": np.asarray(inputs["L1_attn_a_gamma"], f),
        "l1b2": np.asarray(inputs["L1_attn_a_beta"], f),
        "mur": np.asarray(inputs["L1_ffn_mu_r"], f),
        "muk": np.asarray(inputs["L1_ffn_mu_k"], f),
        "bout": np.asarray(inputs["L1_attn_b_b_out"], f),
        "lng": np.asarray(inputs["L1_attn_b_ln_gamma"], f),
        "lnb": np.asarray(inputs["L1_attn_b_ln_beta"], f),
    }
    pv = np.concatenate([_pvcols(pv_src[k]) for k in pv_order], axis=1)
    pv64 = np.stack([np.asarray(inputs["L1_attn_a_log_neg_real"], f),
                     np.asarray(inputs["L1_attn_a_imag"], f)], axis=1)

    bqk_full = np.asarray(inputs["L1_attn_b_b_qk"], f)
    bqk = np.zeros((64, 16), f)
    for h in range(8):
        idx = (h * 64 + np.arange(64))
        bqk[:, 2 * h] = bqk_full[idx * 2]
        bqk[:, 2 * h + 1] = bqk_full[idx * 2 + 1]
    bv_full = np.asarray(inputs["L1_attn_b_b_v"], f)
    bv = np.stack([bv_full[h * 64:(h + 1) * 64] for h in range(8)], axis=1)

    wqk_full = np.asarray(inputs["L1_attn_b_W_qk"], f)
    wqk = np.zeros((D, D), f)
    for h in range(8):
        idx = (h * 64 + np.arange(64))
        wqk[:, h * 128:h * 128 + 64] = wqk_full[:, idx * 2]
        wqk[:, h * 128 + 64:h * 128 + 128] = wqk_full[:, idx * 2 + 1]

    common = {
        "cb": cb, "cf": cf, "onesrow": onesrow, "onesrow_f": onesrow_f,
        "pv": pv, "pv64": pv64,
        "rbias": np.asarray(inputs["L0_ffn_router_b"], f).reshape(1, 8),
        "wkT": np.ascontiguousarray(np.asarray(inputs["L0_attn_a_Wk"], f).T),
        "wvT": np.ascontiguousarray(np.asarray(inputs["L0_attn_a_Wv"], f).T),
        "wrT": np.ascontiguousarray(np.asarray(inputs["L0_attn_a_Wr"], f).T),
        "wckT": np.ascontiguousarray(np.asarray(inputs["L0_attn_a_Wck"], f).T),
        "wcvT": np.ascontiguousarray(np.asarray(inputs["L0_attn_a_Wcv"], f).T),
        "wcrT": np.ascontiguousarray(np.asarray(inputs["L0_attn_a_Wcr"], f).T),
        "awq": np.asarray(inputs["L0_attn_b_Wq"], f),
        "awk": np.asarray(inputs["L0_attn_b_Wk"], f),
        "awv": np.asarray(inputs["L0_attn_b_Wv"], f),
        "awo": np.asarray(inputs["L0_attn_b_Wo"], f),
        "rw": np.asarray(inputs["L0_ffn_router_w"], f),
        "w1": np.asarray(inputs["L0_ffn_w1"], f),
        "w2": np.asarray(inputs["L0_ffn_w2"], f),
        "bmat": np.asarray(inputs["L1_attn_a_B_mat"], f),
        "cmatT": np.ascontiguousarray(np.asarray(inputs["L1_attn_a_C_mat"], f).T),
        "wqk": wqk, "bqk": bqk,
        "wv1": np.asarray(inputs["L1_attn_b_W_v"], f), "bv": bv,
        "bvr": np.ascontiguousarray(bv_full.reshape(1, 512)),
        "wout": np.asarray(inputs["L1_attn_b_W_out"], f),
        "fwrT": np.ascontiguousarray(np.asarray(inputs["L1_ffn_W_r"], f).T),
        "fwkT": np.ascontiguousarray(np.asarray(inputs["L1_ffn_W_k"], f).T),
        "fwvT": np.ascontiguousarray(np.asarray(inputs["L1_ffn_W_v"], f).T),
    }
    in_maps = []
    for b in range(8):
        m = dict(common)
        m["xT"] = np.ascontiguousarray(x[b].T)
        in_maps.append(m)
    return in_maps


def build_program(taps=(), phases=("P0", "P1"), debug=False, races=True):
    nc = bass.Bass("TRN2", target_bir_lowering=False, detect_race_conditions=races)
    mk = MK(nc, taps=taps)
    if debug:
        mk.dbg_init()
    mk.prologue()
    if "SEED" in phases:
        mk.phase_seed()
    if "P1" in phases:
        mk.phase_l0aa_A()
    if "P2" in phases:
        mk.phase_l0aa_B()
    if "P3" in phases:
        mk.phase_moe(0)
    if "P4" in phases:
        mk.phase_attn0()
    if "P5" in phases:
        mk.phase_moe(1)
    if "P6" in phases:
        mk.phase_ssm()
    if "P7" in phases:
        mk.phase_cmix(0)
    if "P8" in phases:
        mk.phase_attn1()
    if "P9" in phases:
        mk.phase_cmix(1, final=True)
    mk.finish()
    return nc


ALL_PHASES = ("P0", "P1", "P2", "P3", "P4", "P5", "P6", "P7", "P8", "P9")


def kernel(**inputs):
    in_maps = host_prep(inputs)
    nc = build_program(taps=(), phases=ALL_PHASES)
    res = run_bass_kernel_spmd(nc, in_maps, core_ids=list(range(8)))
    out = np.stack([np.ascontiguousarray(r["outT"].T) for r in res.results], axis=0)
    return out
